# version 02
# baseline (speedup 1.0000x reference)
import numpy as np
import ml_dtypes
import concourse.bass as bass
import concourse.bacc as bacc
import concourse.mybir as mybir
from concourse.bass_utils import run_bass_kernel_spmd
from concourse.bass_types import AP as BAP
from contextlib import ExitStack

F32 = mybir.dt.float32
BF16 = mybir.dt.bfloat16
I32 = mybir.dt.int32
AF = mybir.ActivationFunctionType
ALU = mybir.AluOpType
AX = mybir.AxisListType

B, T, D = 4, 4096, 1024
H, HKV, HD = 16, 4, 64
INNER = 4096
TOWN = 2048
NT = 8
TT = 512
NQT = 16
PAD = 32
WTP = T + 2 * PAD
EPS = 1e-5
LN127 = float(np.log(127.0))
RMAGIC = 12582912.0

_TAPS = set()

QKEYS = ("QS", "QA", "QG")
EKEYS = ("T", "V", "S", "G") + QKEYS
STREAM = {"T": "tensor", "V": "vector", "S": "scalar", "G": "gpsimd",
          "QS": "sync", "QA": "scalar", "QG": "gpsimd"}


class Prog:
    LIMIT = None
    MARKS = {}

    def __init__(self, sems):
        self.sems = sems
        self.nops = 0
        self.count = {k: 0 for k in EKEYS}
        self.hwm = {}
        self.last_w = {}
        self.readers = {}
        self.pending = {s: [] for s in ("tensor", "vector", "scalar", "gpsimd", "sync")}
        self.qbounds = {}

    def mark(self, name):
        Prog.MARKS[name] = self.nops

    def op(self, eng, a, b=(), c=(), tag=""):
        self.nops += 1
        if Prog.LIMIT is not None and self.nops > Prog.LIMIT:
            return 0
        if callable(a):
            emit, reads, writes = a, b, c
        else:
            reads, writes, emit = a, b, c
        deps = []
        for b in reads:
            if b in self.last_w:
                deps.append(self.last_w[b])
        for b in writes:
            if b in self.last_w:
                deps.append(self.last_w[b])
            for e2, c2 in self.readers.get(b, {}).items():
                deps.append((e2, c2))
        waits = {}
        for (e2, c2) in deps:
            if c2 <= 0:
                continue
            if e2 in QKEYS:
                if eng == e2:
                    continue
                c2 = self.count[e2]
            if self.hwm.get((eng, e2), 0) < c2:
                waits[e2] = max(waits.get(e2, 0), c2)
        for e2, c2 in waits.items():
            self.hwm[(eng, e2)] = c2
            if e2 in QKEYS:
                self.qbounds.setdefault(e2, set()).add(c2)
        inc = 16 if eng in QKEYS else 1
        self.count[eng] += inc
        cnt = self.count[eng]
        self.pending[STREAM[eng]].append((waits, emit, self.sems[eng], inc, tag))
        for b in reads:
            self.readers.setdefault(b, {})[eng] = cnt
        for b in writes:
            self.last_w[b] = (eng, cnt)
            self.readers[b] = {}
        return cnt

    def barrier(self):
        snap = dict(self.count)
        for eng in ("T", "V", "S", "G", "QS"):
            waits = {}
            for e2, c2 in snap.items():
                if e2 != eng and c2 > 0 and self.hwm.get((eng, e2), 0) < c2:
                    waits[e2] = c2
            for e2, c2 in waits.items():
                self.hwm[(eng, e2)] = c2
                if e2 in QKEYS:
                    self.qbounds.setdefault(e2, set()).add(c2)
            if waits:
                self.pending[STREAM[eng]].append((waits, None, None, 0, "bar"))

    def flush(self, block):
        pass

    def final_emit(self, block):
        qsems = {"QS": "sync", "QA": "scalar", "QG": "gpsimd"}
        for qk, stream in qsems.items():
            need = self.qbounds.get(qk, set())
            if not need:
                continue
            sem = self.sems[qk]
            out = []
            cum = 0
            for entry in self.pending[stream]:
                out.append(entry)
                if entry[2] is sem:
                    cum += entry[3]
                    if cum in need:
                        out.append(({qk: cum}, None, None, 0, "qb"))
            self.pending[stream] = out
        for stream in ("sync", "gpsimd", "scalar", "vector", "tensor"):
            lst = self.pending[stream]
            if not lst:
                continue
            snapshot = list(lst)
            lst.clear()
            sems = self.sems

            def run(en, snapshot=snapshot, sems=sems):
                for waits, emit, sem, inc, tag in snapshot:
                    for e2, c2 in waits.items():
                        en.wait_ge(sems[e2], c2)
                    if emit is not None:
                        emit(en).then_inc(sem, inc)
            getattr(block, stream)(run)


def ap3(t, offset, dims):
    h = t.tensor if hasattr(t, "tensor") else t
    return BAP(tensor=h, offset=offset, ap=[list(d) for d in dims])


def build_nc():
    nc = bacc.Bacc("TRN2", target_bir_lowering=False)

    def din(name, shape, dt=F32):
        return nc.dram_tensor(name, shape, dt, kind="ExternalInput").ap()

    def hbm(name, shape, dt):
        kind = "ExternalOutput" if name in _TAPS else "Internal"
        return nc.dram_tensor(name, shape, dt, kind=kind).ap()

    x_in = din("x_in", [T, D])
    wq0 = din("wq0", [D, D]); bq0 = din("bq0", [128, 8])
    wk0 = din("wk0", [D, 256]); bk0 = din("bk0", [128, 2])
    wv0 = din("wv0", [D, 256]); bv0c = din("bv0c", [1, 256])
    wo0 = din("wo0", [D, D]); bo0 = din("bo0", [128, 8])
    hw0 = din("hw0", [1, 16])
    g0 = din("g0", [D, INNER]); u0 = din("u0", [D, INNER]); d0 = din("d0", [INNER, D])
    wq1 = din("wq1", [D, D]); wk1 = din("wk1", [D, D])
    wv1 = din("wv1", [D, D]); wo1 = din("wo1", [D, D])
    w1 = din("w1", [D, INNER]); s1c = din("s1c", [128, 32])
    w2 = din("w2", [D, INNER]); s2c = din("s2c", [128, 32])
    h1 = din("h1", [INNER, INNER]); sh1c = din("sh1c", [128, 32])
    h2 = din("h2", [INNER, INNER]); sh2c = din("sh2c", [128, 32])
    w3 = din("w3", [INNER, D]); s3c = din("s3c", [128, 8])
    identc = din("identc", [128, 128], BF16)
    identfc = din("identfc", [128, 128], F32)
    rope_cosc = din("rope_cosc", [128, T], BF16)
    rope_sinc = din("rope_sinc", [128, T], BF16)
    rot2c = din("rot2c", [128, 128], BF16)
    m0c = din("m0c", [64, 3 * 2048], BF16)
    m0g = din("m0g", [64, 3 * 512], BF16)
    m1c = din("m1c", [NQT * 128, 512], BF16)
    parc = din("parc", [1, 1], mybir.dt.int32)
    selc = din("selc", [128, 2])

    y_own = nc.dram_tensor("y_own", [TOWN, D], F32, kind="ExternalOutput").ap()

    g0q = hbm("g0q", [D, INNER], BF16)
    u0q = hbm("u0q", [D, INNER], BF16)
    d0q = hbm("d0q", [INNER, D], BF16)
    w1b = hbm("w1b", [D, INNER], BF16)
    w2b = hbm("w2b", [D, INNER], BF16)
    h1b = hbm("h1b", [INNER, INNER], BF16)
    h2b = hbm("h2b", [INNER, INNER], BF16)
    w3b = hbm("w3b", [INNER, D], BF16)
    wq1b = hbm("wq1b", [D, D], BF16)
    wk1b = hbm("wk1b", [D, D], BF16)
    wv1b = hbm("wv1b", [D, D], BF16)
    qT0h = hbm("qT0h", [8 * 128, T], BF16)
    xqh = hbm("xqh", [8 * 128, T], BF16)
    aoth = hbm("aoth", [8 * 128, T], BF16)
    xTh = hbm("xTh", [8 * 128, T], F32)
    x1h = hbm("x1h", [8 * 128, TOWN], F32)
    xn1h = hbm("xn1h", [8 * 128, TOWN], BF16)
    qT1h = hbm("qT1h", [8 * 128, TOWN], BF16)
    k1h = hbm("k1h", [8 * 128, T], BF16)
    v1h = hbm("v1h", [T, D], BF16)

    es = ExitStack()
    e = es.enter_context

    def sbf(name, shape, dt=F32):
        return e(nc.sbuf_tensor(name, shape, dt))

    kv0k = sbf("kv0k", [128, 2 * WTP], BF16)
    v0t = sbf("v0t", [128, 32 * 256], BF16)
    ident = sbf("ident", [128, 128], BF16)
    idf = sbf("idf", [128, 128], F32)
    srot2 = sbf("srot2", [128, 128], BF16)
    sbq0 = sbf("sbq0", [128, 8]); sbk0 = sbf("sbk0", [128, 2]); sbo0 = sbf("sbo0", [128, 8])
    bvb = sbf("bvb", [1, 256], BF16)
    shw = sbf("shw", [1, 16]); shw8 = sbf("shw8", [128, 8]); sbqhw = sbf("sbqhw", [128, 8])
    hwst = sbf("hwst", [32, 16])
    s_s1 = sbf("s_s1", [128, 32]); s_s2 = sbf("s_s2", [128, 32])
    s_sh1 = sbf("s_sh1", [128, 32]); s_sh2 = sbf("s_sh2", [128, 32])
    s_s3 = sbf("s_s3", [128, 8])
    ones1f = sbf("ones1f", [128, 1], F32)
    onesrf = sbf("onesrf", [1, 128], F32)
    onesrb = sbf("onesrb", [1, 128], BF16)
    swsc = sbf("swsc", [128, 4])
    swrec = sbf("swrec", [128, 4])
    sCneg = sbf("sCneg", [64, 1])
    pmask = sbf("pmask", [128, 2])
    rowA = sbf("rowA", [1, 640]); rowB = sbf("rowB", [1, 640]); rowC = sbf("rowC", [1, 640])
    wbufA = sbf("wbufA", [128, 8192], BF16)
    wbufB = sbf("wbufB", [128, 8192], BF16)
    arF1 = sbf("arF1", [128, 2048], F32)
    arF2 = sbf("arF2", [128, 4096], F32)
    sqb = sbf("sqb", [128, 1024], F32)
    amaxb = sbf("amaxb", [128, 1024], F32)
    mb_s = amaxb[:, 0:512]
    rb_s = amaxb[:, 512:1024]
    sxb_s = amaxb[:, 0:512]
    rxb_s = amaxb[:, 512:1024]
    arB1 = sbf("arB1", [128, 4096], BF16)
    arB2 = sbf("arB2", [128, 4096], BF16)
    arB3 = sbf("arB3", [128, 4096], BF16)
    arB4 = sbf("arB4", [128, 4096], BF16)
    hbuf = sbf("hbuf", [128, 16384], BF16)
    hbuf2 = sbf("hbuf2", [128, 16384], BF16)
    i32b = sbf("i32b", [128, 512], I32)
    tb1 = sbf("tb1", [128, 512], BF16)
    tb2 = tb1
    l0_s = sbf("l0_s", [64, 16]); r0_s = sbf("r0_s", [64, 16])
    cl127 = sbf("cl127", [128, 1])
    gq_s = sbf("gq_s", [128, 64], BF16)
    class _Slc:
        def __init__(self, parent, base):
            self.parent = parent
            self.base = base

        def __getitem__(self, idx):
            ps, fs = idx
            assert ps == slice(None)
            return self.parent[:, self.base + fs.start:self.base + fs.stop]

    Sg = [_Slc(hbuf, i * 4096) for i in range(4)]
    pTL = [_Slc(hbuf2, i * 4096) for i in range(4)]
    kblk = [_Slc(wbufA, i * 4096) for i in range(2)]
    vblk = [_Slc(wbufB, i * 4096) for i in range(2)]
    mrow = sbf("mrow", [128, 4]); nrow = sbf("nrow", [128, 4])
    lrow = sbf("lrow", [128, 4]); rlrow = sbf("rlrow", [128, 4])
    wo1_s = kv0k[:, 0:8192]
    winD = _Slc(wbufA, 0)
    p0c = _Slc(wbufA, 2208)
    pT0c = _Slc(wbufA, 4256)
    vwinc = _Slc(wbufA, 5280)
    rcos = hbuf2[:, 0:4096]
    rsin = hbuf2[:, 4096:8192]
    sm0 = hbuf2[0:64, 8192:14336]
    sm0g = hbuf2[0:64, 14336:15872]

    psA = e(nc.psum_tensor("psA", [128, 1024], F32))
    psB = e(nc.psum_tensor("psB", [128, 1024], F32))
    psC = e(nc.psum_tensor("psC", [128, 512], F32))
    psD = e(nc.psum_tensor("psD", [128, 512], F32))
    psE = e(nc.psum_tensor("psE", [128, 512], F32))
    psT = e(nc.psum_tensor("psT", [128, 1024], BF16))

    sems = {k: e(nc.semaphore(f"sem_{k}")) for k in EKEYS}
    block = e(nc.Block())
    P = Prog(sems)
    OP = P.op
    ssel = sbf("ssel", [128, 2])
    OP("QS", ("HB",), ("ssel",), lambda en: en.dma_start(out=ssel[:], in_=selc))

    def ld(buf, dst, src, q="QS"):
        OP(q, lambda en, d=dst, s=src: en.dma_start(out=d, in_=s), ("HB",), (buf,))

    OP("V", (), ("kv0k",), lambda v: v.memset(kv0k[:], 0.0))
    OP("V", (), ("v0t",), lambda v: v.memset(v0t[:], 0.0))
    for _nm, _t in (("arF1", arF1), ("arF2", arF2), ("sqb", sqb), ("amaxb", amaxb),
                    ("arB1", arB1), ("arB2", arB2), ("arB3", arB3), ("arB4", arB4),
                    ("i32b", i32b), ("tb1", tb1), ("rowA", rowA), ("rowB", rowB),
                    ("rowC", rowC), ("l0_s", l0_s), ("r0_s", r0_s), ("gq_s", gq_s),
                    ("mrow", mrow), ("nrow", nrow), ("lrow", lrow), ("rlrow", rlrow),
                    ("swsc", swsc), ("swrec", swrec), ("sCneg", sCneg), ("shw8", shw8)):
        OP("V", (), (_nm,), lambda v, _t=_t: v.memset(_t[:], 0.0))
    for _nm, _t in (("wbufA", wbufA), ("wbufB", wbufB), ("hbuf", hbuf), ("hbuf2", hbuf2)):
        OP("G", (), (_nm, "wq0b", "wk0b", "wv0b", "wo0b", "hbuf2w", "p0w", "p0g",
                     "vov", "pT0w", "pT0g", "vg4", "zebE", "zebO",
                     "SgB0", "SgB1", "SgB2", "SgB3",
                     "pTL0", "pTL1", "pTL2", "pTL3", "kblk0", "kblk1", "vblk0", "vblk1",
                     "rcos", "rsin", "sm0", "sm0g", "wo1b"),
           lambda g, _t=_t: g.memset(_t[:], 0.0))

    ld("ident", ident[:], identc); ld("idf", idf[:], identfc)
    ld("rcos", rcos, rope_cosc); ld("rsin", rsin, rope_sinc)
    ld("srot2", srot2[:], rot2c); ld("sm0", sm0, m0c)
    ld("sm0g", sm0g, m0g)
    ld("sbq0", sbq0[:], bq0); ld("sbk0", sbk0[:], bk0); ld("sbo0", sbo0[:], bo0)
    ld("rowB", rowB[0:1, 0:256], bv0c); ld("shw", shw[:], hw0)
    ld("s_s1", s_s1[:], s1c); ld("s_s2", s_s2[:], s2c)
    ld("s_sh1", s_sh1[:], sh1c); ld("s_sh2", s_sh2[:], sh2c); ld("s_s3", s_s3[:], s3c)

    OP("V", (), ("ones",), lambda v: v.memset(ones1f[:], 1.0))
    OP("V", (), ("ones",), lambda v: v.memset(onesrf[:], 1.0))
    OP("V", (), ("ones",), lambda v: v.memset(onesrb[:], 1.0))
    OP("V", (), ("cl127",), lambda v: v.memset(cl127[:], LN127))
    OP("V", (), ("pmask",), lambda v: v.memset(pmask[:], 0.0))
    OP("V", ("pmask",), ("pmask",), lambda v: v.memset(pmask[0:64, 0:1], 1.0))
    OP("V", ("pmask",), ("pmask",), lambda v: v.memset(pmask[64:96, 1:2], 1.0))
    OP("V", ("pmask",), ("pmask",), lambda v: v.memset(pmask[96:128, 1:2], 1.0))
    OP("V", ("rowB",), ("bvb",), lambda v: v.tensor_copy(bvb[:], rowB[0:1, 0:256]))
    for hh in range(16):
        OP("G", ("shw",), ("hwst",),
           lambda g, hh=hh: g.partition_broadcast(
               hwst[0:32, hh:hh + 1], shw[0:1, hh:hh + 1], channels=32))
    for rg in range(4):
        par = rg // 2
        OP("V", ("hwst",), ("shw8",),
           lambda v, rg=rg, par=par: v.tensor_copy(
               shw8[rg * 32:(rg + 1) * 32, 0:8],
               ap3(hwst, par, [[16, 32], [2, 8]])))
    OP("V", ("shw8",), ("shw8",),
       lambda v: v.tensor_scalar(out=shw8[:], in0=shw8[:], scalar1=0.125,
                                 scalar2=None, op0=ALU.mult))
    OP("V", ("shw8", "sbq0"), ("sbqhw",),
       lambda v: v.tensor_tensor(out=sbqhw[:], in0=sbq0[:], in1=shw8[:], op=ALU.mult))
    for cc in range(8):
        OP("QG", ("HB",), ("wq0b",),
           lambda g, cc=cc: g.dma_start(out=wbufA[:, cc * 1024:(cc + 1) * 1024],
                                        in_=wq0[cc * 128:(cc + 1) * 128, :]))
        OP("QG", ("HB",), ("wk0b",),
           lambda g, cc=cc: g.dma_start(out=hbuf[:, cc * 256:(cc + 1) * 256],
                                        in_=wk0[cc * 128:(cc + 1) * 128, :]))
        OP("QG", ("HB",), ("wv0b",),
           lambda g, cc=cc: g.dma_start(out=hbuf[:, 2048 + cc * 256:2048 + (cc + 1) * 256],
                                        in_=wv0[cc * 128:(cc + 1) * 128, :]))
        OP("QG", ("HB",), ("wo0b",),
           lambda g, cc=cc: g.dma_start(out=hbuf[:, 4096 + cc * 1024:4096 + (cc + 1) * 1024],
                                        in_=wo0[cc * 128:(cc + 1) * 128, :]))

    P.mark("W")
    def wq_stats(src, rows, cols, ci):
        nch = rows // 128
        npk = (cols + 2047) // 2048
        for k in range(nch):
            for hh2 in range(0, cols, 2048):
                nn = min(2048, cols - hh2)
                OP("QS", ("HB",), ("arF1",),
                   lambda en, k=k, src=src, hh2=hh2, nn=nn: en.dma_start(
                       out=arF1[:, 0:nn], in_=src[k * 128:(k + 1) * 128, hh2:hh2 + nn]))
                OP("V", ("arF1",), ("arF2",),
                   lambda v, k=k, hh2=hh2, nn=nn, npk=npk: v.reduce_sum(
                       out=arF2[:, npk * k + hh2 // 2048:npk * k + hh2 // 2048 + 1],
                       in_=arF1[:, 0:nn], axis=AX.X, apply_absolute_value=True))
        OP("V", ("arF2",), ("arF2",),
           lambda v, n=nch * npk: v.reduce_sum(out=arF2[:, 80:81], in_=arF2[:, 0:n],
                                               axis=AX.X))
        OP("T", ("arF2", "ones"), ("psE",),
           lambda t: t.matmul(psE[0:1, 0:1], ones1f[:], arF2[:, 80:81],
                              start=True, stop=True))
        OP("S", ("psE",), ("swsc",),
           lambda s, ci=ci, n=rows * cols: s.activation(
               out=swsc[0:1, ci:ci + 1], in_=psE[0:1, 0:1], func=AF.Identity, scale=1.0 / n))
        OP("V", ("swsc",), ("swsc",),
           lambda v, ci=ci: v.tensor_scalar(out=swsc[0:1, ci:ci + 1],
                                            in0=swsc[0:1, ci:ci + 1],
                                            scalar1=1e-5, scalar2=None, op0=ALU.max))
        OP("G", ("swsc",), ("swsc",),
           lambda g, ci=ci: g.partition_broadcast(swsc[:, ci:ci + 1],
                                                  swsc[0:1, ci:ci + 1], channels=128))

    wq_stats(g0, D, INNER, 0)
    wq_stats(u0, D, INNER, 1)
    wq_stats(d0, INNER, D, 2)
    OP("V", ("swsc",), ("swrec",), lambda v: v.reciprocal(out=swrec[:, 0:3], in_=swsc[:, 0:3]))

    def wq_quant(src, dst, rows, cols, ci):
        nch = rows // 128
        for k in range(nch):
            for hh2 in range(0, cols, 2048):
                nnh = min(2048, cols - hh2)
                OP("QS", ("HB",), ("arF1",),
                   lambda en, k=k, src=src, hh2=hh2, nnh=nnh: en.dma_start(
                       out=arF1[:, 0:nnh], in_=src[k * 128:(k + 1) * 128, hh2:hh2 + nnh]))
                for sc0 in range(0, nnh, 512):
                    OP("V", ("arF1", "swrec"), ("sqb",),
                       lambda v, ci=ci, sc0=sc0: v.tensor_scalar(
                           out=sqb[:, 0:512], in0=arF1[:, sc0:sc0 + 512],
                           scalar1=swrec[:, ci:ci + 1], scalar2=1.0,
                           op0=ALU.mult, op1=ALU.min))
                    OP("V", ("sqb",), ("sqb",),
                       lambda v: v.tensor_scalar(out=sqb[:, 512:1024], in0=sqb[:, 0:512],
                                                 scalar1=-1.0, scalar2=RMAGIC,
                                                 op0=ALU.max, op1=ALU.add))
                    OP("V", ("sqb",), ("arB1",),
                       lambda v, hh2=hh2, sc0=sc0: v.tensor_scalar(
                           out=arB1[:, hh2 + sc0:hh2 + sc0 + 512], in0=sqb[:, 512:1024],
                           scalar1=-RMAGIC, scalar2=None, op0=ALU.add))
            OP("QG", ("arB1",), ("HQW",),
               lambda g, k=k, dst=dst, cols=cols: g.dma_start(
                   out=dst[k * 128:(k + 1) * 128, :], in_=arB1[:, 0:cols]))

    wq_quant(g0, g0q, D, INNER, 0)
    wq_quant(u0, u0q, D, INNER, 1)
    wq_quant(d0, d0q, INNER, D, 2)

    def precast(src, dstb, rows, cols):
        for k in range(rows // 128):
            for c0 in range(0, cols, 2048):
                nn = min(2048, cols - c0)
                OP("QS", ("HB",), ("arF1",),
                   lambda en, src=src, k=k, c0=c0, nn=nn: en.dma_start(
                       out=arF1[:, 0:nn], in_=src[k * 128:(k + 1) * 128, c0:c0 + nn]))
                OP("V", ("arF1",), ("arB1",),
                   lambda v, nn=nn: v.tensor_copy(arB1[:, 0:nn], arF1[:, 0:nn]))
                OP("QS", ("arB1",), ("HQW",),
                   lambda en, dstb=dstb, k=k, c0=c0, nn=nn: en.dma_start(
                       out=dstb[k * 128:(k + 1) * 128, c0:c0 + nn], in_=arB1[:, 0:nn]))

    precast(w1, w1b, D, INNER)
    precast(w2, w2b, D, INNER)
    precast(h1, h1b, INNER, INNER)
    precast(h2, h2b, INNER, INNER)
    precast(w3, w3b, INNER, D)
    precast(wq1, wq1b, D, D)
    precast(wk1, wk1b, D, D)
    precast(wv1, wv1b, D, D)
    P.barrier()
    P.flush(block)

    def layernorm(xf_buf, xf, sq_buf, sq, xn_buf, xn):
        def st_emit(t):
            last = None
            for c in range(8):
                last = t.matmul(psC[0:1, 0:512], ones1f[:], xf[:, c * 512:(c + 1) * 512],
                                start=(c == 0), stop=(c == 7))
            return last
        OP("T", (xf_buf, "ones"), ("psC",), st_emit)
        for c in range(8):
            sl = (c % 2) * 512
            OP("S", (xf_buf,), (f"sqs{c % 2}",),
               lambda s, c=c, sl=sl: s.activation(out=sqb[:, sl:sl + 512],
                                                  in_=xf[:, c * 512:(c + 1) * 512],
                                                  func=AF.Square))
            OP("T", (f"sqs{c % 2}", "ones"), ("psD",),
               lambda t, c=c, sl=sl: t.matmul(psD[0:1, 0:512], ones1f[:],
                                              sqb[:, sl:sl + 512],
                                              start=(c == 0), stop=(c == 7)))
        OP("S", ("psC",), ("rowA",),
           lambda s: s.activation(out=rowA[0:1, 0:512], in_=psC[0:1, 0:512],
                                  func=AF.Identity, scale=1.0 / D))
        OP("S", ("psD",), ("rowB",),
           lambda s: s.activation(out=rowB[0:1, 0:512], in_=psD[0:1, 0:512],
                                  func=AF.Identity, scale=1.0 / D))
        OP("V", ("rowA",), ("rowC",),
           lambda v: v.tensor_tensor(out=rowC[0:1, 0:512], in0=rowA[0:1, 0:512],
                                     in1=rowA[0:1, 0:512], op=ALU.mult))
        OP("V", ("rowB", "rowC"), ("rowC",),
           lambda v: v.tensor_tensor(out=rowC[0:1, 0:512], in0=rowB[0:1, 0:512],
                                     in1=rowC[0:1, 0:512], op=ALU.subtract))
        OP("V", ("rowC",), ("rowC",),
           lambda v: v.tensor_scalar(out=rowC[0:1, 0:512], in0=rowC[0:1, 0:512],
                                     scalar1=EPS, scalar2=None, op0=ALU.add))
        OP("S", ("rowC",), ("rowC",),
           lambda s: s.activation(out=rowC[0:1, 0:512], in_=rowC[0:1, 0:512], func=AF.Ln))
        OP("S", ("rowC",), ("rowC",),
           lambda s: s.activation(out=rowC[0:1, 0:512], in_=rowC[0:1, 0:512],
                                  func=AF.Exp, scale=-0.5))
        OP("T", ("rowA", "ones"), ("psC",),
           lambda t: t.matmul(psC[:, 0:512], onesrf[:], rowA[0:1, 0:512],
                              start=True, stop=True))
        OP("S", ("psC",), ("amaxb",), lambda s: s.copy(amaxb[:, 0:512], psC[:, 0:512]))
        OP("T", ("rowC", "ones"), ("psD",),
           lambda t: t.matmul(psD[:, 0:512], onesrf[:], rowC[0:1, 0:512],
                              start=True, stop=True))
        OP("S", ("psD",), ("amaxb",), lambda s: s.copy(amaxb[:, 512:1024], psD[:, 0:512]))
        for c in range(8):
            eng = "V" if c % 4 != 3 else "G"
            sl = (c % 2) * 512
            OP(eng, (xf_buf, "amaxb"), (f"sqs{c % 2}",),
               lambda v, c=c, sl=sl: v.tensor_tensor(out=sqb[:, sl:sl + 512],
                                                     in0=xf[:, c * 512:(c + 1) * 512],
                                                     in1=amaxb[:, 0:512], op=ALU.subtract))
            OP(eng, (f"sqs{c % 2}", "amaxb"), (xn_buf,),
               lambda v, c=c, sl=sl: v.tensor_tensor(out=xn[:, c * 512:(c + 1) * 512],
                                                     in0=sqb[:, sl:sl + 512],
                                                     in1=amaxb[:, 512:1024], op=ALU.mult))

    def rope(src_buf, src, dst_buf, dst, t0, n):
        OP("T", (src_buf, "srot2"), ("psB",),
           lambda t: t.matmul(psB[:, 0:n], srot2[:], src, start=True, stop=True))
        OP("S", ("psB",), ("tb1",), lambda s: s.copy(tb1[:, 0:n], psB[:, 0:n]))
        OP("V", (src_buf, "rcos"), (src_buf,),
           lambda v: v.tensor_tensor(out=src, in0=src, in1=hbuf2[:, t0:t0 + n], op=ALU.mult))
        OP("V", ("tb1", "rsin"), ("tb1",),
           lambda v: v.tensor_tensor(out=tb1[:, 0:n], in0=tb1[:, 0:n],
                                     in1=hbuf2[:, 4096 + t0:4096 + t0 + n], op=ALU.mult))
        OP("V", (src_buf, "tb1"), (dst_buf,),
           lambda v: v.tensor_tensor(out=dst, in0=src, in1=tb1[:, 0:n], op=ALU.add))

    P.mark("A")
    for tau in range(NT):
        t0 = tau * TT
        for hh2 in range(2):
            OP("QS", ("HB",), ("arF1",),
               lambda en, t0=t0, hh2=hh2: en.dma_start(
                   out=arF1[:],
                   in_=ap3(x_in, (t0 + hh2 * 256) * D, [[D, 128], [128 * D, 2], [1, D]])))
            for i2 in range(2):
                i = hh2 * 2 + i2
                ps, pk = (psA, "psA") if i % 2 == 0 else (psB, "psB")
                def tr_emit(t, i2=i2, ps=ps):
                    last = None
                    for c in range(8):
                        last = t.transpose(ps[:, c * 128:(c + 1) * 128],
                                           arF1[:, i2 * 1024 + c * 128:i2 * 1024 + (c + 1) * 128],
                                           idf[:])
                    return last
                OP("T", ("arF1", "idf"), (pk,), tr_emit)
                OP("S", (pk,), ("arF2",),
                   lambda s, i=i, ps=ps: s.copy(
                       ap3(arF2, i * 128, [[4096, 128], [512, 8], [1, 128]]), ps[:]))
        OP("QA", ("arF2",), ("xTh",),
           lambda en, t0=t0: en.dma_start(
               out=ap3(xTh, t0, [[T, 128], [128 * T, 8], [1, 512]]),
               in_=ap3(arF2, 0, [[4096, 128], [512, 8], [1, 512]])))
        layernorm("arF2", arF2, None, None, "arB2", arB2)
        for m2 in range(2):
            def kp_emit(t, m2=m2):
                last = None
                for c in range(8):
                    last = t.matmul(psC[:, 0:512],
                                    hbuf[:, c * 256 + m2 * 128:c * 256 + (m2 + 1) * 128],
                                    arB2[:, c * 512:(c + 1) * 512],
                                    start=(c == 0), stop=(c == 7))
                return last
            OP("T", ("arB2", "wk0b"), ("psC",), kp_emit)
            OP("S", ("psC", "sbk0"), ("arB3",),
               lambda s, m2=m2: s.activation(out=arB3[:, 0:512], in_=psC[:, 0:512],
                                             func=AF.Identity, bias=sbk0[:, m2:m2 + 1]))
            rope("arB3", arB3[:, 0:512], "kv0k",
                 kv0k[:, m2 * WTP + PAD + t0:m2 * WTP + PAD + t0 + 512], t0, 512)
        for i in range(4):
            def vp_emit(t, i=i):
                for c in range(8):
                    t.matmul(psD[:, 0:256],
                             arB2[:, c * 512 + i * 128:c * 512 + (i + 1) * 128],
                             hbuf[:, 2048 + c * 256:2048 + (c + 1) * 256],
                             start=(c == 0), stop=False)
                return t.matmul(psD[:, 0:256], onesrb[:, 0:128], bvb[:],
                                start=False, stop=True)
            OP("T", ("arB2", "wv0b", "bvb", "ones"), ("psD",), vp_emit)
            OP("S", ("psD",), ("v0t",),
               lambda s, tau=tau, i=i: s.copy(
                   v0t[:, (tau * 4 + i) * 256:(tau * 4 + i + 1) * 256], psD[:, 0:256]))
        for m in range(8):
            def qp_emit(t, m=m):
                last = None
                for c in range(8):
                    last = t.matmul(psC[:, 0:512],
                                    wbufA[:, c * 1024 + m * 128:c * 1024 + (m + 1) * 128],
                                    arB2[:, c * 512:(c + 1) * 512],
                                    start=(c == 0), stop=(c == 7))
                return last
            OP("T", ("arB2", "wq0b"), ("psC",), qp_emit)
            OP("S", ("psC", "sbqhw", "shw8"), ("arB3",),
               lambda s, m=m: s.activation(out=arB3[:, 0:512], in_=psC[:, 0:512],
                                           func=AF.Identity, bias=sbqhw[:, m:m + 1],
                                           scale=shw8[:, m:m + 1]))
            rope("arB3", arB3[:, 0:512], "arB1", arB1[:, 0:512], t0, 512)
            OP("QA", ("arB1",), ("qT0h",),
               lambda en, m=m, t0=t0: en.dma_start(
                   out=ap3(qT0h, m * 128 * T + t0, [[T, 128], [1, 512]]),
                   in_=arB1[:, 0:512]))
        for c in range(8):
            sl = (c % 2) * 512
            OP("S", ("arB2",), (f"sqs{c % 2}",),
               lambda s, c=c, sl=sl: s.activation(out=sqb[:, sl:sl + 512],
                                                  in_=arB2[:, c * 512:(c + 1) * 512],
                                                  func=AF.Abs))
            if c == 0:
                OP("V", ("sqs0",), ("amaxb",),
                   lambda v: v.tensor_copy(amaxb[:, 0:512], sqb[:, 0:512]))
            else:
                OP("V", (f"sqs{c % 2}", "amaxb"), ("amaxb",),
                   lambda v, sl=sl: v.tensor_tensor(out=amaxb[:, 0:512],
                                                    in0=amaxb[:, 0:512],
                                                    in1=sqb[:, sl:sl + 512], op=ALU.max))
        for lev in (64, 32):
            for rp in range(0, lev, 32):
                OP("V", ("amaxb",), ("sqb",),
                   lambda v, lev=lev, rp=rp: v.tensor_copy(
                       sqb[rp:rp + 32, 0:512], amaxb[lev + rp:lev + rp + 32, 0:512]))
            OP("V", ("amaxb", "sqb"), ("amaxb",),
               lambda v, lev=lev: v.tensor_tensor(out=amaxb[0:lev, 0:512],
                                                  in0=amaxb[0:lev, 0:512],
                                                  in1=sqb[0:lev, 0:512], op=ALU.max))
        OP("V", ("amaxb",), ("amaxc",),
           lambda v: v.tensor_copy(sqb[0:32, 512:1024], amaxb[0:32, 0:512]))
        def amt_emit(t):
            last = None
            for p in range(4):
                last = t.transpose(psC[:, p * 32:(p + 1) * 32],
                                   sqb[0:32, 512 + p * 128:512 + (p + 1) * 128],
                                   idf[0:32, 0:32])
            return last
        OP("T", ("amaxc", "idf"), ("psC",), amt_emit)
        OP("S", ("psC",), ("sqb",), lambda s: s.copy(sqb[:, 0:128], psC[:, 0:128]))
        OP("V", ("sqb",), ("amaxt",),
           lambda v: v.reduce_max(out=amaxb[:, 0:4],
                                  in_=ap3(sqb, 0, [[1024, 128], [32, 4], [1, 32]]),
                                  axis=AX.X))
        OP("V", ("amaxt",), ("amaxt",),
           lambda v: v.tensor_scalar(out=amaxb[:, 0:4], in0=amaxb[:, 0:4],
                                     scalar1=1e-5, scalar2=None, op0=ALU.max))
        OP("S", ("amaxt",), ("amaxt2",),
           lambda s: s.activation(out=amaxb[:, 4:8], in_=amaxb[:, 0:4], func=AF.Ln))
        OP("S", ("amaxt2",), ("amaxt2",),
           lambda s: s.activation(out=amaxb[:, 4:8], in_=amaxb[:, 4:8],
                                  func=AF.Exp, scale=-1.0, bias=cl127[:, 0:1]))
        OP("V", ("amaxt",), ("amaxt3",),
           lambda v: v.tensor_scalar(out=amaxb[:, 8:12], in0=amaxb[:, 0:4],
                                     scalar1=1.0 / 127.0, scalar2=None, op0=ALU.mult))
        def rowt_emit(t):
            last = None
            for p in range(4):
                last = t.transpose(psC[0:1, p * 128:(p + 1) * 128],
                                   amaxb[:, 4 + p:5 + p], idf[:])
                last = t.transpose(psD[0:1, p * 128:(p + 1) * 128],
                                   amaxb[:, 8 + p:9 + p], idf[:])
            return last
        OP("T", ("amaxt2", "amaxt3", "idf"), ("psC", "psD"), rowt_emit)
        OP("S", ("psC",), ("rowB",), lambda s: s.copy(rowB[0:1, 0:512], psC[0:1, 0:512]))
        OP("S", ("psD",), ("rowC",), lambda s: s.copy(rowC[0:1, 0:512], psD[0:1, 0:512]))
        OP("T", ("rowB", "ones"), ("psC",),
           lambda t: t.matmul(psC[:, 0:512], onesrf[:], rowB[0:1, 0:512],
                              start=True, stop=True))
        OP("S", ("psC",), ("amaxb",), lambda s: s.copy(amaxb[:, 0:512], psC[:, 0:512]))
        OP("T", ("rowC", "ones"), ("psD",),
           lambda t: t.matmul(psD[:, 0:512], onesrf[:], rowC[0:1, 0:512],
                              start=True, stop=True))
        OP("S", ("psD",), ("amaxb",), lambda s: s.copy(amaxb[:, 512:1024], psD[:, 0:512]))
        for c in range(8):
            OP("V", ("arB2", "amaxb"), ("sqb",),
               lambda v, c=c: v.tensor_tensor(out=sqb[:, 0:512],
                                              in0=arB2[:, c * 512:(c + 1) * 512],
                                              in1=amaxb[:, 0:512], op=ALU.mult))
            OP("V", ("sqb",), ("sqb",),
               lambda v: v.tensor_scalar(out=sqb[:, 512:1024], in0=sqb[:, 0:512],
                                         scalar1=RMAGIC, scalar2=-RMAGIC,
                                         op0=ALU.add, op1=ALU.add))
            OP("V", ("sqb", "amaxb"), ("arB1",),
               lambda v: v.tensor_tensor(out=arB1[:, 0:512], in0=sqb[:, 512:1024],
                                         in1=amaxb[:, 512:1024], op=ALU.mult))
            OP("QA", ("arB1",), ("xqh",),
               lambda en, c=c, t0=t0: en.dma_start(
                   out=ap3(xqh, c * 128 * T + t0, [[T, 128], [1, 512]]),
                   in_=arB1[:, 0:512]))
    P.barrier()
    P.flush(block)

    P.mark("B")
    OP("V", (), ("knrun",), lambda v: v.memset(rowA[0:1, 600:601], 0.0))
    for seg0 in range(0, WTP, 512):
        L = min(512, WTP - seg0)
        for ck in range(2):
            sl = (ck % 2) * 512
            OP("V", ("kv0k",), (f"sqs{ck}",),
               lambda v, ck=ck, seg0=seg0, L=L, sl=sl: v.tensor_tensor(
                   out=sqb[:, sl:sl + L],
                   in0=kv0k[:, ck * WTP + seg0:ck * WTP + seg0 + L],
                   in1=kv0k[:, ck * WTP + seg0:ck * WTP + seg0 + L], op=ALU.mult))
            OP("T", (f"sqs{ck}", "ones"), ("psC",),
               lambda t, ck=ck, L=L, sl=sl: t.matmul(
                   psC[0:1, 0:L], ones1f[:], sqb[:, sl:sl + L],
                   start=(ck == 0), stop=(ck == 1)))
        OP("S", ("psC",), ("rowB",),
           lambda sc, L=L: sc.copy(rowB[0:1, 0:L], psC[0:1, 0:L]))
        OP("V", ("rowB",), ("knseg",),
           lambda v, L=L: v.reduce_max(out=rowB[0:1, 630:631], in_=rowB[0:1, 0:L],
                                       axis=AX.X))
        OP("V", ("knseg", "knrun"), ("knrun",),
           lambda v: v.tensor_tensor(out=rowA[0:1, 600:601], in0=rowA[0:1, 600:601],
                                     in1=rowB[0:1, 630:631], op=ALU.max))
    OP("S", ("knrun",), ("knrun",),
       lambda sc: sc.activation(out=rowA[0:1, 600:601], in_=rowA[0:1, 600:601],
                                func=AF.Sqrt))
    C_PT0W, C_PT0G, C_VGP, C_ZEBE, C_ZEBO = 2560, 3584, 4608, 5120, 5632
    OP("V", (), ("vg4",),
       lambda v: v.memset(wbufA[0:32, C_VGP:C_VGP + 512], 0.0))
    for g in range(4):
        OP("V", ("v0t",), ("vg4",),
           lambda v, g=g: v.tensor_copy(
               wbufA[0:32, C_VGP + g * 128 + (g % 2) * 64:C_VGP + g * 128 + (g % 2) * 64 + 64],
               v0t[0:32, g * 64:(g + 1) * 64]))
    for tau in range(NT):
        t0 = tau * TT
        OP("QS", ("qT0h",), ("arB2",),
           lambda en, t0=t0: en.dma_start(
               out=ap3(arB2, 0, [[4096, 128], [512, 8], [1, 512]]),
               in_=ap3(qT0h, t0, [[T, 128], [128 * T, 8], [1, 512]])))
        for c in range(8):
            sl = (c % 2) * 512
            OP("V", ("arB2",), (f"sqs{c % 2}",),
               lambda v, c=c, sl=sl: v.tensor_tensor(
                   out=sqb[:, sl:sl + 512], in0=arB2[:, c * 512:(c + 1) * 512],
                   in1=arB2[:, c * 512:(c + 1) * 512], op=ALU.mult))
            OP("T", (f"sqs{c % 2}", "ones"), ("psC",),
               lambda t, c=c, sl=sl: t.matmul(psC[0:1, 0:512], ones1f[:],
                                              sqb[:, sl:sl + 512],
                                              start=(c == 0), stop=(c == 7)))
        OP("S", ("psC",), ("rowB",),
           lambda sc: sc.copy(rowB[0:1, 0:512], psC[0:1, 0:512]))
        OP("V", ("rowB",), ("qn2",),
           lambda v: v.reduce_max(out=rowB[0:1, 620:621], in_=rowB[0:1, 0:512],
                                  axis=AX.X))
        OP("S", ("qn2",), ("qn2",),
           lambda sc: sc.activation(out=rowB[0:1, 620:621], in_=rowB[0:1, 620:621],
                                    func=AF.Sqrt))
        OP("V", ("qn2", "knrun"), ("Cval",),
           lambda v: v.tensor_tensor(out=rowB[0:1, 621:622], in0=rowB[0:1, 620:621],
                                     in1=rowA[0:1, 600:601], op=ALU.mult))
        OP("V", ("Cval",), ("Cval",),
           lambda v: v.tensor_scalar(out=rowB[0:1, 621:622], in0=rowB[0:1, 621:622],
                                     scalar1=-1.0, scalar2=None, op0=ALU.mult))
        OP("G", ("Cval",), ("sCneg",),
           lambda g: g.partition_broadcast(sCneg[:], rowB[0:1, 621:622], channels=64))
        OP("V", (), ("vov",), lambda v: v.memset(wbufB[:, 0:4096], 0.0))
        for s8 in range(8):
            sg = tau * 8 + s8
            base = s8 * 512
            for rp in range(4):
                tok0 = 64 * sg - 32 + rp * 32
                if tok0 < 0 or tok0 >= T:
                    continue
                c0 = tok0 // 128
                r0 = tok0 % 128
                for g in range(4):
                    OP("V", ("v0t",), ("vov",),
                       lambda v, base=base, rp=rp, c0=c0, r0=r0, g=g: v.tensor_copy(
                           wbufB[rp * 32:(rp + 1) * 32,
                                 base + g * 128 + (g % 2) * 64:base + g * 128 + (g % 2) * 64 + 64],
                           v0t[r0:r0 + 32, c0 * 256 + g * 64:c0 * 256 + (g + 1) * 64]))
        for s8 in range(8):
            i0 = t0 + s8 * 64
            sg = i0 // 64
            mvar = 0 if sg == 0 else (2 if sg == 63 else 1)
            wcol0 = PAD + 64 * sg - 32
            OP("V", ("arB2", "pmask"), ("zebE",),
               lambda v, s8=s8: v.tensor_scalar(
                   out=ap3(wbufA, C_ZEBE, [[8192, 128], [64, 8], [1, 64]]),
                   in0=ap3(arB2, s8 * 64, [[4096, 128], [512, 8], [1, 64]]),
                   scalar1=pmask[:, 0:1], scalar2=None, op0=ALU.mult))
            OP("V", ("arB2", "pmask"), ("zebO",),
               lambda v, s8=s8: v.tensor_scalar(
                   out=ap3(wbufA, C_ZEBO, [[8192, 128], [64, 8], [1, 64]]),
                   in0=ap3(arB2, s8 * 64, [[4096, 128], [512, 8], [1, 64]]),
                   scalar1=pmask[:, 1:2], scalar2=None, op0=ALU.mult))
            def sc_emit(t, wcol0=wcol0, lo=True):
                last = None
                ps = psA if lo else psB
                for j in (range(0, 8) if lo else range(8, 16)):
                    cch = j // 2
                    p = j % 2
                    ck = 0 if cch < 4 else 1
                    slot = j % 8
                    zc = C_ZEBE if p == 0 else C_ZEBO
                    last = t.matmul(ps[0:64, slot * 128:(slot + 1) * 128],
                                    wbufA[:, zc + cch * 64:zc + (cch + 1) * 64],
                                    kv0k[:, ck * WTP + wcol0:ck * WTP + wcol0 + 128],
                                    start=True, stop=True)
                return last
            OP("T", ("zebE", "zebO", "kv0k"), ("psA",), lambda t, w=wcol0: sc_emit(t, w, True))
            OP("T", ("zebE", "zebO", "kv0k"), ("psB",), lambda t, w=wcol0: sc_emit(t, w, False))
            def sg_emit(t):
                last = None
                for j in range(16):
                    cch = j // 2
                    p = j % 2
                    ck = 0 if cch < 4 else 1
                    zc = C_ZEBE if p == 0 else C_ZEBO
                    last = t.matmul(psE[0:64, j * 8:(j + 1) * 8],
                                    wbufA[:, zc + cch * 64:zc + (cch + 1) * 64],
                                    kv0k[:, ck * WTP + PAD:ck * WTP + PAD + 8],
                                    start=True, stop=True)
                return last
            OP("T", ("zebE", "zebO", "kv0k"), ("psE",), sg_emit)
            OP("S", ("psA", "sCneg"), ("p0w",),
               lambda sc: sc.activation(out=wbufA[0:64, 0:1024], in_=psA[0:64, :],
                                        func=AF.Exp, bias=sCneg[:]))
            OP("S", ("psB", "sCneg"), ("p0w",),
               lambda sc: sc.activation(out=wbufA[0:64, 1024:2048], in_=psB[0:64, :],
                                        func=AF.Exp, bias=sCneg[:]))
            OP("V", (), ("p0g",), lambda v: v.memset(wbufA[0:64, 2048:2560], 0.0))
            OP("S", ("psE", "sCneg", "p0g"), ("p0g",),
               lambda sc: sc.activation(
                   out=ap3(wbufA, 2048, [[8192, 64], [32, 16], [1, 8]]),
                   in_=ap3(psE, 0, [[512, 64], [8, 16], [1, 8]]),
                   func=AF.Exp, bias=sCneg[:]))
            OP("V", ("p0w", "sm0"), ("p0w",),
               lambda v, mvar=mvar: v.tensor_tensor(
                   out=wbufA[0:64, 0:2048], in0=wbufA[0:64, 0:2048],
                   in1=hbuf2[0:64, 8192 + mvar * 2048:8192 + (mvar + 1) * 2048],
                   op=ALU.mult))
            OP("V", ("p0g", "sm0g"), ("p0g",),
               lambda v, mvar=mvar: v.tensor_tensor(
                   out=wbufA[0:64, 2048:2560], in0=wbufA[0:64, 2048:2560],
                   in1=hbuf2[0:64, 14336 + mvar * 512:14336 + (mvar + 1) * 512],
                   op=ALU.mult))
            OP("V", ("p0w",), ("l0_s",),
               lambda v: v.reduce_sum(out=l0_s[:],
                                      in_=ap3(wbufA, 0, [[8192, 64], [128, 16], [1, 128]]),
                                      axis=AX.X))
            OP("V", ("p0g",), ("r0_s",),
               lambda v: v.reduce_sum(out=r0_s[:],
                                      in_=ap3(wbufA, 2048, [[8192, 64], [32, 16], [1, 32]]),
                                      axis=AX.X))
            OP("V", ("l0_s", "r0_s"), ("l0_s",),
               lambda v: v.tensor_tensor(out=l0_s[:], in0=l0_s[:], in1=r0_s[:], op=ALU.add))
            OP("V", ("l0_s",), ("r0_s",), lambda v: v.reciprocal(out=r0_s[:], in_=l0_s[:]))
            OP("V", ("p0w", "r0_s"), ("p0w",),
               lambda v: v.tensor_tensor(
                   out=ap3(wbufA, 0, [[8192, 64], [128, 16], [1, 128]]),
                   in0=ap3(wbufA, 0, [[8192, 64], [128, 16], [1, 128]]),
                   in1=ap3(r0_s, 0, [[16, 64], [1, 16], [0, 128]]), op=ALU.mult))
            OP("V", ("p0g", "r0_s"), ("p0g",),
               lambda v: v.tensor_tensor(
                   out=ap3(wbufA, 2048, [[8192, 64], [32, 16], [1, 32]]),
                   in0=ap3(wbufA, 2048, [[8192, 64], [32, 16], [1, 32]]),
                   in1=ap3(r0_s, 0, [[16, 64], [1, 16], [0, 32]]), op=ALU.mult))
            def tpw_emit(t):
                last = None
                for h in range(16):
                    last = t.transpose(psT[:, h * 64:(h + 1) * 64],
                                       wbufA[0:64, h * 128:(h + 1) * 128],
                                       ident[0:64, 0:64])
                return last
            OP("T", ("p0w", "ident"), ("psT",), tpw_emit)
            OP("S", ("psT",), ("pT0w",),
               lambda sc: sc.copy(wbufA[:, C_PT0W:C_PT0W + 1024], psT[:]))
            def tpg_emit(t):
                last = None
                for j in range(16):
                    last = t.transpose(psT[0:32, j * 64:(j + 1) * 64],
                                       wbufA[0:64, 2048 + j * 32:2048 + (j + 1) * 32],
                                       ident[0:64, 0:64])
                return last
            OP("T", ("p0g", "ident"), ("psT",), tpg_emit)
            OP("S", ("psT",), ("pT0g",),
               lambda sc: sc.copy(wbufA[0:32, C_PT0G:C_PT0G + 1024], psT[0:32, 0:1024]))
            def av_emit(t, s8=s8):
                last = None
                for cch in range(8):
                    for p in range(2):
                        j = 2 * cch + p
                        g = (0 if cch < 4 else 2) + p
                        last = t.matmul(
                            psC[:, cch * 64:(cch + 1) * 64],
                            wbufB[:, s8 * 512 + g * 128:s8 * 512 + (g + 1) * 128],
                            wbufA[:, C_PT0W + j * 64:C_PT0W + (j + 1) * 64],
                            start=(p == 0), stop=False)
                        last = t.matmul(
                            psC[:, cch * 64:(cch + 1) * 64],
                            wbufA[0:32, C_VGP + g * 128:C_VGP + (g + 1) * 128],
                            wbufA[0:32, C_PT0G + j * 64:C_PT0G + (j + 1) * 64],
                            start=False, stop=(p == 1))
                return last
            OP("T", ("vov", "pT0w", "pT0g", "vg4"), ("psC",), av_emit)
            OP("S", ("psC",), ("arB4",),
               lambda sc, s8=s8: sc.copy(
                   ap3(arB4, s8 * 64, [[4096, 128], [512, 8], [1, 64]]),
                   ap3(psC, 0, [[512, 128], [64, 8], [1, 64]])))
        for m in range(8):
            def wo_emit(t, m=m):
                last = None
                for c in range(8):
                    last = t.matmul(psD[:, 0:512],
                                    hbuf[:, 4096 + c * 1024 + m * 128:4096 + c * 1024 + (m + 1) * 128],
                                    arB4[:, c * 512:(c + 1) * 512],
                                    start=(c == 0), stop=(c == 7))
                return last
            OP("T", ("arB4", "wo0b"), ("psD",), wo_emit)
            OP("S", ("psD", "sbo0"), ("arB1",),
               lambda s, m=m: s.activation(out=arB1[:, 0:512], in_=psD[:, 0:512],
                                           func=AF.Identity, bias=sbo0[:, m:m + 1]))
            OP("QA", ("arB1",), ("aoth",),
               lambda en, m=m, t0=t0: en.dma_start(
                   out=ap3(aoth, m * 128 * T + t0, [[T, 128], [1, 512]]),
                   in_=arB1[:, 0:512]))

    P.mark("Bg")
    OP("QS", ("qT0h",), ("gq_s",),
       lambda en: en.dma_start(out=ap3(gq_s, 0, [[64, 128], [8, 8], [1, 8]]),
                               in_=ap3(qT0h, 0, [[T, 128], [128 * T, 8], [1, 8]])))
    OP("V", (), ("arB1",), lambda v: v.memset(arB1[:, 0:256], 0.0))
    for j in range(16):
        cch = j // 2
        p = j % 2
        g = (0 if cch < 4 else 2) + p
        ck = g // 2
        for rp in range(2):
            OP("V", ("gq_s",), ("arB1",),
               lambda v, j=j, p=p, ck=ck, rp=rp: v.tensor_copy(
                   arB1[p * 64 + rp * 32:p * 64 + rp * 32 + 32,
                        ck * 128 + j * 8:ck * 128 + (j + 1) * 8],
                   gq_s[p * 64 + rp * 32:p * 64 + rp * 32 + 32,
                        cch * 8:cch * 8 + 8]))
    for nb in range(8):
        ps, pk = (psC, "psC") if nb % 2 == 0 else (psD, "psD")
        def gs_emit(t, nb=nb, ps=ps):
            last = None
            for ck in range(2):
                last = t.matmul(ps[:, 0:512], arB1[:, ck * 128:(ck + 1) * 128],
                                kv0k[:, ck * WTP + PAD + nb * 512:ck * WTP + PAD + (nb + 1) * 512],
                                start=(ck == 0), stop=(ck == 1))
            return last
        OP("T", ("arB1", "kv0k"), (pk,), gs_emit)
        OP("S", (pk,), ("arF2",),
           lambda s, nb=nb, ps=ps: s.copy(arF2[:, nb * 512:(nb + 1) * 512], ps[:, 0:512]))
    OP("V", ("arF2",), ("gm",),
       lambda v: v.reduce_max(out=arF1[:, 0:1], in_=arF2[:], axis=AX.X))
    OP("V", ("gm",), ("gm",),
       lambda v: v.tensor_scalar(out=arF1[:, 1:2], in0=arF1[:, 0:1],
                                 scalar1=-1.0, scalar2=None, op0=ALU.mult))
    OP("S", ("arF2", "gm"), ("arB3", "gl"),
       lambda s: s.activation(out=arB3[:], in_=arF2[:], func=AF.Exp,
                              bias=arF1[:, 1:2], accum_out=arF1[:, 2:3]))
    OP("V", ("gl",), ("gr",),
       lambda v: v.reciprocal(out=arF1[:, 3:4], in_=arF1[:, 2:3]))
    OP("V", ("arB3", "gr"), ("arB3",),
       lambda v: v.tensor_scalar(out=arB3[:], in0=arB3[:], scalar1=arF1[:, 3:4],
                                 scalar2=None, op0=ALU.mult))
    for grp in range(4):
        def gt_emit(t, grp=grp):
            last = None
            for j in range(8):
                kk = grp * 8 + j
                last = t.transpose(psT[:, j * 128:(j + 1) * 128],
                                   arB3[:, kk * 128:(kk + 1) * 128], ident[:])
            return last
        OP("T", ("arB3", "ident"), ("psT",), gt_emit)
        OP("S", ("psT",), ("arB2",),
           lambda s, grp=grp: s.copy(arB2[:, grp * 1024:(grp + 1) * 1024], psT[:]))
    def gav_emit(t):
        last = None
        for kk in range(32):
            last = t.matmul(psE[:, 0:256], arB2[:, kk * 128:(kk + 1) * 128],
                            v0t[:, kk * 256:(kk + 1) * 256],
                            start=(kk == 0), stop=(kk == 31))
        return last
    OP("T", ("arB2", "v0t"), ("psE",), gav_emit)
    OP("S", ("psE",), ("arB4",), lambda s: s.copy(arB4[:, 0:256], psE[:, 0:256]))
    def ghead_emit(t):
        t.transpose(psT[:, 0:128], arB4[:, 0:128], ident[:])
        return t.transpose(psT[:, 128:256], arB4[:, 128:256], ident[:])
    OP("T", ("arB4", "ident"), ("psT",), ghead_emit)
    OP("S", ("psT",), ("arB3",), lambda s: s.copy(arB3[:, 0:256], psT[:, 0:256]))
    for j in range(16):
        cch = j // 2
        p = j % 2
        g = (0 if cch < 4 else 2) + p
        for rp in range(2):
            OP("V", ("arB3",), ("gout",),
               lambda v, j=j, cch=cch, p=p, g=g, rp=rp: v.tensor_copy(
                   arF2[p * 64 + rp * 32:p * 64 + rp * 32 + 32,
                        cch * 8:cch * 8 + 8],
                   arB3[p * 64 + rp * 32:p * 64 + rp * 32 + 32,
                        (g // 2) * 128 + j * 8:(g // 2) * 128 + (j + 1) * 8]))
    for m in range(8):
        def gwo_emit(t, m=m):
            last = None
            for c in range(8):
                last = t.matmul(psE[:, 0:8],
                                hbuf[:, 4096 + c * 1024 + m * 128:4096 + c * 1024 + (m + 1) * 128],
                                arF3[:, c * 8:(c + 1) * 8].astype(BF16) if False else arF3[:, c * 8:(c + 1) * 8],
                                start=(c == 0), stop=(c == 7))
            return last
        pass
    OP("V", ("gout",), ("goutb",), lambda v: v.tensor_copy(arB4[:, 0:64], arF2[:, 0:64]))
    for m in range(8):
        def gwo_emit(t, m=m):
            last = None
            for c in range(8):
                last = t.matmul(psE[:, 0:8],
                                hbuf[:, 4096 + c * 1024 + m * 128:4096 + c * 1024 + (m + 1) * 128],
                                arB4[:, c * 8:(c + 1) * 8],
                                start=(c == 0), stop=(c == 7))
            return last
        OP("T", ("goutb", "wo0b"), ("psE",), gwo_emit)
        OP("S", ("psE", "sbo0"), ("arB1",),
           lambda s, m=m: s.activation(out=arB1[:, m * 8:(m + 1) * 8], in_=psE[:, 0:8],
                                       func=AF.Identity, bias=sbo0[:, m:m + 1]))
    OP("QA", ("arB1",), ("aoth",),
       lambda en: en.dma_start(out=ap3(aoth, 0, [[T, 128], [128 * T, 8], [1, 8]]),
                               in_=ap3(arB1, 0, [[4096, 128], [8, 8], [1, 8]])))
    P.barrier()
    P.flush(block)

    P.mark("C")
    for tau in range(NT):
        t0 = tau * TT
        OP("QS", ("xqh",), ("arB2",),
           lambda en, t0=t0: en.dma_start(
               out=ap3(arB2, 0, [[4096, 128], [512, 8], [1, 512]]),
               in_=ap3(xqh, t0, [[T, 128], [128 * T, 8], [1, 512]])))
        OP("QS", ("aoth",), ("arB3",),
           lambda en, t0=t0: en.dma_start(
               out=ap3(arB3, 0, [[4096, 128], [512, 8], [1, 512]]),
               in_=ap3(aoth, t0, [[T, 128], [128 * T, 8], [1, 512]])))
        OP("QS", ("xTh",), ("arF2",),
           lambda en, t0=t0: en.dma_start(
               out=ap3(arF2, 0, [[4096, 128], [512, 8], [1, 512]]),
               in_=ap3(xTh, t0, [[T, 128], [128 * T, 8], [1, 512]])))
        for qm in range(4):
            OP("QS", ("HQW",), ("wbufA",),
               lambda en, qm=qm: en.dma_start(
                   out=ap3(wbufA, 0, [[8192, 128], [1024, 8], [1, 1024]]),
                   in_=ap3(g0q, qm * 1024, [[INNER, 128], [128 * INNER, 8], [1, 1024]])))
            OP("QS", ("HQW",), ("hbuf2w",),
               lambda en, qm=qm: en.dma_start(
                   out=ap3(hbuf2, 0, [[16384, 128], [1024, 8], [1, 1024]]),
                   in_=ap3(u0q, qm * 1024, [[INNER, 128], [128 * INNER, 8], [1, 1024]])))
            for mh in range(8):
                m = qm * 8 + mh
                def ga_emit(t, mh=mh):
                    last = None
                    for c in range(8):
                        last = t.matmul(psC[:, 0:512],
                                        wbufA[:, c * 1024 + mh * 128:c * 1024 + (mh + 1) * 128],
                                        arB2[:, c * 512:(c + 1) * 512],
                                        start=(c == 0), stop=(c == 7))
                    return last
                OP("T", ("arB2", "wbufA"), ("psC",), ga_emit)
                def up_emit(t, mh=mh):
                    last = None
                    for c in range(8):
                        last = t.matmul(psD[:, 0:512],
                                        hbuf2[:, c * 1024 + mh * 128:c * 1024 + (mh + 1) * 128],
                                        arB2[:, c * 512:(c + 1) * 512],
                                        start=(c == 0), stop=(c == 7))
                    return last
                OP("T", ("arB2", "hbuf2w"), ("psD",), up_emit)
                OP("S", ("psC", "swsc"), ("arF1",),
                   lambda s: s.activation(out=arF1[:, 0:512], in_=psC[:, 0:512],
                                          func=AF.Silu, scale=swsc[:, 0:1]))
                OP("S", ("psD", "swsc"), ("arF1b",),
                   lambda s: s.activation(out=arF1[:, 512:1024], in_=psD[:, 0:512],
                                          func=AF.Identity, scale=swsc[:, 1:2]))
                OP("V", ("arF1", "arF1b"), ("hbuf",),
                   lambda v, m=m: v.tensor_tensor(out=hbuf[:, m * 512:(m + 1) * 512],
                                                  in0=arF1[:, 0:512], in1=arF1[:, 512:1024],
                                                  op=ALU.mult))
                if m == 0:
                    OP("V", ("hbuf",), ("amaxb",),
                       lambda v: v.tensor_copy(amaxb[:, 0:512], hbuf[:, 0:512]))
                    OP("V", ("hbuf",), ("amaxb",),
                       lambda v: v.tensor_copy(amaxb[:, 512:1024], hbuf[:, 0:512]))
                else:
                    OP("V", ("hbuf", "amaxb"), ("amaxb",),
                       lambda v, m=m: v.tensor_tensor(out=amaxb[:, 0:512],
                                                      in0=amaxb[:, 0:512],
                                                      in1=hbuf[:, m * 512:(m + 1) * 512],
                                                      op=ALU.max))
                    OP("V", ("hbuf", "amaxb"), ("amaxb",),
                       lambda v, m=m: v.tensor_tensor(out=amaxb[:, 512:1024],
                                                      in0=amaxb[:, 512:1024],
                                                      in1=hbuf[:, m * 512:(m + 1) * 512],
                                                      op=ALU.min))
        OP("V", ("amaxb",), ("amaxb",),
           lambda v: v.tensor_scalar(out=amaxb[:, 512:1024], in0=amaxb[:, 512:1024],
                                     scalar1=-1.0, scalar2=None, op0=ALU.mult))
        OP("V", ("amaxb", "amaxb"), ("amaxb",),
           lambda v: v.tensor_tensor(out=amaxb[:, 0:512], in0=amaxb[:, 0:512],
                                     in1=amaxb[:, 512:1024], op=ALU.max))
        for lev in (64, 32):
            for rp in range(0, lev, 32):
                OP("V", ("amaxb",), ("sqb",),
                   lambda v, lev=lev, rp=rp: v.tensor_copy(
                       sqb[rp:rp + 32, 0:512], amaxb[lev + rp:lev + rp + 32, 0:512]))
            OP("V", ("amaxb", "sqb"), ("amaxb",),
               lambda v, lev=lev: v.tensor_tensor(out=amaxb[0:lev, 0:512],
                                                  in0=amaxb[0:lev, 0:512],
                                                  in1=sqb[0:lev, 0:512], op=ALU.max))
        OP("V", ("amaxb",), ("amaxc",),
           lambda v: v.tensor_copy(sqb[0:32, 512:1024], amaxb[0:32, 0:512]))
        def amt_emit(t):
            last = None
            for p in range(4):
                last = t.transpose(psC[:, p * 32:(p + 1) * 32],
                                   sqb[0:32, 512 + p * 128:512 + (p + 1) * 128],
                                   idf[0:32, 0:32])
            return last
        OP("T", ("amaxc", "idf"), ("psC",), amt_emit)
        OP("S", ("psC",), ("sqb",), lambda s: s.copy(sqb[:, 0:128], psC[:, 0:128]))
        OP("V", ("sqb",), ("amaxt",),
           lambda v: v.reduce_max(out=amaxb[:, 0:4],
                                  in_=ap3(sqb, 0, [[1024, 128], [32, 4], [1, 32]]),
                                  axis=AX.X))
        OP("V", ("amaxt",), ("amaxt",),
           lambda v: v.tensor_scalar(out=amaxb[:, 0:4], in0=amaxb[:, 0:4],
                                     scalar1=1e-5, scalar2=None, op0=ALU.max))
        OP("S", ("amaxt",), ("amaxt2",),
           lambda s: s.activation(out=amaxb[:, 4:8], in_=amaxb[:, 0:4], func=AF.Ln))
        OP("S", ("amaxt2",), ("amaxt2",),
           lambda s: s.activation(out=amaxb[:, 4:8], in_=amaxb[:, 4:8],
                                  func=AF.Exp, scale=-1.0, bias=cl127[:, 0:1]))
        OP("V", ("amaxt",), ("amaxt3",),
           lambda v: v.tensor_scalar(out=amaxb[:, 8:12], in0=amaxb[:, 0:4],
                                     scalar1=1.0 / 127.0, scalar2=None, op0=ALU.mult))
        def rowt_emit(t):
            last = None
            for p in range(4):
                last = t.transpose(psC[0:1, p * 128:(p + 1) * 128],
                                   amaxb[:, 4 + p:5 + p], idf[:])
                last = t.transpose(psD[0:1, p * 128:(p + 1) * 128],
                                   amaxb[:, 8 + p:9 + p], idf[:])
            return last
        OP("T", ("amaxt2", "amaxt3", "idf"), ("psC", "psD"), rowt_emit)
        OP("S", ("psC",), ("rowB",), lambda s: s.copy(rowB[0:1, 0:512], psC[0:1, 0:512]))
        OP("S", ("psD",), ("rowC",), lambda s: s.copy(rowC[0:1, 0:512], psD[0:1, 0:512]))
        OP("T", ("rowB", "ones"), ("psC",),
           lambda t: t.matmul(psC[:, 0:512], onesrf[:], rowB[0:1, 0:512],
                              start=True, stop=True))
        OP("S", ("psC",), ("amaxb",), lambda s: s.copy(amaxb[:, 0:512], psC[:, 0:512]))
        OP("T", ("rowC", "ones"), ("psD",),
           lambda t: t.matmul(psD[:, 0:512], onesrf[:], rowC[0:1, 0:512],
                              start=True, stop=True))
        OP("S", ("psD",), ("amaxb",), lambda s: s.copy(amaxb[:, 512:1024], psD[:, 0:512]))
        for m in range(32):
            OP("V", ("hbuf", "amaxb"), ("sqb",),
               lambda v, m=m: v.tensor_tensor(out=sqb[:, 0:512],
                                              in0=hbuf[:, m * 512:(m + 1) * 512],
                                              in1=amaxb[:, 0:512], op=ALU.mult))
            OP("V", ("sqb",), ("sqb",),
               lambda v: v.tensor_scalar(out=sqb[:, 512:1024], in0=sqb[:, 0:512],
                                         scalar1=RMAGIC, scalar2=-RMAGIC,
                                         op0=ALU.add, op1=ALU.add))
            OP("V", ("sqb", "amaxb"), ("hbuf",),
               lambda v, m=m: v.tensor_tensor(out=hbuf[:, m * 512:(m + 1) * 512],
                                              in0=sqb[:, 512:1024], in1=amaxb[:, 512:1024], op=ALU.mult))
        for qm in range(4):
            OP("QS", ("HQW",), ("wbufA",),
               lambda en, qm=qm: en.dma_start(
                   out=ap3(wbufA, 0, [[8192, 128], [256, 32], [1, 256]]),
                   in_=ap3(d0q, qm * 256, [[D, 128], [128 * D, 32], [1, 256]])))
            for mh in range(2):
                m = qm * 2 + mh
                def dn_emit(t, mh=mh):
                    last = None
                    for k in range(32):
                        last = t.matmul(psC[:, 0:512],
                                        wbufA[:, k * 256 + mh * 128:k * 256 + (mh + 1) * 128],
                                        hbuf[:, k * 512:(k + 1) * 512],
                                        start=(k == 0), stop=(k == 31))
                    return last
                OP("T", ("hbuf", "wbufA"), ("psC",), dn_emit)
                OP("S", ("psC", "swsc"), ("arF1",),
                   lambda s: s.activation(out=arF1[:, 0:512], in_=psC[:, 0:512],
                                          func=AF.Identity, scale=swsc[:, 2:3]))
                OP("V", ("arF1", "arB3"), ("arF1",),
                   lambda v, m=m: v.tensor_tensor(out=arF1[:, 0:512], in0=arF1[:, 0:512],
                                                  in1=arB3[:, m * 512:(m + 1) * 512],
                                                  op=ALU.add))
                OP("V", ("arF1", "arF2"), ("arF2",),
                   lambda v, m=m: v.tensor_tensor(out=arF2[:, m * 512:(m + 1) * 512],
                                                  in0=arF1[:, 0:512],
                                                  in1=arF2[:, m * 512:(m + 1) * 512],
                                                  op=ALU.add))
        layernorm("arF2", arF2, None, None, "arB4", arB4)
        for widx, (wsrc, wkey) in enumerate(((wk1b, "l1k"), (wv1b, "l1v"), (wq1b, "l1q"))):
            OP("QS", ("HQW",), ("wbufB",),
               lambda en, wsrc=wsrc: en.dma_start(
                   out=ap3(wbufB, 0, [[8192, 128], [1024, 8], [1, 1024]]),
                   in_=ap3(wsrc, 0, [[D, 128], [128 * D, 8], [1, 1024]])))
            if wkey == "l1v":
                for i in range(4):
                    def v1_emit(t, i=i):
                        last = None
                        for c in range(8):
                            for nn in range(2):
                                last = t.matmul(
                                    psB[:, nn * 512:(nn + 1) * 512],
                                    arB4[:, c * 512 + i * 128:c * 512 + (i + 1) * 128],
                                    wbufB[:, c * 1024 + nn * 512:c * 1024 + (nn + 1) * 512],
                                    start=(c == 0), stop=(c == 7))
                        return last
                    OP("T", ("arB4", "wbufB"), ("psB",), v1_emit)
                    OP("S", ("psB",), ("arB1",),
                       lambda s, i=i: s.copy(arB1[:, i * 1024:(i + 1) * 1024], psB[:]))
                OP("QA", ("arB1",), ("v1h",),
                   lambda en, t0=t0: en.dma_start(
                       out=ap3(v1h, t0 * D, [[D, 128], [128 * D, 4], [1, D]]),
                       in_=ap3(arB1, 0, [[4096, 128], [1024, 4], [1, 1024]])))
            else:
                for m in range(8):
                    def kq_emit(t, m=m):
                        last = None
                        for c in range(8):
                            last = t.matmul(psC[:, 0:512],
                                            wbufB[:, c * 1024 + m * 128:c * 1024 + (m + 1) * 128],
                                            arB4[:, c * 512:(c + 1) * 512],
                                            start=(c == 0), stop=(c == 7))
                        return last
                    OP("T", ("arB4", "wbufB"), ("psC",), kq_emit)
                    if wkey == "l1q":
                        OP("S", ("psC",), ("arB1",),
                           lambda s, m=m: s.activation(
                               out=arB1[:, m * 512:(m + 1) * 512], in_=psC[:, 0:512],
                               func=AF.Identity, scale=1.0 / 32.0))
                    else:
                        OP("S", ("psC",), ("arB1",),
                           lambda s, m=m: s.copy(arB1[:, m * 512:(m + 1) * 512], psC[:, 0:512]))
                if wkey == "l1k":
                    OP("QA", ("arB1",), ("k1h",),
                       lambda en, t0=t0: en.dma_start(
                           out=ap3(k1h, t0, [[T, 128], [128 * T, 8], [1, 512]]),
                           in_=ap3(arB1, 0, [[4096, 128], [512, 8], [1, 512]])))
                else:
                    OP("V", ("arB1", "ssel"), ("arB3",),
                       lambda v: v.tensor_scalar(
                           out=ap3(arB3, 0, [[4096, 128], [256, 8], [128, 2], [1, 128]]),
                           in0=ap3(arB1, 0, [[4096, 128], [512, 8], [256, 2], [1, 128]]),
                           scalar1=ssel[:, 0:1], scalar2=None, op0=ALU.mult))
                    OP("V", ("arB1", "ssel", "arB3"), ("arB3",),
                       lambda v: v.scalar_tensor_tensor(
                           out=ap3(arB3, 0, [[4096, 128], [256, 8], [128, 2], [1, 128]]),
                           in0=ap3(arB1, 128, [[4096, 128], [512, 8], [256, 2], [1, 128]]),
                           scalar=ssel[:, 1:2],
                           in1=ap3(arB3, 0, [[4096, 128], [256, 8], [128, 2], [1, 128]]),
                           op0=ALU.mult, op1=ALU.add))
                    OP("QA", ("arB3",), ("qT1h",),
                       lambda en, t0=t0: en.dma_start(
                           out=ap3(qT1h, t0 // 2, [[TOWN, 128], [128 * TOWN, 8], [1, 256]]),
                           in_=ap3(arB3, 0, [[4096, 128], [256, 8], [1, 256]])))
        OP("V", ("arF2", "ssel"), ("arF1",),
           lambda v: v.tensor_scalar(
               out=ap3(arF1, 0, [[2048, 128], [256, 8], [128, 2], [1, 128]]),
               in0=ap3(arF2, 0, [[4096, 128], [512, 8], [256, 2], [1, 128]]),
               scalar1=ssel[:, 0:1], scalar2=None, op0=ALU.mult))
        OP("V", ("arF2", "ssel", "arF1"), ("arF1",),
           lambda v: v.scalar_tensor_tensor(
               out=ap3(arF1, 0, [[2048, 128], [256, 8], [128, 2], [1, 128]]),
               in0=ap3(arF2, 128, [[4096, 128], [512, 8], [256, 2], [1, 128]]),
               scalar=ssel[:, 1:2],
               in1=ap3(arF1, 0, [[2048, 128], [256, 8], [128, 2], [1, 128]]),
               op0=ALU.mult, op1=ALU.add))
        OP("V", ("arB4", "ssel"), ("arB2",),
           lambda v: v.tensor_scalar(
               out=ap3(arB2, 0, [[4096, 128], [256, 8], [128, 2], [1, 128]]),
               in0=ap3(arB4, 0, [[4096, 128], [512, 8], [256, 2], [1, 128]]),
               scalar1=ssel[:, 0:1], scalar2=None, op0=ALU.mult))
        OP("V", ("arB4", "ssel", "arB2"), ("arB2",),
           lambda v: v.scalar_tensor_tensor(
               out=ap3(arB2, 0, [[4096, 128], [256, 8], [128, 2], [1, 128]]),
               in0=ap3(arB4, 128, [[4096, 128], [512, 8], [256, 2], [1, 128]]),
               scalar=ssel[:, 1:2],
               in1=ap3(arB2, 0, [[4096, 128], [256, 8], [128, 2], [1, 128]]),
               op0=ALU.mult, op1=ALU.add))
        OP("QA", ("arF1",), ("x1h",),
           lambda en, t0=t0: en.dma_start(
               out=ap3(x1h, t0 // 2, [[TOWN, 128], [128 * TOWN, 8], [1, 256]]),
               in_=ap3(arF1, 0, [[2048, 128], [256, 8], [1, 256]])))
        OP("QA", ("arB2",), ("xn1h",),
           lambda en, t0=t0: en.dma_start(
               out=ap3(xn1h, t0 // 2, [[TOWN, 128], [128 * TOWN, 8], [1, 256]]),
               in_=ap3(arB2, 0, [[4096, 128], [256, 8], [1, 256]])))
    P.barrier()
    P.flush(block)

    P.mark("F")
    for it in range(4):
        o0 = it * TT
        OP("QS", ("xn1h",), ("arB2",),
           lambda en, o0=o0: en.dma_start(
               out=ap3(arB2, 0, [[4096, 128], [512, 8], [1, 512]]),
               in_=ap3(xn1h, o0, [[TOWN, 128], [128 * TOWN, 8], [1, 512]])))
        OP("QS", ("x1h",), ("arF2",),
           lambda en, o0=o0: en.dma_start(
               out=ap3(arF2, 0, [[4096, 128], [512, 8], [1, 512]]),
               in_=ap3(x1h, o0, [[TOWN, 128], [128 * TOWN, 8], [1, 512]])))
        for qm in range(4):
            OP("QS", ("HQW",), ("wbufA",),
               lambda en, qm=qm: en.dma_start(
                   out=ap3(wbufA, 0, [[8192, 128], [1024, 8], [1, 1024]]),
                   in_=ap3(w1b, qm * 1024, [[INNER, 128], [128 * INNER, 8], [1, 1024]])))
            OP("QS", ("HQW",), ("wbufB",),
               lambda en, qm=qm: en.dma_start(
                   out=ap3(wbufB, 0, [[8192, 128], [1024, 8], [1, 1024]]),
                   in_=ap3(w2b, qm * 1024, [[INNER, 128], [128 * INNER, 8], [1, 1024]])))
            for mh in range(8):
                m = qm * 8 + mh
                def f1_emit(t, mh=mh):
                    last = None
                    for c in range(8):
                        last = t.matmul(psC[:, 0:512],
                                        wbufA[:, c * 1024 + mh * 128:c * 1024 + (mh + 1) * 128],
                                        arB2[:, c * 512:(c + 1) * 512],
                                        start=(c == 0), stop=(c == 7))
                    return last
                OP("T", ("arB2", "wbufA"), ("psC",), f1_emit)
                def f2_emit(t, mh=mh):
                    last = None
                    for c in range(8):
                        last = t.matmul(psD[:, 0:512],
                                        wbufB[:, c * 1024 + mh * 128:c * 1024 + (mh + 1) * 128],
                                        arB2[:, c * 512:(c + 1) * 512],
                                        start=(c == 0), stop=(c == 7))
                    return last
                OP("T", ("arB2", "wbufB"), ("psD",), f2_emit)
                OP("S", ("psC", "s_s1"), ("arF1",),
                   lambda s, m=m: s.activation(out=arF1[:, 0:512], in_=psC[:, 0:512],
                                               func=AF.Silu, scale=s_s1[:, m:m + 1]))
                OP("S", ("psD", "s_s2"), ("arF1b",),
                   lambda s, m=m: s.activation(out=arF1[:, 512:1024], in_=psD[:, 0:512],
                                               func=AF.Identity, scale=s_s2[:, m:m + 1]))
                OP("V", ("arF1", "arF1b"), ("hbuf",),
                   lambda v, m=m: v.tensor_tensor(out=hbuf[:, m * 512:(m + 1) * 512],
                                                  in0=arF1[:, 0:512], in1=arF1[:, 512:1024],
                                                  op=ALU.mult))
        for hi, (hsrc, hscale, src_buf, dst_buf, srcb, dstb) in enumerate(
                ((h1b, s_sh1, "hbuf", "hbuf2", hbuf, hbuf2),
                 (h2b, s_sh2, "hbuf2", "hbuf", hbuf2, hbuf))):
            for g in range(16):
                wb, wbk = (wbufA, "wbufA") if g % 2 == 0 else (wbufB, "wbufB")
                OP("QS", ("HQW",), (wbk,),
                   lambda en, g=g, wb=wb, hsrc=hsrc: en.dma_start(
                       out=ap3(wb, 0, [[8192, 128], [256, 32], [1, 256]]),
                       in_=ap3(hsrc, g * 256, [[INNER, 128], [128 * INNER, 32], [1, 256]])))
                for mg in range(2):
                    m = g * 2 + mg
                    def hh_emit(t, mg=mg, wb=wb, srcb=srcb):
                        last = None
                        for k in range(32):
                            last = t.matmul(psC[:, 0:512],
                                            wb[:, k * 256 + mg * 128:k * 256 + (mg + 1) * 128],
                                            srcb[:, k * 512:(k + 1) * 512],
                                            start=(k == 0), stop=(k == 31))
                        return last
                    OP("T", (src_buf, wbk), ("psC",), hh_emit)
                    OP("S", ("psC", "sc"), (dst_buf,),
                       lambda s, m=m, hscale=hscale, dstb=dstb: s.activation(
                           out=dstb[:, m * 512:(m + 1) * 512], in_=psC[:, 0:512],
                           func=AF.Gelu, scale=hscale[:, m:m + 1]))
        for qm in range(4):
            OP("QS", ("HQW",), ("wbufA",),
               lambda en, qm=qm: en.dma_start(
                   out=ap3(wbufA, 0, [[8192, 128], [256, 32], [1, 256]]),
                   in_=ap3(w3b, qm * 256, [[D, 128], [128 * D, 32], [1, 256]])))
            for mh in range(2):
                m = qm * 2 + mh
                def w3_emit(t, mh=mh):
                    last = None
                    for k in range(32):
                        last = t.matmul(psC[:, 0:512],
                                        wbufA[:, k * 256 + mh * 128:k * 256 + (mh + 1) * 128],
                                        hbuf[:, k * 512:(k + 1) * 512],
                                        start=(k == 0), stop=(k == 31))
                    return last
                OP("T", ("hbuf", "wbufA"), ("psC",), w3_emit)
                OP("S", ("psC", "s_s3"), ("arF1",),
                   lambda s, m=m: s.activation(out=arF1[:, 0:512], in_=psC[:, 0:512],
                                               func=AF.Identity, scale=s_s3[:, m:m + 1]))
                OP("V", ("arF1", "arF2"), ("arF2",),
                   lambda v, m=m: v.tensor_tensor(out=arF2[:, m * 512:(m + 1) * 512],
                                                  in0=arF1[:, 0:512],
                                                  in1=arF2[:, m * 512:(m + 1) * 512],
                                                  op=ALU.add))
        OP("QA", ("arF2",), ("x1h",),
           lambda en, o0=o0: en.dma_start(
               out=ap3(x1h, o0, [[TOWN, 128], [128 * TOWN, 8], [1, 512]]),
               in_=ap3(arF2, 0, [[4096, 128], [512, 8], [1, 512]])))
    P.barrier()
    P.flush(block)

    P.mark("G")
    def nkb_of(j):
        return (j + 2) // 2

    for cc in range(8):
        OP("QG", ("HB",), ("wo1b",),
           lambda g, cc=cc: g.dma_start(out=kv0k[:, cc * 1024:(cc + 1) * 1024],
                                        in_=wo1[cc * 128:(cc + 1) * 128, :]))
    for g4 in range(4):
        o0 = g4 * TT
        OP("QS", ("qT1h",), ("arB2",),
           lambda en, o0=o0: en.dma_start(
               out=ap3(arB2, 0, [[4096, 128], [512, 8], [1, 512]]),
               in_=ap3(qT1h, o0, [[TOWN, 128], [128 * TOWN, 8], [1, 512]])))
        nkb_max = nkb_of(g4 * 4 + 3)
        for kb in range(nkb_max):
            kbuf, kbk = (kblk[0], "kblk0") if kb % 2 == 0 else (kblk[1], "kblk1")
            OP("QS", ("k1h",), (kbk,),
               lambda en, kb=kb, kbuf=kbuf: en.dma_start(
                   out=ap3(wbufA, kbuf.base, [[8192, 128], [512, 8], [1, 512]]),
                   in_=ap3(k1h, kb * 512, [[T, 128], [128 * T, 8], [1, 512]])))
            for j4 in range(4):
                j = g4 * 4 + j4
                if kb >= nkb_of(j):
                    continue
                ps, pk = (psC, "psC") if (kb * 4 + j4) % 2 == 0 else (psD, "psD")
                def ls_emit(t, j4=j4, kbuf=kbuf, ps=ps):
                    last = None
                    for c in range(8):
                        last = t.matmul(ps[:, 0:512],
                                        arB2[:, c * 512 + j4 * 128:c * 512 + (j4 + 1) * 128],
                                        kbuf[:, c * 512:(c + 1) * 512],
                                        start=(c == 0), stop=(c == 7))
                    return last
                OP("T", ("arB2", kbk), (pk,), ls_emit)
                OP("S", (pk,), (f"SgB{j4}",),
                   lambda s, kb=kb, j4=j4, ps=ps: s.copy(
                       Sg[j4][:, kb * 512:(kb + 1) * 512], ps[:, 0:512]))
        for j4 in range(4):
            j = g4 * 4 + j4
            Ec = nkb_of(j) * 512
            OP("QS", ("HB",), ("tb1",),
               lambda en, j=j: en.dma_start(
                   out=tb2[:, 0:512],
                   in_=ap3(m1c, j * 128 * 512, [[512, 128], [1, 512]])))
            OP("V", (f"SgB{j4}", "tb1"), (f"SgB{j4}",),
               lambda v, j4=j4, Ec=Ec: v.tensor_tensor(
                   out=Sg[j4][:, Ec - 512:Ec], in0=Sg[j4][:, Ec - 512:Ec],
                   in1=tb2[:, 0:512], op=ALU.add))
            OP("V", (f"SgB{j4}",), ("mrow",),
               lambda v, j4=j4, Ec=Ec: v.reduce_max(out=mrow[:, j4:j4 + 1],
                                                    in_=Sg[j4][:, 0:Ec], axis=AX.X))
            OP("V", ("mrow",), ("nrow",),
               lambda v, j4=j4: v.tensor_scalar(out=nrow[:, j4:j4 + 1],
                                                in0=mrow[:, j4:j4 + 1],
                                                scalar1=-1.0, scalar2=None, op0=ALU.mult))
            OP("S", (f"SgB{j4}", "nrow"), ("arB1", "lrow"),
               lambda s, j4=j4, Ec=Ec: s.activation(out=arB1[:, 0:Ec],
                                                    in_=Sg[j4][:, 0:Ec], func=AF.Exp,
                                                    bias=nrow[:, j4:j4 + 1],
                                                    accum_out=lrow[:, j4:j4 + 1]))
            OP("V", ("lrow",), ("rlrow",),
               lambda v, j4=j4: v.reciprocal(out=rlrow[:, j4:j4 + 1],
                                             in_=lrow[:, j4:j4 + 1]))
            nkc = Ec // 128
            for gb in range(0, nkc, 8):
                nn = min(8, nkc - gb)
                def pt_emit(t, gb=gb, nn=nn):
                    last = None
                    for k in range(nn):
                        last = t.transpose(psT[:, k * 128:(k + 1) * 128],
                                           arB1[:, (gb + k) * 128:(gb + k + 1) * 128],
                                           ident[:])
                    return last
                OP("T", ("arB1", "ident"), ("psT",), pt_emit)
                OP("S", ("psT",), (f"pTL{j4}",),
                   lambda s, j4=j4, gb=gb, nn=nn: s.copy(
                       pTL[j4][:, gb * 128:(gb + nn) * 128], psT[:, 0:nn * 128]))
        for pair in range(2):
            jlo = g4 * 4 + pair * 2
            jhi = jlo + 1
            nkbp = nkb_of(jhi)
            for kb in range(nkbp):
                vb, vbk = (vblk[0], "vblk0") if kb % 2 == 0 else (vblk[1], "vblk1")
                OP("QS", ("v1h",), (vbk,),
                   lambda en, kb=kb, vb=vb: en.dma_start(
                       out=ap3(wbufB, vb.base, [[8192, 128], [1024, 4], [1, 1024]]),
                       in_=ap3(v1h, kb * 512 * D, [[D, 128], [128 * D, 4], [1, D]])))
                for jj, psv, pvk in ((jlo, psA, "psA"), (jhi, psB, "psB")):
                    j4 = jj - g4 * 4
                    nkc_j = nkb_of(jj) * 4
                    if kb * 4 >= nkc_j:
                        continue
                    def av_emit(t, kb=kb, j4=j4, vb=vb, psv=psv, nkc_j=nkc_j):
                        last = None
                        for kk in range(4):
                            kc = kb * 4 + kk
                            if kc >= nkc_j:
                                break
                            for nn in range(2):
                                last = t.matmul(psv[:, nn * 512:(nn + 1) * 512],
                                                pTL[j4][:, kc * 128:(kc + 1) * 128],
                                                vb[:, kk * 1024 + nn * 512:kk * 1024 + (nn + 1) * 512],
                                                start=(kc == 0), stop=(kc == nkc_j - 1))
                        return last
                    OP("T", (f"pTL{j4}", vbk), (pvk,), av_emit)
            for jj, psv, pvk in ((jlo, psA, "psA"), (jhi, psB, "psB")):
                j4 = jj - g4 * 4
                OP("S", (pvk, "rlrow"), ("arB3",),
                   lambda s, j4=j4, psv=psv: s.activation(out=arB3[:, 0:1024], in_=psv[:],
                                                          func=AF.Identity,
                                                          scale=rlrow[:, j4:j4 + 1]))
                def ot_emit(t):
                    last = None
                    for dd in range(8):
                        last = t.transpose(psT[:, dd * 128:(dd + 1) * 128],
                                           arB3[:, dd * 128:(dd + 1) * 128], ident[:])
                    return last
                OP("T", ("arB3", "ident"), ("psT",), ot_emit)
                OP("S", ("psT",), ("arB4",),
                   lambda s, j4=j4: s.copy(
                       ap3(arB4, j4 * 128, [[4096, 128], [512, 8], [1, 128]]),
                       ap3(psT, 0, [[1024, 128], [128, 8], [1, 128]])))
        OP("QS", ("x1h",), ("arF2",),
           lambda en, o0=o0: en.dma_start(
               out=ap3(arF2, 0, [[4096, 128], [512, 8], [1, 512]]),
               in_=ap3(x1h, o0, [[TOWN, 128], [128 * TOWN, 8], [1, 512]])))
        for m in range(8):
            def wo1_emit(t, m=m):
                last = None
                for c in range(8):
                    last = t.matmul(psC[:, 0:512],
                                    kv0k[:, c * 1024 + m * 128:c * 1024 + (m + 1) * 128],
                                    arB4[:, c * 512:(c + 1) * 512],
                                    start=(c == 0), stop=(c == 7))
                return last
            OP("T", ("arB4", "wo1b"), ("psC",), wo1_emit)
            OP("S", ("psC",), ("arF1",),
               lambda s: s.copy(arF1[:, 0:512], psC[:, 0:512]))
            OP("V", ("arF1", "arF2"), ("arF2",),
               lambda v, m=m: v.tensor_tensor(out=arF2[:, m * 512:(m + 1) * 512],
                                              in0=arF1[:, 0:512],
                                              in1=arF2[:, m * 512:(m + 1) * 512],
                                              op=ALU.add))
        for tc in range(4):
            ps, pk = (psA, "psA") if tc % 2 == 0 else (psB, "psB")
            def ft_emit(t, tc=tc, ps=ps):
                last = None
                for c in range(8):
                    last = t.transpose(ps[:, c * 128:(c + 1) * 128],
                                       arF2[:, c * 512 + tc * 128:c * 512 + (tc + 1) * 128],
                                       idf[:])
                return last
            OP("T", ("arF2", "idf"), (pk,), ft_emit)
            OP("S", (pk,), ("otb",),
               lambda s, ps=ps: s.copy(sqb[:], ps[:]))
            OP("QS", ("otb",), ("yout",),
               lambda en, o0=o0, tc=tc: en.dma_start(
                   out=y_own[o0 + tc * 128:o0 + (tc + 1) * 128, :],
                   in_=sqb[:]))
    P.barrier()
    P.final_emit(block)

    es.close()
    nc.finalize()
    return nc


_NC_CACHE = []


def _host_prep(inputs):
    f32 = np.float32
    bf16 = ml_dtypes.bfloat16

    def c(a, dt=f32):
        return np.ascontiguousarray(np.asarray(a, dtype=dt))

    x = c(inputs["L0_attn_x"])
    HP = [0, 4, 1, 5, 2, 6, 3, 7, 8, 12, 9, 13, 10, 14, 11, 15]
    hperm = np.concatenate([np.arange(h * 64, (h + 1) * 64) for h in HP])
    wq0 = c(np.asarray(inputs["L0_attn_Wq"])[:, hperm])
    wk0 = c(inputs["L0_attn_Wk"])
    wv0 = c(inputs["L0_attn_Wv"])
    wo0 = c(np.asarray(inputs["L0_attn_Wo"])[hperm, :])
    bq0 = c(np.asarray(inputs["L0_attn_bq"])[hperm].reshape(8, 128).T)
    bk0 = c(np.asarray(inputs["L0_attn_bk"]).reshape(2, 128).T)
    bv0c = c(np.asarray(inputs["L0_attn_bv"]).reshape(1, 256))
    bo0 = c(np.asarray(inputs["L0_attn_bo"]).reshape(8, 128).T)
    hw0 = c(np.asarray(inputs["L0_attn_head_weights"])[HP].reshape(1, 16))
    g0 = c(np.asarray(inputs["L0_ffn_w_gate"]).T)
    u0 = c(np.asarray(inputs["L0_ffn_w_up"]).T)
    d0 = c(np.asarray(inputs["L0_ffn_w_down"]).T)
    wq1 = c(np.asarray(inputs["L1_attn_Wq"]).T)
    wk1 = c(np.asarray(inputs["L1_attn_Wk"]).T)
    wv1 = c(np.asarray(inputs["L1_attn_Wv"]).T)
    wo1 = c(np.asarray(inputs["L1_attn_Wo"]).T)
    w1 = c(np.asarray(inputs["L1_ffn_w1_w"]).T)
    w2 = c(np.asarray(inputs["L1_ffn_w2_w"]).T)
    h1 = c(np.asarray(inputs["L1_ffn_h1_w"]).T)
    h2 = c(np.asarray(inputs["L1_ffn_h2_w"]).T)
    w3 = c(np.asarray(inputs["L1_ffn_w3_w"]).T)
    s1c = c(np.asarray(inputs["L1_ffn_w1_s"]).reshape(32, 128).T)
    s2c = c(np.asarray(inputs["L1_ffn_w2_s"]).reshape(32, 128).T)
    sh1c = c(np.asarray(inputs["L1_ffn_h1_s"]).reshape(32, 128).T)
    sh2c = c(np.asarray(inputs["L1_ffn_h2_s"]).reshape(32, 128).T)
    s3c = c(np.asarray(inputs["L1_ffn_w3_s"]).reshape(8, 128).T)

    identc = np.eye(128, dtype=bf16)
    identfc = np.eye(128, dtype=f32)
    nb = HD // 4
    inv_freq = 1.0 / (10000.0 ** (np.arange(nb, dtype=np.float64) / nb))
    tpos = np.arange(T, dtype=np.float64)
    base_cos = np.zeros((64, T), dtype=np.float64)
    base_sin = np.zeros((64, T), dtype=np.float64)
    for dd in range(64):
        ff = (dd % 32) // 2
        ang = tpos * inv_freq[ff]
        base_cos[dd] = np.cos(ang)
        base_sin[dd] = np.sin(ang) * (-1.0 if dd < 32 else 1.0)
    rope_cosc = np.concatenate([base_cos, base_cos], 0).astype(bf16)
    rope_sinc = np.concatenate([base_sin, base_sin], 0).astype(bf16)
    rot2c = np.zeros((128, 128), dtype=bf16)
    for blk in range(2):
        for dd in range(64):
            sdd = blk * 64 + (dd + 32 if dd < 32 else dd - 32)
            rot2c[sdd, blk * 64 + dd] = 1
    m0 = np.zeros((64, 3 * 2048), dtype=f32)
    m0gg = np.zeros((64, 3 * 512), dtype=f32)
    for var, sg in ((0, 0), (1, 1), (2, 63)):
        blockm = np.zeros((64, 128), dtype=f32)
        blockg = np.zeros((64, 32), dtype=f32)
        for q in range(64):
            ii = 64 * sg + q
            for k in range(128):
                jj = 64 * sg - 32 + k
                ok = (0 <= jj < T) and (abs(ii - jj) <= 12 or jj < 8)
                blockm[q, k] = 1.0 if ok else 0.0
            for kk in range(8):
                inwin = (64 * sg - 32 <= kk < 64 * sg + 96)
                blockg[q, kk] = 0.0 if inwin else 1.0
        m0[:, var * 2048:(var + 1) * 2048] = np.tile(blockm, (1, 16))
        m0gg[:, var * 512:(var + 1) * 512] = np.tile(blockg, (1, 16))
    m0c = m0.astype(bf16)
    m0gc = m0gg.astype(bf16)

    y = np.empty((B, T, D), dtype=f32)
    in_maps = []
    shared = dict(wq0=wq0, bq0=bq0, wk0=wk0, bk0=bk0, wv0=wv0, bv0c=bv0c,
                  wo0=wo0, bo0=bo0, hw0=hw0, g0=g0, u0=u0, d0=d0,
                  wq1=wq1, wk1=wk1, wv1=wv1, wo1=wo1,
                  w1=w1, s1c=s1c, w2=w2, s2c=s2c, h1=h1, sh1c=sh1c,
                  h2=h2, sh2c=sh2c, w3=w3, s3c=s3c,
                  identc=identc, identfc=identfc, rope_cosc=rope_cosc,
                  rope_sinc=rope_sinc, rot2c=rot2c, m0c=m0c, m0g=m0gc)
    for core in range(8):
        b = core // 2
        cpar = core % 2
        m1 = np.zeros((NQT * 128, 512), dtype=f32)
        for j in range(NQT):
            Ec = 512 * ((j + 2) // 2)
            for q in range(128):
                ig = 128 * (2 * j + cpar) + q
                keys = Ec - 512 + np.arange(512)
                m1[j * 128 + q] = np.where(keys <= ig, 0.0, -1e9)
        m1c = m1.astype(bf16)
        im = dict(shared)
        im["x_in"] = np.ascontiguousarray(x[b])
        im["m1c"] = m1c
        im["parc"] = np.array([[cpar]], dtype=np.int32)
        sel = np.zeros((128, 2), dtype=f32)
        sel[:, 0] = 1.0 - cpar
        sel[:, 1] = float(cpar)
        im["selc"] = sel
        in_maps.append(im)
    return in_maps, y


_LAST_RES = []


def kernel(**inputs):
    if not _NC_CACHE:
        _NC_CACHE.append(build_nc())
    nc = _NC_CACHE[0]
    in_maps, y = _host_prep(inputs)
    res = run_bass_kernel_spmd(nc, in_maps, core_ids=list(range(8)))
    _LAST_RES.clear()
    _LAST_RES.append(res)
    for core in range(8):
        b = core // 2
        cpar = core % 2
        yo = np.asarray(res.results[core]["y_own"]).reshape(16, 128, D)
        y[b].reshape(32, 128, D)[cpar::2] = yo
    return y


if __name__ == "__main__":
    nc = build_nc()
    print("built ok")


# version 03
# speedup vs baseline: 1.0856x; 1.0856x over previous
import numpy as np
import ml_dtypes
import concourse.bass as bass
import concourse.bacc as bacc
import concourse.mybir as mybir
from concourse.bass_utils import run_bass_kernel_spmd
from concourse.bass_types import AP as BAP
from contextlib import ExitStack

F32 = mybir.dt.float32
BF16 = mybir.dt.bfloat16
I32 = mybir.dt.int32
AF = mybir.ActivationFunctionType
ALU = mybir.AluOpType
AX = mybir.AxisListType

B, T, D = 4, 4096, 1024
H, HKV, HD = 16, 4, 64
INNER = 4096
TOWN = 2048
NT = 8
TT = 512
NQT = 16
PAD = 32
WTP = T + 2 * PAD
EPS = 1e-5
LN127 = float(np.log(127.0))
RMAGIC = 12582912.0

_TAPS = set()

QKEYS = ("QS", "QA", "QG")
EKEYS = ("T", "V", "S", "G") + QKEYS
STREAM = {"T": "tensor", "V": "vector", "S": "scalar", "G": "gpsimd",
          "QS": "sync", "QA": "scalar", "QG": "gpsimd"}


class Prog:
    LIMIT = None
    MARKS = {}

    def __init__(self, sems):
        self.sems = sems
        self.nops = 0
        self.count = {k: 0 for k in EKEYS}
        self.hwm = {}
        self.last_w = {}
        self.readers = {}
        self.pending = {s: [] for s in ("tensor", "vector", "scalar", "gpsimd", "sync")}
        self.qbounds = {}

    def mark(self, name):
        Prog.MARKS[name] = self.nops

    def op(self, eng, a, b=(), c=(), tag=""):
        self.nops += 1
        if Prog.LIMIT is not None and self.nops > Prog.LIMIT:
            return 0
        if callable(a):
            emit, reads, writes = a, b, c
        else:
            reads, writes, emit = a, b, c
        deps = []
        for b in reads:
            if b in self.last_w:
                deps.append(self.last_w[b])
        for b in writes:
            if b in self.last_w:
                deps.append(self.last_w[b])
            for e2, c2 in self.readers.get(b, {}).items():
                deps.append((e2, c2))
        waits = {}
        for (e2, c2) in deps:
            if c2 <= 0:
                continue
            if e2 in QKEYS:
                if eng == e2:
                    continue
                c2 = self.count[e2]
            if self.hwm.get((eng, e2), 0) < c2:
                waits[e2] = max(waits.get(e2, 0), c2)
        for e2, c2 in waits.items():
            self.hwm[(eng, e2)] = c2
            if e2 in QKEYS:
                self.qbounds.setdefault(e2, set()).add(c2)
        inc = 16 if eng in QKEYS else 1
        self.count[eng] += inc
        cnt = self.count[eng]
        self.pending[STREAM[eng]].append((waits, emit, self.sems[eng], inc, tag))
        for b in reads:
            self.readers.setdefault(b, {})[eng] = cnt
        for b in writes:
            self.last_w[b] = (eng, cnt)
            self.readers[b] = {}
        return cnt

    def barrier(self):
        snap = dict(self.count)
        for eng in ("T", "V", "S", "G", "QS"):
            waits = {}
            for e2, c2 in snap.items():
                if e2 != eng and c2 > 0 and self.hwm.get((eng, e2), 0) < c2:
                    waits[e2] = c2
            for e2, c2 in waits.items():
                self.hwm[(eng, e2)] = c2
                if e2 in QKEYS:
                    self.qbounds.setdefault(e2, set()).add(c2)
            if waits:
                self.pending[STREAM[eng]].append((waits, None, None, 0, "bar"))

    def flush(self, block):
        pass

    def final_emit(self, block):
        qsems = {"QS": "sync", "QA": "scalar", "QG": "gpsimd"}
        for qk, stream in qsems.items():
            need = self.qbounds.get(qk, set())
            if not need:
                continue
            sem = self.sems[qk]
            out = []
            cum = 0
            for entry in self.pending[stream]:
                out.append(entry)
                if entry[2] is sem:
                    cum += entry[3]
                    if cum in need:
                        out.append(({qk: cum}, None, None, 0, "qb"))
            self.pending[stream] = out
        for stream in ("sync", "gpsimd", "scalar", "vector", "tensor"):
            lst = self.pending[stream]
            if not lst:
                continue
            snapshot = list(lst)
            lst.clear()
            sems = self.sems

            def run(en, snapshot=snapshot, sems=sems):
                for waits, emit, sem, inc, tag in snapshot:
                    for e2, c2 in waits.items():
                        en.wait_ge(sems[e2], c2)
                    if emit is not None:
                        emit(en).then_inc(sem, inc)
            getattr(block, stream)(run)


def ap3(t, offset, dims):
    h = t.tensor if hasattr(t, "tensor") else t
    return BAP(tensor=h, offset=offset, ap=[list(d) for d in dims])


def build_nc():
    nc = bacc.Bacc("TRN2", target_bir_lowering=False)

    def din(name, shape, dt=F32):
        return nc.dram_tensor(name, shape, dt, kind="ExternalInput").ap()

    def hbm(name, shape, dt):
        kind = "ExternalOutput" if name in _TAPS else "Internal"
        return nc.dram_tensor(name, shape, dt, kind=kind).ap()

    x_in = din("x_in", [T, D])
    wq0 = din("wq0", [D, D]); bq0 = din("bq0", [128, 8])
    wk0 = din("wk0", [D, 256]); bk0 = din("bk0", [128, 2])
    wv0 = din("wv0", [D, 256]); bv0c = din("bv0c", [1, 256])
    wo0 = din("wo0", [D, D]); bo0 = din("bo0", [128, 8])
    hw0 = din("hw0", [1, 16])
    g0 = din("g0", [D, INNER]); u0 = din("u0", [D, INNER]); d0 = din("d0", [INNER, D])
    wq1 = din("wq1", [D, D]); wk1 = din("wk1", [D, D])
    wv1 = din("wv1", [D, D]); wo1 = din("wo1", [D, D])
    w1 = din("w1", [D, INNER]); s1c = din("s1c", [128, 32])
    w2 = din("w2", [D, INNER]); s2c = din("s2c", [128, 32])
    h1 = din("h1", [INNER, INNER]); sh1c = din("sh1c", [128, 32])
    h2 = din("h2", [INNER, INNER]); sh2c = din("sh2c", [128, 32])
    w3 = din("w3", [INNER, D]); s3c = din("s3c", [128, 8])
    identc = din("identc", [128, 128], BF16)
    identfc = din("identfc", [128, 128], F32)
    rope_cosc = din("rope_cosc", [128, T], BF16)
    rope_sinc = din("rope_sinc", [128, T], BF16)
    rot2c = din("rot2c", [128, 128], BF16)
    m0c = din("m0c", [64, 3 * 2048], BF16)
    m0g = din("m0g", [64, 3 * 512], BF16)
    m1c = din("m1c", [NQT * 128, 512], BF16)
    parc = din("parc", [1, 1], mybir.dt.int32)
    selc = din("selc", [128, 2])

    y_own = nc.dram_tensor("y_own", [TOWN, D], F32, kind="ExternalOutput").ap()

    g0q = hbm("g0q", [D, INNER], BF16)
    u0q = hbm("u0q", [D, INNER], BF16)
    d0q = hbm("d0q", [INNER, D], BF16)
    qT0h = hbm("qT0h", [8 * 128, T], BF16)
    xqh = hbm("xqh", [8 * 128, T], BF16)
    aoth = hbm("aoth", [8 * 128, T], BF16)
    xTh = hbm("xTh", [8 * 128, T], F32)
    x1h = hbm("x1h", [8 * 128, TOWN], F32)
    xn1h = hbm("xn1h", [8 * 128, TOWN], BF16)
    qT1h = hbm("qT1h", [8 * 128, TOWN], BF16)
    k1h = hbm("k1h", [8 * 128, T], BF16)
    v1h = hbm("v1h", [T, D], BF16)

    es = ExitStack()
    e = es.enter_context

    def sbf(name, shape, dt=F32):
        return e(nc.sbuf_tensor(name, shape, dt))

    kv0k = sbf("kv0k", [128, 2 * WTP], BF16)
    v0t = sbf("v0t", [128, 32 * 256], BF16)
    ident = sbf("ident", [128, 128], BF16)
    idf = sbf("idf", [128, 128], F32)
    srot2 = sbf("srot2", [128, 128], BF16)
    sbq0 = sbf("sbq0", [128, 8]); sbk0 = sbf("sbk0", [128, 2]); sbo0 = sbf("sbo0", [128, 8])
    bvb = sbf("bvb", [1, 256], BF16)
    shw = sbf("shw", [1, 16]); shw8 = sbf("shw8", [128, 8]); sbqhw = sbf("sbqhw", [128, 8])
    hwst = sbf("hwst", [32, 16])
    s_s1 = sbf("s_s1", [128, 32]); s_s2 = sbf("s_s2", [128, 32])
    s_sh1 = sbf("s_sh1", [128, 32]); s_sh2 = sbf("s_sh2", [128, 32])
    s_s3 = sbf("s_s3", [128, 8])
    ones1f = sbf("ones1f", [128, 1], F32)
    onesrf = sbf("onesrf", [1, 128], F32)
    onesrb = sbf("onesrb", [1, 128], BF16)
    swsc = sbf("swsc", [128, 4])
    swrec = sbf("swrec", [128, 4])
    sCneg = sbf("sCneg", [64, 1])
    pmask = sbf("pmask", [128, 2])
    rowA = sbf("rowA", [1, 640]); rowB = sbf("rowB", [1, 640]); rowC = sbf("rowC", [1, 640])
    wbufA = sbf("wbufA", [128, 8192], BF16)
    wbufB = sbf("wbufB", [128, 8192], BF16)
    arF1 = sbf("arF1", [128, 2048], F32)
    arF2 = sbf("arF2", [128, 4096], F32)
    sqb = sbf("sqb", [128, 1024], F32)
    amaxb = sbf("amaxb", [128, 1024], F32)
    mb_s = amaxb[:, 0:512]
    rb_s = amaxb[:, 512:1024]
    sxb_s = amaxb[:, 0:512]
    rxb_s = amaxb[:, 512:1024]
    arB1 = sbf("arB1", [128, 4096], BF16)
    arB2 = sbf("arB2", [128, 4096], BF16)
    arB3 = sbf("arB3", [128, 4096], BF16)
    arB4 = sbf("arB4", [128, 4096], BF16)
    hbuf = sbf("hbuf", [128, 16384], BF16)
    hbuf2 = sbf("hbuf2", [128, 16384], BF16)
    i32b = sbf("i32b", [128, 512], I32)
    tb1 = sbf("tb1", [128, 512], BF16)
    tb2 = tb1
    l0_s = sbf("l0_s", [64, 16]); r0_s = sbf("r0_s", [64, 16])
    cl127 = sbf("cl127", [128, 1])
    gq_s = sbf("gq_s", [128, 64], BF16)
    class _Slc:
        def __init__(self, parent, base):
            self.parent = parent
            self.base = base

        def __getitem__(self, idx):
            ps, fs = idx
            assert ps == slice(None)
            return self.parent[:, self.base + fs.start:self.base + fs.stop]

    Sg = [_Slc(hbuf, i * 4096) for i in range(4)]
    pTL = [_Slc(hbuf2, i * 4096) for i in range(4)]
    kblk = [_Slc(wbufA, i * 4096) for i in range(2)]
    vblk = [_Slc(wbufB, i * 4096) for i in range(2)]
    mrow = sbf("mrow", [128, 4]); nrow = sbf("nrow", [128, 4])
    lrow = sbf("lrow", [128, 4]); rlrow = sbf("rlrow", [128, 4])
    wo1_s = kv0k[:, 0:8192]
    winD = _Slc(wbufA, 0)
    p0c = _Slc(wbufA, 2208)
    pT0c = _Slc(wbufA, 4256)
    vwinc = _Slc(wbufA, 5280)
    rcos = hbuf2[:, 0:4096]
    rsin = hbuf2[:, 4096:8192]
    sm0 = hbuf2[0:64, 8192:14336]
    sm0g = hbuf2[0:64, 14336:15872]

    psA = e(nc.psum_tensor("psA", [128, 1024], F32))
    psB = e(nc.psum_tensor("psB", [128, 1024], F32))
    psC = e(nc.psum_tensor("psC", [128, 512], F32))
    psD = e(nc.psum_tensor("psD", [128, 512], F32))
    psE = e(nc.psum_tensor("psE", [128, 512], F32))
    psT = e(nc.psum_tensor("psT", [128, 1024], BF16))

    sems = {k: e(nc.semaphore(f"sem_{k}")) for k in EKEYS}
    block = e(nc.Block())
    P = Prog(sems)
    OP = P.op
    ssel = sbf("ssel", [128, 2])
    OP("QS", ("HB",), ("ssel",), lambda en: en.dma_start(out=ssel[:], in_=selc))

    def ld(buf, dst, src, q="QS"):
        OP(q, lambda en, d=dst, s=src: en.dma_start(out=d, in_=s), ("HB",), (buf,))

    OP("V", (), ("kv0k",), lambda v: v.memset(kv0k[:], 0.0))
    OP("V", (), ("v0t",), lambda v: v.memset(v0t[:], 0.0))
    for _nm, _t in (("arF1", arF1), ("arF2", arF2), ("sqb", sqb), ("amaxb", amaxb),
                    ("arB1", arB1), ("arB2", arB2), ("arB3", arB3), ("arB4", arB4),
                    ("i32b", i32b), ("tb1", tb1), ("rowA", rowA), ("rowB", rowB),
                    ("rowC", rowC), ("l0_s", l0_s), ("r0_s", r0_s), ("gq_s", gq_s),
                    ("mrow", mrow), ("nrow", nrow), ("lrow", lrow), ("rlrow", rlrow),
                    ("swsc", swsc), ("swrec", swrec), ("sCneg", sCneg), ("shw8", shw8)):
        OP("V", (), (_nm,), lambda v, _t=_t: v.memset(_t[:], 0.0))
    for _nm, _t in (("wbufA", wbufA), ("wbufB", wbufB), ("hbuf", hbuf), ("hbuf2", hbuf2)):
        OP("G", (), (_nm, "wq0b", "wk0b", "wv0b", "wo0b", "hbuf2w", "hbuf2x", "p0w", "p0g",
                     "vov", "pT0w", "pT0g", "vg4", "zebE", "zebO",
                     "SgB0", "SgB1", "SgB2", "SgB3",
                     "pTL0", "pTL1", "pTL2", "pTL3", "kblk0", "kblk1", "vblk0", "vblk1",
                     "rcos", "rsin", "sm0", "sm0g", "wo1b"),
           lambda g, _t=_t: g.memset(_t[:], 0.0))

    ld("ident", ident[:], identc); ld("idf", idf[:], identfc)
    ld("rcos", rcos, rope_cosc); ld("rsin", rsin, rope_sinc)
    ld("srot2", srot2[:], rot2c); ld("sm0", sm0, m0c)
    ld("sm0g", sm0g, m0g)
    ld("sbq0", sbq0[:], bq0); ld("sbk0", sbk0[:], bk0); ld("sbo0", sbo0[:], bo0)
    ld("rowB", rowB[0:1, 0:256], bv0c); ld("shw", shw[:], hw0)
    ld("s_s1", s_s1[:], s1c); ld("s_s2", s_s2[:], s2c)
    ld("s_sh1", s_sh1[:], sh1c); ld("s_sh2", s_sh2[:], sh2c); ld("s_s3", s_s3[:], s3c)

    OP("V", (), ("ones",), lambda v: v.memset(ones1f[:], 1.0))
    OP("V", (), ("ones",), lambda v: v.memset(onesrf[:], 1.0))
    OP("V", (), ("ones",), lambda v: v.memset(onesrb[:], 1.0))
    OP("V", (), ("cl127",), lambda v: v.memset(cl127[:], LN127))
    OP("V", (), ("pmask",), lambda v: v.memset(pmask[:], 0.0))
    OP("V", ("pmask",), ("pmask",), lambda v: v.memset(pmask[0:64, 0:1], 1.0))
    OP("V", ("pmask",), ("pmask",), lambda v: v.memset(pmask[64:96, 1:2], 1.0))
    OP("V", ("pmask",), ("pmask",), lambda v: v.memset(pmask[96:128, 1:2], 1.0))
    OP("V", ("rowB",), ("bvb",), lambda v: v.tensor_copy(bvb[:], rowB[0:1, 0:256]))
    for hh in range(16):
        OP("G", ("shw",), ("hwst",),
           lambda g, hh=hh: g.partition_broadcast(
               hwst[0:32, hh:hh + 1], shw[0:1, hh:hh + 1], channels=32))
    for rg in range(4):
        par = rg // 2
        OP("V", ("hwst",), ("shw8",),
           lambda v, rg=rg, par=par: v.tensor_copy(
               shw8[rg * 32:(rg + 1) * 32, 0:8],
               ap3(hwst, par, [[16, 32], [2, 8]])))
    OP("V", ("shw8",), ("shw8",),
       lambda v: v.tensor_scalar(out=shw8[:], in0=shw8[:], scalar1=0.125,
                                 scalar2=None, op0=ALU.mult))
    OP("V", ("shw8", "sbq0"), ("sbqhw",),
       lambda v: v.tensor_tensor(out=sbqhw[:], in0=sbq0[:], in1=shw8[:], op=ALU.mult))
    for cc in range(8):
        OP("QG", ("HB",), ("wq0b",),
           lambda g, cc=cc: g.dma_start(out=wbufA[:, cc * 1024:(cc + 1) * 1024],
                                        in_=wq0[cc * 128:(cc + 1) * 128, :]))
        OP("QG", ("HB",), ("wk0b",),
           lambda g, cc=cc: g.dma_start(out=hbuf[:, cc * 256:(cc + 1) * 256],
                                        in_=wk0[cc * 128:(cc + 1) * 128, :]))
        OP("QG", ("HB",), ("wv0b",),
           lambda g, cc=cc: g.dma_start(out=hbuf[:, 2048 + cc * 256:2048 + (cc + 1) * 256],
                                        in_=wv0[cc * 128:(cc + 1) * 128, :]))
        OP("QG", ("HB",), ("wo0b",),
           lambda g, cc=cc: g.dma_start(out=hbuf[:, 4096 + cc * 1024:4096 + (cc + 1) * 1024],
                                        in_=wo0[cc * 128:(cc + 1) * 128, :]))

    P.mark("W")
    def wq_stats(src, rows, cols, ci):
        nch = rows // 128
        npk = (cols + 2047) // 2048
        for k in range(nch):
            for hh2 in range(0, cols, 2048):
                nn = min(2048, cols - hh2)
                OP("QS", ("HB",), ("arF1",),
                   lambda en, k=k, src=src, hh2=hh2, nn=nn: en.dma_start(
                       out=arF1[:, 0:nn], in_=src[k * 128:(k + 1) * 128, hh2:hh2 + nn]))
                OP("V", ("arF1",), ("arF2",),
                   lambda v, k=k, hh2=hh2, nn=nn, npk=npk: v.reduce_sum(
                       out=arF2[:, npk * k + hh2 // 2048:npk * k + hh2 // 2048 + 1],
                       in_=arF1[:, 0:nn], axis=AX.X, apply_absolute_value=True))
        OP("V", ("arF2",), ("arF2",),
           lambda v, n=nch * npk: v.reduce_sum(out=arF2[:, 80:81], in_=arF2[:, 0:n],
                                               axis=AX.X))
        OP("T", ("arF2", "ones"), ("psE",),
           lambda t: t.matmul(psE[0:1, 0:1], ones1f[:], arF2[:, 80:81],
                              start=True, stop=True))
        OP("S", ("psE",), ("swsc",),
           lambda s, ci=ci, n=rows * cols: s.activation(
               out=swsc[0:1, ci:ci + 1], in_=psE[0:1, 0:1], func=AF.Identity, scale=1.0 / n))
        OP("V", ("swsc",), ("swsc",),
           lambda v, ci=ci: v.tensor_scalar(out=swsc[0:1, ci:ci + 1],
                                            in0=swsc[0:1, ci:ci + 1],
                                            scalar1=1e-5, scalar2=None, op0=ALU.max))
        OP("G", ("swsc",), ("swsc",),
           lambda g, ci=ci: g.partition_broadcast(swsc[:, ci:ci + 1],
                                                  swsc[0:1, ci:ci + 1], channels=128))

    wq_stats(g0, D, INNER, 0)
    wq_stats(u0, D, INNER, 1)
    wq_stats(d0, INNER, D, 2)
    OP("V", ("swsc",), ("swrec",), lambda v: v.reciprocal(out=swrec[:, 0:3], in_=swsc[:, 0:3]))

    def wq_quant(src, dst, rows, cols, ci):
        nch = rows // 128
        for k in range(nch):
            for hh2 in range(0, cols, 2048):
                nnh = min(2048, cols - hh2)
                OP("QS", ("HB",), ("arF1",),
                   lambda en, k=k, src=src, hh2=hh2, nnh=nnh: en.dma_start(
                       out=arF1[:, 0:nnh], in_=src[k * 128:(k + 1) * 128, hh2:hh2 + nnh]))
                for sc0 in range(0, nnh, 512):
                    OP("V", ("arF1", "swrec"), ("sqb",),
                       lambda v, ci=ci, sc0=sc0: v.tensor_scalar(
                           out=sqb[:, 0:512], in0=arF1[:, sc0:sc0 + 512],
                           scalar1=swrec[:, ci:ci + 1], scalar2=1.0,
                           op0=ALU.mult, op1=ALU.min))
                    OP("V", ("sqb",), ("sqb",),
                       lambda v: v.tensor_scalar(out=sqb[:, 512:1024], in0=sqb[:, 0:512],
                                                 scalar1=-1.0, scalar2=RMAGIC,
                                                 op0=ALU.max, op1=ALU.add))
                    OP("V", ("sqb",), ("arB1",),
                       lambda v, hh2=hh2, sc0=sc0: v.tensor_scalar(
                           out=arB1[:, hh2 + sc0:hh2 + sc0 + 512], in0=sqb[:, 512:1024],
                           scalar1=-RMAGIC, scalar2=None, op0=ALU.add))
            OP("QG", ("arB1",), ("HQW",),
               lambda g, k=k, dst=dst, cols=cols: g.dma_start(
                   out=dst[k * 128:(k + 1) * 128, :], in_=arB1[:, 0:cols]))

    wq_quant(g0, g0q, D, INNER, 0)
    wq_quant(u0, u0q, D, INNER, 1)
    wq_quant(d0, d0q, INNER, D, 2)
    P.barrier()
    P.flush(block)

    def layernorm(xf_buf, xf, sq_buf, sq, xn_buf, xn):
        def st_emit(t):
            last = None
            for c in range(8):
                last = t.matmul(psC[0:1, 0:512], ones1f[:], xf[:, c * 512:(c + 1) * 512],
                                start=(c == 0), stop=(c == 7))
            return last
        OP("T", (xf_buf, "ones"), ("psC",), st_emit)
        for c in range(8):
            sl = (c % 2) * 512
            OP("S", (xf_buf,), (f"sqs{c % 2}",),
               lambda s, c=c, sl=sl: s.activation(out=sqb[:, sl:sl + 512],
                                                  in_=xf[:, c * 512:(c + 1) * 512],
                                                  func=AF.Square))
            OP("T", (f"sqs{c % 2}", "ones"), ("psD",),
               lambda t, c=c, sl=sl: t.matmul(psD[0:1, 0:512], ones1f[:],
                                              sqb[:, sl:sl + 512],
                                              start=(c == 0), stop=(c == 7)))
        OP("S", ("psC",), ("rowA",),
           lambda s: s.activation(out=rowA[0:1, 0:512], in_=psC[0:1, 0:512],
                                  func=AF.Identity, scale=1.0 / D))
        OP("S", ("psD",), ("rowB",),
           lambda s: s.activation(out=rowB[0:1, 0:512], in_=psD[0:1, 0:512],
                                  func=AF.Identity, scale=1.0 / D))
        OP("V", ("rowA",), ("rowC",),
           lambda v: v.tensor_tensor(out=rowC[0:1, 0:512], in0=rowA[0:1, 0:512],
                                     in1=rowA[0:1, 0:512], op=ALU.mult))
        OP("V", ("rowB", "rowC"), ("rowC",),
           lambda v: v.tensor_tensor(out=rowC[0:1, 0:512], in0=rowB[0:1, 0:512],
                                     in1=rowC[0:1, 0:512], op=ALU.subtract))
        OP("V", ("rowC",), ("rowC",),
           lambda v: v.tensor_scalar(out=rowC[0:1, 0:512], in0=rowC[0:1, 0:512],
                                     scalar1=EPS, scalar2=None, op0=ALU.add))
        OP("S", ("rowC",), ("rowC",),
           lambda s: s.activation(out=rowC[0:1, 0:512], in_=rowC[0:1, 0:512], func=AF.Ln))
        OP("S", ("rowC",), ("rowC",),
           lambda s: s.activation(out=rowC[0:1, 0:512], in_=rowC[0:1, 0:512],
                                  func=AF.Exp, scale=-0.5))
        OP("T", ("rowA", "ones"), ("psC",),
           lambda t: t.matmul(psC[:, 0:512], onesrf[:], rowA[0:1, 0:512],
                              start=True, stop=True))
        OP("S", ("psC",), ("amaxb",), lambda s: s.copy(amaxb[:, 0:512], psC[:, 0:512]))
        OP("T", ("rowC", "ones"), ("psD",),
           lambda t: t.matmul(psD[:, 0:512], onesrf[:], rowC[0:1, 0:512],
                              start=True, stop=True))
        OP("S", ("psD",), ("amaxb",), lambda s: s.copy(amaxb[:, 512:1024], psD[:, 0:512]))
        for c in range(8):
            eng = "V" if c % 4 != 3 else "G"
            sl = (c % 2) * 512
            OP(eng, (xf_buf, "amaxb"), (f"sqs{c % 2}",),
               lambda v, c=c, sl=sl: v.tensor_tensor(out=sqb[:, sl:sl + 512],
                                                     in0=xf[:, c * 512:(c + 1) * 512],
                                                     in1=amaxb[:, 0:512], op=ALU.subtract))
            OP(eng, (f"sqs{c % 2}", "amaxb"), (xn_buf,),
               lambda v, c=c, sl=sl: v.tensor_tensor(out=xn[:, c * 512:(c + 1) * 512],
                                                     in0=sqb[:, sl:sl + 512],
                                                     in1=amaxb[:, 512:1024], op=ALU.mult))

    def rope(src_buf, src, dst_buf, dst, t0, n):
        OP("T", (src_buf, "srot2"), ("psB",),
           lambda t: t.matmul(psB[:, 0:n], srot2[:], src, start=True, stop=True))
        OP("S", ("psB",), ("tb1",), lambda s: s.copy(tb1[:, 0:n], psB[:, 0:n]))
        OP("V", (src_buf, "rcos"), (src_buf,),
           lambda v: v.tensor_tensor(out=src, in0=src, in1=hbuf2[:, t0:t0 + n], op=ALU.mult))
        OP("V", ("tb1", "rsin"), ("tb1",),
           lambda v: v.tensor_tensor(out=tb1[:, 0:n], in0=tb1[:, 0:n],
                                     in1=hbuf2[:, 4096 + t0:4096 + t0 + n], op=ALU.mult))
        OP("V", (src_buf, "tb1"), (dst_buf,),
           lambda v: v.tensor_tensor(out=dst, in0=src, in1=tb1[:, 0:n], op=ALU.add))

    P.mark("A")
    for tau in range(NT):
        t0 = tau * TT
        for hh2 in range(2):
            OP("QS", ("HB",), ("arF1",),
               lambda en, t0=t0, hh2=hh2: en.dma_start(
                   out=arF1[:],
                   in_=ap3(x_in, (t0 + hh2 * 256) * D, [[D, 128], [128 * D, 2], [1, D]])))
            for i2 in range(2):
                i = hh2 * 2 + i2
                ps, pk = (psA, "psA") if i % 2 == 0 else (psB, "psB")
                def tr_emit(t, i2=i2, ps=ps):
                    last = None
                    for c in range(8):
                        last = t.transpose(ps[:, c * 128:(c + 1) * 128],
                                           arF1[:, i2 * 1024 + c * 128:i2 * 1024 + (c + 1) * 128],
                                           idf[:])
                    return last
                OP("T", ("arF1", "idf"), (pk,), tr_emit)
                OP("S", (pk,), ("arF2",),
                   lambda s, i=i, ps=ps: s.copy(
                       ap3(arF2, i * 128, [[4096, 128], [512, 8], [1, 128]]), ps[:]))
        OP("QA", ("arF2",), ("xTh",),
           lambda en, t0=t0: en.dma_start(
               out=ap3(xTh, t0, [[T, 128], [128 * T, 8], [1, 512]]),
               in_=ap3(arF2, 0, [[4096, 128], [512, 8], [1, 512]])))
        layernorm("arF2", arF2, None, None, "arB2", arB2)
        for m2 in range(2):
            def kp_emit(t, m2=m2):
                last = None
                for c in range(8):
                    last = t.matmul(psC[:, 0:512],
                                    hbuf[:, c * 256 + m2 * 128:c * 256 + (m2 + 1) * 128],
                                    arB2[:, c * 512:(c + 1) * 512],
                                    start=(c == 0), stop=(c == 7))
                return last
            OP("T", ("arB2", "wk0b"), ("psC",), kp_emit)
            OP("S", ("psC", "sbk0"), ("arB3",),
               lambda s, m2=m2: s.activation(out=arB3[:, 0:512], in_=psC[:, 0:512],
                                             func=AF.Identity, bias=sbk0[:, m2:m2 + 1]))
            rope("arB3", arB3[:, 0:512], "kv0k",
                 kv0k[:, m2 * WTP + PAD + t0:m2 * WTP + PAD + t0 + 512], t0, 512)
        for i in range(4):
            def vp_emit(t, i=i):
                for c in range(8):
                    t.matmul(psD[:, 0:256],
                             arB2[:, c * 512 + i * 128:c * 512 + (i + 1) * 128],
                             hbuf[:, 2048 + c * 256:2048 + (c + 1) * 256],
                             start=(c == 0), stop=False)
                return t.matmul(psD[:, 0:256], onesrb[:, 0:128], bvb[:],
                                start=False, stop=True)
            OP("T", ("arB2", "wv0b", "bvb", "ones"), ("psD",), vp_emit)
            OP("S", ("psD",), ("v0t",),
               lambda s, tau=tau, i=i: s.copy(
                   v0t[:, (tau * 4 + i) * 256:(tau * 4 + i + 1) * 256], psD[:, 0:256]))
        for m in range(8):
            def qp_emit(t, m=m):
                last = None
                for c in range(8):
                    last = t.matmul(psC[:, 0:512],
                                    wbufA[:, c * 1024 + m * 128:c * 1024 + (m + 1) * 128],
                                    arB2[:, c * 512:(c + 1) * 512],
                                    start=(c == 0), stop=(c == 7))
                return last
            OP("T", ("arB2", "wq0b"), ("psC",), qp_emit)
            OP("S", ("psC", "sbqhw", "shw8"), ("arB3",),
               lambda s, m=m: s.activation(out=arB3[:, 0:512], in_=psC[:, 0:512],
                                           func=AF.Identity, bias=sbqhw[:, m:m + 1],
                                           scale=shw8[:, m:m + 1]))
            rope("arB3", arB3[:, 0:512], "arB1", arB1[:, 0:512], t0, 512)
            OP("QA", ("arB1",), ("qT0h",),
               lambda en, m=m, t0=t0: en.dma_start(
                   out=ap3(qT0h, m * 128 * T + t0, [[T, 128], [1, 512]]),
                   in_=arB1[:, 0:512]))
        for c in range(8):
            sl = (c % 2) * 512
            OP("S", ("arB2",), (f"sqs{c % 2}",),
               lambda s, c=c, sl=sl: s.activation(out=sqb[:, sl:sl + 512],
                                                  in_=arB2[:, c * 512:(c + 1) * 512],
                                                  func=AF.Abs))
            if c == 0:
                OP("V", ("sqs0",), ("amaxb",),
                   lambda v: v.tensor_copy(amaxb[:, 0:512], sqb[:, 0:512]))
            else:
                OP("V", (f"sqs{c % 2}", "amaxb"), ("amaxb",),
                   lambda v, sl=sl: v.tensor_tensor(out=amaxb[:, 0:512],
                                                    in0=amaxb[:, 0:512],
                                                    in1=sqb[:, sl:sl + 512], op=ALU.max))
        for lev in (64, 32):
            for rp in range(0, lev, 32):
                OP("V", ("amaxb",), ("sqb",),
                   lambda v, lev=lev, rp=rp: v.tensor_copy(
                       sqb[rp:rp + 32, 0:512], amaxb[lev + rp:lev + rp + 32, 0:512]))
            OP("V", ("amaxb", "sqb"), ("amaxb",),
               lambda v, lev=lev: v.tensor_tensor(out=amaxb[0:lev, 0:512],
                                                  in0=amaxb[0:lev, 0:512],
                                                  in1=sqb[0:lev, 0:512], op=ALU.max))
        OP("V", ("amaxb",), ("amaxc",),
           lambda v: v.tensor_copy(sqb[0:32, 512:1024], amaxb[0:32, 0:512]))
        def amt_emit(t):
            last = None
            for p in range(4):
                last = t.transpose(psC[:, p * 32:(p + 1) * 32],
                                   sqb[0:32, 512 + p * 128:512 + (p + 1) * 128],
                                   idf[0:32, 0:32])
            return last
        OP("T", ("amaxc", "idf"), ("psC",), amt_emit)
        OP("S", ("psC",), ("sqb",), lambda s: s.copy(sqb[:, 0:128], psC[:, 0:128]))
        OP("V", ("sqb",), ("amaxt",),
           lambda v: v.reduce_max(out=amaxb[:, 0:4],
                                  in_=ap3(sqb, 0, [[1024, 128], [32, 4], [1, 32]]),
                                  axis=AX.X))
        OP("V", ("amaxt",), ("amaxt",),
           lambda v: v.tensor_scalar(out=amaxb[:, 0:4], in0=amaxb[:, 0:4],
                                     scalar1=1e-5, scalar2=None, op0=ALU.max))
        OP("S", ("amaxt",), ("amaxt2",),
           lambda s: s.activation(out=amaxb[:, 4:8], in_=amaxb[:, 0:4], func=AF.Ln))
        OP("S", ("amaxt2",), ("amaxt2",),
           lambda s: s.activation(out=amaxb[:, 4:8], in_=amaxb[:, 4:8],
                                  func=AF.Exp, scale=-1.0, bias=cl127[:, 0:1]))
        OP("V", ("amaxt",), ("amaxt3",),
           lambda v: v.tensor_scalar(out=amaxb[:, 8:12], in0=amaxb[:, 0:4],
                                     scalar1=1.0 / 127.0, scalar2=None, op0=ALU.mult))
        def rowt_emit(t):
            last = None
            for p in range(4):
                last = t.transpose(psC[0:1, p * 128:(p + 1) * 128],
                                   amaxb[:, 4 + p:5 + p], idf[:])
                last = t.transpose(psD[0:1, p * 128:(p + 1) * 128],
                                   amaxb[:, 8 + p:9 + p], idf[:])
            return last
        OP("T", ("amaxt2", "amaxt3", "idf"), ("psC", "psD"), rowt_emit)
        OP("S", ("psC",), ("rowB",), lambda s: s.copy(rowB[0:1, 0:512], psC[0:1, 0:512]))
        OP("S", ("psD",), ("rowC",), lambda s: s.copy(rowC[0:1, 0:512], psD[0:1, 0:512]))
        OP("T", ("rowB", "ones"), ("psC",),
           lambda t: t.matmul(psC[:, 0:512], onesrf[:], rowB[0:1, 0:512],
                              start=True, stop=True))
        OP("S", ("psC",), ("amaxb",), lambda s: s.copy(amaxb[:, 0:512], psC[:, 0:512]))
        OP("T", ("rowC", "ones"), ("psD",),
           lambda t: t.matmul(psD[:, 0:512], onesrf[:], rowC[0:1, 0:512],
                              start=True, stop=True))
        OP("S", ("psD",), ("amaxb",), lambda s: s.copy(amaxb[:, 512:1024], psD[:, 0:512]))
        for c in range(8):
            OP("V", ("arB2", "amaxb"), ("sqb",),
               lambda v, c=c: v.tensor_tensor(out=sqb[:, 0:512],
                                              in0=arB2[:, c * 512:(c + 1) * 512],
                                              in1=amaxb[:, 0:512], op=ALU.mult))
            OP("V", ("sqb",), ("sqb",),
               lambda v: v.tensor_scalar(out=sqb[:, 512:1024], in0=sqb[:, 0:512],
                                         scalar1=RMAGIC, scalar2=-RMAGIC,
                                         op0=ALU.add, op1=ALU.add))
            OP("V", ("sqb", "amaxb"), ("arB1",),
               lambda v: v.tensor_tensor(out=arB1[:, 0:512], in0=sqb[:, 512:1024],
                                         in1=amaxb[:, 512:1024], op=ALU.mult))
            OP("QA", ("arB1",), ("xqh",),
               lambda en, c=c, t0=t0: en.dma_start(
                   out=ap3(xqh, c * 128 * T + t0, [[T, 128], [1, 512]]),
                   in_=arB1[:, 0:512]))
    P.barrier()
    P.flush(block)

    P.mark("B")
    OP("V", (), ("knrun",), lambda v: v.memset(rowA[0:1, 600:601], 0.0))
    for seg0 in range(0, WTP, 512):
        L = min(512, WTP - seg0)
        for ck in range(2):
            sl = (ck % 2) * 512
            OP("V", ("kv0k",), (f"sqs{ck}",),
               lambda v, ck=ck, seg0=seg0, L=L, sl=sl: v.tensor_tensor(
                   out=sqb[:, sl:sl + L],
                   in0=kv0k[:, ck * WTP + seg0:ck * WTP + seg0 + L],
                   in1=kv0k[:, ck * WTP + seg0:ck * WTP + seg0 + L], op=ALU.mult))
            OP("T", (f"sqs{ck}", "ones"), ("psC",),
               lambda t, ck=ck, L=L, sl=sl: t.matmul(
                   psC[0:1, 0:L], ones1f[:], sqb[:, sl:sl + L],
                   start=(ck == 0), stop=(ck == 1)))
        OP("S", ("psC",), ("rowB",),
           lambda sc, L=L: sc.copy(rowB[0:1, 0:L], psC[0:1, 0:L]))
        OP("V", ("rowB",), ("knseg",),
           lambda v, L=L: v.reduce_max(out=rowB[0:1, 630:631], in_=rowB[0:1, 0:L],
                                       axis=AX.X))
        OP("V", ("knseg", "knrun"), ("knrun",),
           lambda v: v.tensor_tensor(out=rowA[0:1, 600:601], in0=rowA[0:1, 600:601],
                                     in1=rowB[0:1, 630:631], op=ALU.max))
    OP("S", ("knrun",), ("knrun",),
       lambda sc: sc.activation(out=rowA[0:1, 600:601], in_=rowA[0:1, 600:601],
                                func=AF.Sqrt))
    C_PT0W, C_PT0G, C_VGP, C_ZEBE, C_ZEBO = 2560, 3584, 4608, 5120, 5632
    OP("V", (), ("vg4",),
       lambda v: v.memset(wbufA[0:32, C_VGP:C_VGP + 512], 0.0))
    for g in range(4):
        OP("V", ("v0t",), ("vg4",),
           lambda v, g=g: v.tensor_copy(
               wbufA[0:32, C_VGP + g * 128 + (g % 2) * 64:C_VGP + g * 128 + (g % 2) * 64 + 64],
               v0t[0:32, g * 64:(g + 1) * 64]))
    for tau in range(NT):
        t0 = tau * TT
        OP("QS", ("qT0h",), ("arB2",),
           lambda en, t0=t0: en.dma_start(
               out=ap3(arB2, 0, [[4096, 128], [512, 8], [1, 512]]),
               in_=ap3(qT0h, t0, [[T, 128], [128 * T, 8], [1, 512]])))
        for c in range(8):
            sl = (c % 2) * 512
            OP("V", ("arB2",), (f"sqs{c % 2}",),
               lambda v, c=c, sl=sl: v.tensor_tensor(
                   out=sqb[:, sl:sl + 512], in0=arB2[:, c * 512:(c + 1) * 512],
                   in1=arB2[:, c * 512:(c + 1) * 512], op=ALU.mult))
            OP("T", (f"sqs{c % 2}", "ones"), ("psC",),
               lambda t, c=c, sl=sl: t.matmul(psC[0:1, 0:512], ones1f[:],
                                              sqb[:, sl:sl + 512],
                                              start=(c == 0), stop=(c == 7)))
        OP("S", ("psC",), ("rowB",),
           lambda sc: sc.copy(rowB[0:1, 0:512], psC[0:1, 0:512]))
        OP("V", ("rowB",), ("qn2",),
           lambda v: v.reduce_max(out=rowB[0:1, 620:621], in_=rowB[0:1, 0:512],
                                  axis=AX.X))
        OP("S", ("qn2",), ("qn2",),
           lambda sc: sc.activation(out=rowB[0:1, 620:621], in_=rowB[0:1, 620:621],
                                    func=AF.Sqrt))
        OP("V", ("qn2", "knrun"), ("Cval",),
           lambda v: v.tensor_tensor(out=rowB[0:1, 621:622], in0=rowB[0:1, 620:621],
                                     in1=rowA[0:1, 600:601], op=ALU.mult))
        OP("V", ("Cval",), ("Cval",),
           lambda v: v.tensor_scalar(out=rowB[0:1, 621:622], in0=rowB[0:1, 621:622],
                                     scalar1=-1.0, scalar2=None, op0=ALU.mult))
        OP("G", ("Cval",), ("sCneg",),
           lambda g: g.partition_broadcast(sCneg[:], rowB[0:1, 621:622], channels=64))
        OP("V", (), ("vov",), lambda v: v.memset(wbufB[:, 0:4096], 0.0))
        for s8 in range(8):
            sg = tau * 8 + s8
            base = s8 * 512
            for rp in range(4):
                tok0 = 64 * sg - 32 + rp * 32
                if tok0 < 0 or tok0 >= T:
                    continue
                c0 = tok0 // 128
                r0 = tok0 % 128
                for g in range(4):
                    OP("V", ("v0t",), ("vov",),
                       lambda v, base=base, rp=rp, c0=c0, r0=r0, g=g: v.tensor_copy(
                           wbufB[rp * 32:(rp + 1) * 32,
                                 base + g * 128 + (g % 2) * 64:base + g * 128 + (g % 2) * 64 + 64],
                           v0t[r0:r0 + 32, c0 * 256 + g * 64:c0 * 256 + (g + 1) * 64]))
        for s8 in range(8):
            i0 = t0 + s8 * 64
            sg = i0 // 64
            mvar = 0 if sg == 0 else (2 if sg == 63 else 1)
            wcol0 = PAD + 64 * sg - 32
            OP("V", ("arB2", "pmask"), ("zebE",),
               lambda v, s8=s8: v.tensor_scalar(
                   out=ap3(wbufA, C_ZEBE, [[8192, 128], [64, 8], [1, 64]]),
                   in0=ap3(arB2, s8 * 64, [[4096, 128], [512, 8], [1, 64]]),
                   scalar1=pmask[:, 0:1], scalar2=None, op0=ALU.mult))
            OP("V", ("arB2", "pmask"), ("zebO",),
               lambda v, s8=s8: v.tensor_scalar(
                   out=ap3(wbufA, C_ZEBO, [[8192, 128], [64, 8], [1, 64]]),
                   in0=ap3(arB2, s8 * 64, [[4096, 128], [512, 8], [1, 64]]),
                   scalar1=pmask[:, 1:2], scalar2=None, op0=ALU.mult))
            def sc_emit(t, wcol0=wcol0, lo=True):
                last = None
                ps = psA if lo else psB
                for j in (range(0, 8) if lo else range(8, 16)):
                    cch = j // 2
                    p = j % 2
                    ck = 0 if cch < 4 else 1
                    slot = j % 8
                    zc = C_ZEBE if p == 0 else C_ZEBO
                    last = t.matmul(ps[0:64, slot * 128:(slot + 1) * 128],
                                    wbufA[:, zc + cch * 64:zc + (cch + 1) * 64],
                                    kv0k[:, ck * WTP + wcol0:ck * WTP + wcol0 + 128],
                                    start=True, stop=True)
                return last
            OP("T", ("zebE", "zebO", "kv0k"), ("psA",), lambda t, w=wcol0: sc_emit(t, w, True))
            OP("T", ("zebE", "zebO", "kv0k"), ("psB",), lambda t, w=wcol0: sc_emit(t, w, False))
            def sg_emit(t):
                last = None
                for j in range(16):
                    cch = j // 2
                    p = j % 2
                    ck = 0 if cch < 4 else 1
                    zc = C_ZEBE if p == 0 else C_ZEBO
                    last = t.matmul(psE[0:64, j * 8:(j + 1) * 8],
                                    wbufA[:, zc + cch * 64:zc + (cch + 1) * 64],
                                    kv0k[:, ck * WTP + PAD:ck * WTP + PAD + 8],
                                    start=True, stop=True)
                return last
            OP("T", ("zebE", "zebO", "kv0k"), ("psE",), sg_emit)
            OP("S", ("psA", "sCneg"), ("p0w",),
               lambda sc: sc.activation(out=wbufA[0:64, 0:1024], in_=psA[0:64, :],
                                        func=AF.Exp, bias=sCneg[:]))
            OP("S", ("psB", "sCneg"), ("p0w",),
               lambda sc: sc.activation(out=wbufA[0:64, 1024:2048], in_=psB[0:64, :],
                                        func=AF.Exp, bias=sCneg[:]))
            OP("V", (), ("p0g",), lambda v: v.memset(wbufA[0:64, 2048:2560], 0.0))
            OP("S", ("psE", "sCneg", "p0g"), ("p0g",),
               lambda sc: sc.activation(
                   out=ap3(wbufA, 2048, [[8192, 64], [32, 16], [1, 8]]),
                   in_=ap3(psE, 0, [[512, 64], [8, 16], [1, 8]]),
                   func=AF.Exp, bias=sCneg[:]))
            OP("V", ("p0w", "sm0"), ("p0w",),
               lambda v, mvar=mvar: v.tensor_tensor(
                   out=wbufA[0:64, 0:2048], in0=wbufA[0:64, 0:2048],
                   in1=hbuf2[0:64, 8192 + mvar * 2048:8192 + (mvar + 1) * 2048],
                   op=ALU.mult))
            OP("V", ("p0g", "sm0g"), ("p0g",),
               lambda v, mvar=mvar: v.tensor_tensor(
                   out=wbufA[0:64, 2048:2560], in0=wbufA[0:64, 2048:2560],
                   in1=hbuf2[0:64, 14336 + mvar * 512:14336 + (mvar + 1) * 512],
                   op=ALU.mult))
            OP("V", ("p0w",), ("l0_s",),
               lambda v: v.reduce_sum(out=l0_s[:],
                                      in_=ap3(wbufA, 0, [[8192, 64], [128, 16], [1, 128]]),
                                      axis=AX.X))
            OP("V", ("p0g",), ("r0_s",),
               lambda v: v.reduce_sum(out=r0_s[:],
                                      in_=ap3(wbufA, 2048, [[8192, 64], [32, 16], [1, 32]]),
                                      axis=AX.X))
            OP("V", ("l0_s", "r0_s"), ("l0_s",),
               lambda v: v.tensor_tensor(out=l0_s[:], in0=l0_s[:], in1=r0_s[:], op=ALU.add))
            OP("V", ("l0_s",), ("r0_s",), lambda v: v.reciprocal(out=r0_s[:], in_=l0_s[:]))
            OP("V", ("p0w", "r0_s"), ("p0w",),
               lambda v: v.tensor_tensor(
                   out=ap3(wbufA, 0, [[8192, 64], [128, 16], [1, 128]]),
                   in0=ap3(wbufA, 0, [[8192, 64], [128, 16], [1, 128]]),
                   in1=ap3(r0_s, 0, [[16, 64], [1, 16], [0, 128]]), op=ALU.mult))
            OP("V", ("p0g", "r0_s"), ("p0g",),
               lambda v: v.tensor_tensor(
                   out=ap3(wbufA, 2048, [[8192, 64], [32, 16], [1, 32]]),
                   in0=ap3(wbufA, 2048, [[8192, 64], [32, 16], [1, 32]]),
                   in1=ap3(r0_s, 0, [[16, 64], [1, 16], [0, 32]]), op=ALU.mult))
            def tpw_emit(t):
                last = None
                for h in range(16):
                    last = t.transpose(psT[:, h * 64:(h + 1) * 64],
                                       wbufA[0:64, h * 128:(h + 1) * 128],
                                       ident[0:64, 0:64])
                return last
            OP("T", ("p0w", "ident"), ("psT",), tpw_emit)
            OP("S", ("psT",), ("pT0w",),
               lambda sc: sc.copy(wbufA[:, C_PT0W:C_PT0W + 1024], psT[:]))
            def tpg_emit(t):
                last = None
                for j in range(16):
                    last = t.transpose(psT[0:32, j * 64:(j + 1) * 64],
                                       wbufA[0:64, 2048 + j * 32:2048 + (j + 1) * 32],
                                       ident[0:64, 0:64])
                return last
            OP("T", ("p0g", "ident"), ("psT",), tpg_emit)
            OP("S", ("psT",), ("pT0g",),
               lambda sc: sc.copy(wbufA[0:32, C_PT0G:C_PT0G + 1024], psT[0:32, 0:1024]))
            def av_emit(t, s8=s8):
                last = None
                for cch in range(8):
                    for p in range(2):
                        j = 2 * cch + p
                        g = (0 if cch < 4 else 2) + p
                        last = t.matmul(
                            psC[:, cch * 64:(cch + 1) * 64],
                            wbufB[:, s8 * 512 + g * 128:s8 * 512 + (g + 1) * 128],
                            wbufA[:, C_PT0W + j * 64:C_PT0W + (j + 1) * 64],
                            start=(p == 0), stop=False)
                        last = t.matmul(
                            psC[:, cch * 64:(cch + 1) * 64],
                            wbufA[0:32, C_VGP + g * 128:C_VGP + (g + 1) * 128],
                            wbufA[0:32, C_PT0G + j * 64:C_PT0G + (j + 1) * 64],
                            start=False, stop=(p == 1))
                return last
            OP("T", ("vov", "pT0w", "pT0g", "vg4"), ("psC",), av_emit)
            OP("S", ("psC",), ("arB4",),
               lambda sc, s8=s8: sc.copy(
                   ap3(arB4, s8 * 64, [[4096, 128], [512, 8], [1, 64]]),
                   ap3(psC, 0, [[512, 128], [64, 8], [1, 64]])))
        for m in range(8):
            def wo_emit(t, m=m):
                last = None
                for c in range(8):
                    last = t.matmul(psD[:, 0:512],
                                    hbuf[:, 4096 + c * 1024 + m * 128:4096 + c * 1024 + (m + 1) * 128],
                                    arB4[:, c * 512:(c + 1) * 512],
                                    start=(c == 0), stop=(c == 7))
                return last
            OP("T", ("arB4", "wo0b"), ("psD",), wo_emit)
            OP("S", ("psD", "sbo0"), ("arB1",),
               lambda s, m=m: s.activation(out=arB1[:, 0:512], in_=psD[:, 0:512],
                                           func=AF.Identity, bias=sbo0[:, m:m + 1]))
            OP("QA", ("arB1",), ("aoth",),
               lambda en, m=m, t0=t0: en.dma_start(
                   out=ap3(aoth, m * 128 * T + t0, [[T, 128], [1, 512]]),
                   in_=arB1[:, 0:512]))

    P.mark("Bg")
    OP("QS", ("qT0h",), ("gq_s",),
       lambda en: en.dma_start(out=ap3(gq_s, 0, [[64, 128], [8, 8], [1, 8]]),
                               in_=ap3(qT0h, 0, [[T, 128], [128 * T, 8], [1, 8]])))
    OP("V", (), ("arB1",), lambda v: v.memset(arB1[:, 0:256], 0.0))
    for j in range(16):
        cch = j // 2
        p = j % 2
        g = (0 if cch < 4 else 2) + p
        ck = g // 2
        for rp in range(2):
            OP("V", ("gq_s",), ("arB1",),
               lambda v, j=j, p=p, ck=ck, rp=rp: v.tensor_copy(
                   arB1[p * 64 + rp * 32:p * 64 + rp * 32 + 32,
                        ck * 128 + j * 8:ck * 128 + (j + 1) * 8],
                   gq_s[p * 64 + rp * 32:p * 64 + rp * 32 + 32,
                        cch * 8:cch * 8 + 8]))
    for nb in range(8):
        ps, pk = (psC, "psC") if nb % 2 == 0 else (psD, "psD")
        def gs_emit(t, nb=nb, ps=ps):
            last = None
            for ck in range(2):
                last = t.matmul(ps[:, 0:512], arB1[:, ck * 128:(ck + 1) * 128],
                                kv0k[:, ck * WTP + PAD + nb * 512:ck * WTP + PAD + (nb + 1) * 512],
                                start=(ck == 0), stop=(ck == 1))
            return last
        OP("T", ("arB1", "kv0k"), (pk,), gs_emit)
        OP("S", (pk,), ("arF2",),
           lambda s, nb=nb, ps=ps: s.copy(arF2[:, nb * 512:(nb + 1) * 512], ps[:, 0:512]))
    OP("V", ("arF2",), ("gm",),
       lambda v: v.reduce_max(out=arF1[:, 0:1], in_=arF2[:], axis=AX.X))
    OP("V", ("gm",), ("gm",),
       lambda v: v.tensor_scalar(out=arF1[:, 1:2], in0=arF1[:, 0:1],
                                 scalar1=-1.0, scalar2=None, op0=ALU.mult))
    OP("S", ("arF2", "gm"), ("arB3", "gl"),
       lambda s: s.activation(out=arB3[:], in_=arF2[:], func=AF.Exp,
                              bias=arF1[:, 1:2], accum_out=arF1[:, 2:3]))
    OP("V", ("gl",), ("gr",),
       lambda v: v.reciprocal(out=arF1[:, 3:4], in_=arF1[:, 2:3]))
    OP("V", ("arB3", "gr"), ("arB3",),
       lambda v: v.tensor_scalar(out=arB3[:], in0=arB3[:], scalar1=arF1[:, 3:4],
                                 scalar2=None, op0=ALU.mult))
    for grp in range(4):
        def gt_emit(t, grp=grp):
            last = None
            for j in range(8):
                kk = grp * 8 + j
                last = t.transpose(psT[:, j * 128:(j + 1) * 128],
                                   arB3[:, kk * 128:(kk + 1) * 128], ident[:])
            return last
        OP("T", ("arB3", "ident"), ("psT",), gt_emit)
        OP("S", ("psT",), ("arB2",),
           lambda s, grp=grp: s.copy(arB2[:, grp * 1024:(grp + 1) * 1024], psT[:]))
    def gav_emit(t):
        last = None
        for kk in range(32):
            last = t.matmul(psE[:, 0:256], arB2[:, kk * 128:(kk + 1) * 128],
                            v0t[:, kk * 256:(kk + 1) * 256],
                            start=(kk == 0), stop=(kk == 31))
        return last
    OP("T", ("arB2", "v0t"), ("psE",), gav_emit)
    OP("S", ("psE",), ("arB4",), lambda s: s.copy(arB4[:, 0:256], psE[:, 0:256]))
    def ghead_emit(t):
        t.transpose(psT[:, 0:128], arB4[:, 0:128], ident[:])
        return t.transpose(psT[:, 128:256], arB4[:, 128:256], ident[:])
    OP("T", ("arB4", "ident"), ("psT",), ghead_emit)
    OP("S", ("psT",), ("arB3",), lambda s: s.copy(arB3[:, 0:256], psT[:, 0:256]))
    for j in range(16):
        cch = j // 2
        p = j % 2
        g = (0 if cch < 4 else 2) + p
        for rp in range(2):
            OP("V", ("arB3",), ("gout",),
               lambda v, j=j, cch=cch, p=p, g=g, rp=rp: v.tensor_copy(
                   arF2[p * 64 + rp * 32:p * 64 + rp * 32 + 32,
                        cch * 8:cch * 8 + 8],
                   arB3[p * 64 + rp * 32:p * 64 + rp * 32 + 32,
                        (g // 2) * 128 + j * 8:(g // 2) * 128 + (j + 1) * 8]))
    for m in range(8):
        def gwo_emit(t, m=m):
            last = None
            for c in range(8):
                last = t.matmul(psE[:, 0:8],
                                hbuf[:, 4096 + c * 1024 + m * 128:4096 + c * 1024 + (m + 1) * 128],
                                arF3[:, c * 8:(c + 1) * 8].astype(BF16) if False else arF3[:, c * 8:(c + 1) * 8],
                                start=(c == 0), stop=(c == 7))
            return last
        pass
    OP("V", ("gout",), ("goutb",), lambda v: v.tensor_copy(arB4[:, 0:64], arF2[:, 0:64]))
    for m in range(8):
        def gwo_emit(t, m=m):
            last = None
            for c in range(8):
                last = t.matmul(psE[:, 0:8],
                                hbuf[:, 4096 + c * 1024 + m * 128:4096 + c * 1024 + (m + 1) * 128],
                                arB4[:, c * 8:(c + 1) * 8],
                                start=(c == 0), stop=(c == 7))
            return last
        OP("T", ("goutb", "wo0b"), ("psE",), gwo_emit)
        OP("S", ("psE", "sbo0"), ("arB1",),
           lambda s, m=m: s.activation(out=arB1[:, m * 8:(m + 1) * 8], in_=psE[:, 0:8],
                                       func=AF.Identity, bias=sbo0[:, m:m + 1]))
    OP("QA", ("arB1",), ("aoth",),
       lambda en: en.dma_start(out=ap3(aoth, 0, [[T, 128], [128 * T, 8], [1, 8]]),
                               in_=ap3(arB1, 0, [[4096, 128], [8, 8], [1, 8]])))
    P.barrier()
    P.flush(block)

    P.mark("C")
    for tau in range(NT):
        t0 = tau * TT
        OP("QS", ("xqh",), ("arB2",),
           lambda en, t0=t0: en.dma_start(
               out=ap3(arB2, 0, [[4096, 128], [512, 8], [1, 512]]),
               in_=ap3(xqh, t0, [[T, 128], [128 * T, 8], [1, 512]])))
        OP("QS", ("aoth",), ("arB3",),
           lambda en, t0=t0: en.dma_start(
               out=ap3(arB3, 0, [[4096, 128], [512, 8], [1, 512]]),
               in_=ap3(aoth, t0, [[T, 128], [128 * T, 8], [1, 512]])))
        OP("QS", ("xTh",), ("arF2",),
           lambda en, t0=t0: en.dma_start(
               out=ap3(arF2, 0, [[4096, 128], [512, 8], [1, 512]]),
               in_=ap3(xTh, t0, [[T, 128], [128 * T, 8], [1, 512]])))
        for qm in range(4):
            gbuf, gk = (wbufA, "wbufA") if qm % 2 == 0 else (wbufB, "wbufB")
            uof, uk = (0, "hbuf2w") if qm % 2 == 0 else (8192, "hbuf2x")
            OP("QG", ("HQW",), (gk,),
               lambda en, qm=qm, gbuf=gbuf: en.dma_start(
                   out=ap3(gbuf, 0, [[8192, 128], [1024, 8], [1, 1024]]),
                   in_=ap3(g0q, qm * 1024, [[INNER, 128], [128 * INNER, 8], [1, 1024]])))
            OP("QG", ("HQW",), (uk,),
               lambda en, qm=qm, uof=uof: en.dma_start(
                   out=ap3(hbuf2, uof, [[16384, 128], [1024, 8], [1, 1024]]),
                   in_=ap3(u0q, qm * 1024, [[INNER, 128], [128 * INNER, 8], [1, 1024]])))
            for mh in range(8):
                m = qm * 8 + mh
                def ga_emit(t, mh=mh, gbuf=gbuf):
                    last = None
                    for c in range(8):
                        last = t.matmul(psC[:, 0:512],
                                        gbuf[:, c * 1024 + mh * 128:c * 1024 + (mh + 1) * 128],
                                        arB2[:, c * 512:(c + 1) * 512],
                                        start=(c == 0), stop=(c == 7))
                    return last
                OP("T", ("arB2", gk), ("psC",), ga_emit)
                def up_emit(t, mh=mh, uof=uof):
                    last = None
                    for c in range(8):
                        last = t.matmul(psD[:, 0:512],
                                        hbuf2[:, uof + c * 1024 + mh * 128:uof + c * 1024 + (mh + 1) * 128],
                                        arB2[:, c * 512:(c + 1) * 512],
                                        start=(c == 0), stop=(c == 7))
                    return last
                OP("T", ("arB2", uk), ("psD",), up_emit)
                OP("S", ("psC", "swsc"), ("arF1",),
                   lambda s: s.activation(out=arF1[:, 0:512], in_=psC[:, 0:512],
                                          func=AF.Silu, scale=swsc[:, 0:1]))
                OP("S", ("psD", "swsc"), ("arF1b",),
                   lambda s: s.activation(out=arF1[:, 512:1024], in_=psD[:, 0:512],
                                          func=AF.Identity, scale=swsc[:, 1:2]))
                OP("V", ("arF1", "arF1b"), ("hbuf",),
                   lambda v, m=m: v.tensor_tensor(out=hbuf[:, m * 512:(m + 1) * 512],
                                                  in0=arF1[:, 0:512], in1=arF1[:, 512:1024],
                                                  op=ALU.mult))
                if m == 0:
                    OP("V", ("hbuf",), ("amaxb",),
                       lambda v: v.tensor_copy(amaxb[:, 0:512], hbuf[:, 0:512]))
                    OP("V", ("hbuf",), ("amaxb",),
                       lambda v: v.tensor_copy(amaxb[:, 512:1024], hbuf[:, 0:512]))
                else:
                    OP("V", ("hbuf", "amaxb"), ("amaxb",),
                       lambda v, m=m: v.tensor_tensor(out=amaxb[:, 0:512],
                                                      in0=amaxb[:, 0:512],
                                                      in1=hbuf[:, m * 512:(m + 1) * 512],
                                                      op=ALU.max))
                    OP("V", ("hbuf", "amaxb"), ("amaxb",),
                       lambda v, m=m: v.tensor_tensor(out=amaxb[:, 512:1024],
                                                      in0=amaxb[:, 512:1024],
                                                      in1=hbuf[:, m * 512:(m + 1) * 512],
                                                      op=ALU.min))
        OP("V", ("amaxb",), ("amaxb",),
           lambda v: v.tensor_scalar(out=amaxb[:, 512:1024], in0=amaxb[:, 512:1024],
                                     scalar1=-1.0, scalar2=None, op0=ALU.mult))
        OP("V", ("amaxb", "amaxb"), ("amaxb",),
           lambda v: v.tensor_tensor(out=amaxb[:, 0:512], in0=amaxb[:, 0:512],
                                     in1=amaxb[:, 512:1024], op=ALU.max))
        for lev in (64, 32):
            for rp in range(0, lev, 32):
                OP("V", ("amaxb",), ("sqb",),
                   lambda v, lev=lev, rp=rp: v.tensor_copy(
                       sqb[rp:rp + 32, 0:512], amaxb[lev + rp:lev + rp + 32, 0:512]))
            OP("V", ("amaxb", "sqb"), ("amaxb",),
               lambda v, lev=lev: v.tensor_tensor(out=amaxb[0:lev, 0:512],
                                                  in0=amaxb[0:lev, 0:512],
                                                  in1=sqb[0:lev, 0:512], op=ALU.max))
        OP("V", ("amaxb",), ("amaxc",),
           lambda v: v.tensor_copy(sqb[0:32, 512:1024], amaxb[0:32, 0:512]))
        def amt_emit(t):
            last = None
            for p in range(4):
                last = t.transpose(psC[:, p * 32:(p + 1) * 32],
                                   sqb[0:32, 512 + p * 128:512 + (p + 1) * 128],
                                   idf[0:32, 0:32])
            return last
        OP("T", ("amaxc", "idf"), ("psC",), amt_emit)
        OP("S", ("psC",), ("sqb",), lambda s: s.copy(sqb[:, 0:128], psC[:, 0:128]))
        OP("V", ("sqb",), ("amaxt",),
           lambda v: v.reduce_max(out=amaxb[:, 0:4],
                                  in_=ap3(sqb, 0, [[1024, 128], [32, 4], [1, 32]]),
                                  axis=AX.X))
        OP("V", ("amaxt",), ("amaxt",),
           lambda v: v.tensor_scalar(out=amaxb[:, 0:4], in0=amaxb[:, 0:4],
                                     scalar1=1e-5, scalar2=None, op0=ALU.max))
        OP("S", ("amaxt",), ("amaxt2",),
           lambda s: s.activation(out=amaxb[:, 4:8], in_=amaxb[:, 0:4], func=AF.Ln))
        OP("S", ("amaxt2",), ("amaxt2",),
           lambda s: s.activation(out=amaxb[:, 4:8], in_=amaxb[:, 4:8],
                                  func=AF.Exp, scale=-1.0, bias=cl127[:, 0:1]))
        OP("V", ("amaxt",), ("amaxt3",),
           lambda v: v.tensor_scalar(out=amaxb[:, 8:12], in0=amaxb[:, 0:4],
                                     scalar1=1.0 / 127.0, scalar2=None, op0=ALU.mult))
        def rowt_emit(t):
            last = None
            for p in range(4):
                last = t.transpose(psC[0:1, p * 128:(p + 1) * 128],
                                   amaxb[:, 4 + p:5 + p], idf[:])
                last = t.transpose(psD[0:1, p * 128:(p + 1) * 128],
                                   amaxb[:, 8 + p:9 + p], idf[:])
            return last
        OP("T", ("amaxt2", "amaxt3", "idf"), ("psC", "psD"), rowt_emit)
        OP("S", ("psC",), ("rowB",), lambda s: s.copy(rowB[0:1, 0:512], psC[0:1, 0:512]))
        OP("S", ("psD",), ("rowC",), lambda s: s.copy(rowC[0:1, 0:512], psD[0:1, 0:512]))
        OP("T", ("rowB", "ones"), ("psC",),
           lambda t: t.matmul(psC[:, 0:512], onesrf[:], rowB[0:1, 0:512],
                              start=True, stop=True))
        OP("S", ("psC",), ("amaxb",), lambda s: s.copy(amaxb[:, 0:512], psC[:, 0:512]))
        OP("T", ("rowC", "ones"), ("psD",),
           lambda t: t.matmul(psD[:, 0:512], onesrf[:], rowC[0:1, 0:512],
                              start=True, stop=True))
        OP("S", ("psD",), ("amaxb",), lambda s: s.copy(amaxb[:, 512:1024], psD[:, 0:512]))
        for m in range(32):
            OP("V", ("hbuf", "amaxb"), ("sqb",),
               lambda v, m=m: v.tensor_tensor(out=sqb[:, 0:512],
                                              in0=hbuf[:, m * 512:(m + 1) * 512],
                                              in1=amaxb[:, 0:512], op=ALU.mult))
            OP("V", ("sqb",), ("sqb",),
               lambda v: v.tensor_scalar(out=sqb[:, 512:1024], in0=sqb[:, 0:512],
                                         scalar1=RMAGIC, scalar2=-RMAGIC,
                                         op0=ALU.add, op1=ALU.add))
            OP("V", ("sqb", "amaxb"), ("hbuf",),
               lambda v, m=m: v.tensor_tensor(out=hbuf[:, m * 512:(m + 1) * 512],
                                              in0=sqb[:, 512:1024], in1=amaxb[:, 512:1024], op=ALU.mult))
        for qm in range(4):
            dbuf, dk = (wbufA, "wbufA") if qm % 2 == 0 else (wbufB, "wbufB")
            OP("QG", ("HQW",), (dk,),
               lambda en, qm=qm, dbuf=dbuf: en.dma_start(
                   out=ap3(dbuf, 0, [[8192, 128], [256, 32], [1, 256]]),
                   in_=ap3(d0q, qm * 256, [[D, 128], [128 * D, 32], [1, 256]])))
            for mh in range(2):
                m = qm * 2 + mh
                def dn_emit(t, mh=mh, dbuf=dbuf):
                    last = None
                    for k in range(32):
                        last = t.matmul(psC[:, 0:512],
                                        dbuf[:, k * 256 + mh * 128:k * 256 + (mh + 1) * 128],
                                        hbuf[:, k * 512:(k + 1) * 512],
                                        start=(k == 0), stop=(k == 31))
                    return last
                OP("T", ("hbuf", dk), ("psC",), dn_emit)
                OP("S", ("psC", "swsc"), ("arF1",),
                   lambda s: s.activation(out=arF1[:, 0:512], in_=psC[:, 0:512],
                                          func=AF.Identity, scale=swsc[:, 2:3]))
                OP("V", ("arF1", "arB3"), ("arF1",),
                   lambda v, m=m: v.tensor_tensor(out=arF1[:, 0:512], in0=arF1[:, 0:512],
                                                  in1=arB3[:, m * 512:(m + 1) * 512],
                                                  op=ALU.add))
                OP("V", ("arF1", "arF2"), ("arF2",),
                   lambda v, m=m: v.tensor_tensor(out=arF2[:, m * 512:(m + 1) * 512],
                                                  in0=arF1[:, 0:512],
                                                  in1=arF2[:, m * 512:(m + 1) * 512],
                                                  op=ALU.add))
        layernorm("arF2", arF2, None, None, "arB4", arB4)
        for widx, (wsrc, wkey) in enumerate(((wk1, "l1k"), (wv1, "l1v"), (wq1, "l1q"))):
            OP("QG", ("HB",), ("wbufB",),
               lambda en, wsrc=wsrc: en.dma_start(
                   out=ap3(wbufB, 0, [[8192, 128], [1024, 8], [1, 1024]]),
                   in_=ap3(wsrc, 0, [[D, 128], [128 * D, 8], [1, 1024]])))
            if wkey == "l1v":
                for i in range(4):
                    def v1_emit(t, i=i):
                        last = None
                        for c in range(8):
                            for nn in range(2):
                                last = t.matmul(
                                    psB[:, nn * 512:(nn + 1) * 512],
                                    arB4[:, c * 512 + i * 128:c * 512 + (i + 1) * 128],
                                    wbufB[:, c * 1024 + nn * 512:c * 1024 + (nn + 1) * 512],
                                    start=(c == 0), stop=(c == 7))
                        return last
                    OP("T", ("arB4", "wbufB"), ("psB",), v1_emit)
                    OP("S", ("psB",), ("arB1",),
                       lambda s, i=i: s.copy(arB1[:, i * 1024:(i + 1) * 1024], psB[:]))
                OP("QA", ("arB1",), ("v1h",),
                   lambda en, t0=t0: en.dma_start(
                       out=ap3(v1h, t0 * D, [[D, 128], [128 * D, 4], [1, D]]),
                       in_=ap3(arB1, 0, [[4096, 128], [1024, 4], [1, 1024]])))
            else:
                for m in range(8):
                    def kq_emit(t, m=m):
                        last = None
                        for c in range(8):
                            last = t.matmul(psC[:, 0:512],
                                            wbufB[:, c * 1024 + m * 128:c * 1024 + (m + 1) * 128],
                                            arB4[:, c * 512:(c + 1) * 512],
                                            start=(c == 0), stop=(c == 7))
                        return last
                    OP("T", ("arB4", "wbufB"), ("psC",), kq_emit)
                    if wkey == "l1q":
                        OP("S", ("psC",), ("arB1",),
                           lambda s, m=m: s.activation(
                               out=arB1[:, m * 512:(m + 1) * 512], in_=psC[:, 0:512],
                               func=AF.Identity, scale=1.0 / 32.0))
                    else:
                        OP("S", ("psC",), ("arB1",),
                           lambda s, m=m: s.copy(arB1[:, m * 512:(m + 1) * 512], psC[:, 0:512]))
                if wkey == "l1k":
                    OP("QA", ("arB1",), ("k1h",),
                       lambda en, t0=t0: en.dma_start(
                           out=ap3(k1h, t0, [[T, 128], [128 * T, 8], [1, 512]]),
                           in_=ap3(arB1, 0, [[4096, 128], [512, 8], [1, 512]])))
                else:
                    OP("V", ("arB1", "ssel"), ("arB3",),
                       lambda v: v.tensor_scalar(
                           out=ap3(arB3, 0, [[4096, 128], [256, 8], [128, 2], [1, 128]]),
                           in0=ap3(arB1, 0, [[4096, 128], [512, 8], [256, 2], [1, 128]]),
                           scalar1=ssel[:, 0:1], scalar2=None, op0=ALU.mult))
                    OP("V", ("arB1", "ssel", "arB3"), ("arB3",),
                       lambda v: v.scalar_tensor_tensor(
                           out=ap3(arB3, 0, [[4096, 128], [256, 8], [128, 2], [1, 128]]),
                           in0=ap3(arB1, 128, [[4096, 128], [512, 8], [256, 2], [1, 128]]),
                           scalar=ssel[:, 1:2],
                           in1=ap3(arB3, 0, [[4096, 128], [256, 8], [128, 2], [1, 128]]),
                           op0=ALU.mult, op1=ALU.add))
                    OP("QA", ("arB3",), ("qT1h",),
                       lambda en, t0=t0: en.dma_start(
                           out=ap3(qT1h, t0 // 2, [[TOWN, 128], [128 * TOWN, 8], [1, 256]]),
                           in_=ap3(arB3, 0, [[4096, 128], [256, 8], [1, 256]])))
        OP("V", ("arF2", "ssel"), ("arF1",),
           lambda v: v.tensor_scalar(
               out=ap3(arF1, 0, [[2048, 128], [256, 8], [128, 2], [1, 128]]),
               in0=ap3(arF2, 0, [[4096, 128], [512, 8], [256, 2], [1, 128]]),
               scalar1=ssel[:, 0:1], scalar2=None, op0=ALU.mult))
        OP("V", ("arF2", "ssel", "arF1"), ("arF1",),
           lambda v: v.scalar_tensor_tensor(
               out=ap3(arF1, 0, [[2048, 128], [256, 8], [128, 2], [1, 128]]),
               in0=ap3(arF2, 128, [[4096, 128], [512, 8], [256, 2], [1, 128]]),
               scalar=ssel[:, 1:2],
               in1=ap3(arF1, 0, [[2048, 128], [256, 8], [128, 2], [1, 128]]),
               op0=ALU.mult, op1=ALU.add))
        OP("V", ("arB4", "ssel"), ("arB2",),
           lambda v: v.tensor_scalar(
               out=ap3(arB2, 0, [[4096, 128], [256, 8], [128, 2], [1, 128]]),
               in0=ap3(arB4, 0, [[4096, 128], [512, 8], [256, 2], [1, 128]]),
               scalar1=ssel[:, 0:1], scalar2=None, op0=ALU.mult))
        OP("V", ("arB4", "ssel", "arB2"), ("arB2",),
           lambda v: v.scalar_tensor_tensor(
               out=ap3(arB2, 0, [[4096, 128], [256, 8], [128, 2], [1, 128]]),
               in0=ap3(arB4, 128, [[4096, 128], [512, 8], [256, 2], [1, 128]]),
               scalar=ssel[:, 1:2],
               in1=ap3(arB2, 0, [[4096, 128], [256, 8], [128, 2], [1, 128]]),
               op0=ALU.mult, op1=ALU.add))
        OP("QA", ("arF1",), ("x1h",),
           lambda en, t0=t0: en.dma_start(
               out=ap3(x1h, t0 // 2, [[TOWN, 128], [128 * TOWN, 8], [1, 256]]),
               in_=ap3(arF1, 0, [[2048, 128], [256, 8], [1, 256]])))
        OP("QA", ("arB2",), ("xn1h",),
           lambda en, t0=t0: en.dma_start(
               out=ap3(xn1h, t0 // 2, [[TOWN, 128], [128 * TOWN, 8], [1, 256]]),
               in_=ap3(arB2, 0, [[4096, 128], [256, 8], [1, 256]])))
    P.barrier()
    P.flush(block)

    P.mark("F")
    for it in range(4):
        o0 = it * TT
        OP("QS", ("xn1h",), ("arB2",),
           lambda en, o0=o0: en.dma_start(
               out=ap3(arB2, 0, [[4096, 128], [512, 8], [1, 512]]),
               in_=ap3(xn1h, o0, [[TOWN, 128], [128 * TOWN, 8], [1, 512]])))
        OP("QS", ("x1h",), ("arF2",),
           lambda en, o0=o0: en.dma_start(
               out=ap3(arF2, 0, [[4096, 128], [512, 8], [1, 512]]),
               in_=ap3(x1h, o0, [[TOWN, 128], [128 * TOWN, 8], [1, 512]])))
        for qm in range(4):
            OP("QG", ("HB",), ("wbufA",),
               lambda en, qm=qm: en.dma_start(
                   out=ap3(wbufA, 0, [[8192, 128], [1024, 8], [1, 1024]]),
                   in_=ap3(w1, qm * 1024, [[INNER, 128], [128 * INNER, 8], [1, 1024]])))
            OP("QG", ("HB",), ("wbufB",),
               lambda en, qm=qm: en.dma_start(
                   out=ap3(wbufB, 0, [[8192, 128], [1024, 8], [1, 1024]]),
                   in_=ap3(w2, qm * 1024, [[INNER, 128], [128 * INNER, 8], [1, 1024]])))
            for mh in range(8):
                m = qm * 8 + mh
                def f1_emit(t, mh=mh):
                    last = None
                    for c in range(8):
                        last = t.matmul(psC[:, 0:512],
                                        wbufA[:, c * 1024 + mh * 128:c * 1024 + (mh + 1) * 128],
                                        arB2[:, c * 512:(c + 1) * 512],
                                        start=(c == 0), stop=(c == 7))
                    return last
                OP("T", ("arB2", "wbufA"), ("psC",), f1_emit)
                def f2_emit(t, mh=mh):
                    last = None
                    for c in range(8):
                        last = t.matmul(psD[:, 0:512],
                                        wbufB[:, c * 1024 + mh * 128:c * 1024 + (mh + 1) * 128],
                                        arB2[:, c * 512:(c + 1) * 512],
                                        start=(c == 0), stop=(c == 7))
                    return last
                OP("T", ("arB2", "wbufB"), ("psD",), f2_emit)
                OP("S", ("psC", "s_s1"), ("arF1",),
                   lambda s, m=m: s.activation(out=arF1[:, 0:512], in_=psC[:, 0:512],
                                               func=AF.Silu, scale=s_s1[:, m:m + 1]))
                OP("S", ("psD", "s_s2"), ("arF1b",),
                   lambda s, m=m: s.activation(out=arF1[:, 512:1024], in_=psD[:, 0:512],
                                               func=AF.Identity, scale=s_s2[:, m:m + 1]))
                OP("V", ("arF1", "arF1b"), ("hbuf",),
                   lambda v, m=m: v.tensor_tensor(out=hbuf[:, m * 512:(m + 1) * 512],
                                                  in0=arF1[:, 0:512], in1=arF1[:, 512:1024],
                                                  op=ALU.mult))
        for hi, (hsrc, hscale, src_buf, dst_buf, srcb, dstb) in enumerate(
                ((h1, s_sh1, "hbuf", "hbuf2", hbuf, hbuf2),
                 (h2, s_sh2, "hbuf2", "hbuf", hbuf2, hbuf))):
            for g in range(16):
                wb, wbk = (wbufA, "wbufA") if g % 2 == 0 else (wbufB, "wbufB")
                OP("QG", ("HB",), (wbk,),
                   lambda en, g=g, wb=wb, hsrc=hsrc: en.dma_start(
                       out=ap3(wb, 0, [[8192, 128], [256, 32], [1, 256]]),
                       in_=ap3(hsrc, g * 256, [[INNER, 128], [128 * INNER, 32], [1, 256]])))
                for mg in range(2):
                    m = g * 2 + mg
                    def hh_emit(t, mg=mg, wb=wb, srcb=srcb):
                        last = None
                        for k in range(32):
                            last = t.matmul(psC[:, 0:512],
                                            wb[:, k * 256 + mg * 128:k * 256 + (mg + 1) * 128],
                                            srcb[:, k * 512:(k + 1) * 512],
                                            start=(k == 0), stop=(k == 31))
                        return last
                    OP("T", (src_buf, wbk), ("psC",), hh_emit)
                    OP("S", ("psC", "sc"), (dst_buf,),
                       lambda s, m=m, hscale=hscale, dstb=dstb: s.activation(
                           out=dstb[:, m * 512:(m + 1) * 512], in_=psC[:, 0:512],
                           func=AF.Gelu, scale=hscale[:, m:m + 1]))
        for qm in range(4):
            wbuf3, w3k = (wbufA, "wbufA") if qm % 2 == 0 else (wbufB, "wbufB")
            OP("QG", ("HB",), (w3k,),
               lambda en, qm=qm, wbuf3=wbuf3: en.dma_start(
                   out=ap3(wbuf3, 0, [[8192, 128], [256, 32], [1, 256]]),
                   in_=ap3(w3, qm * 256, [[D, 128], [128 * D, 32], [1, 256]])))
            for mh in range(2):
                m = qm * 2 + mh
                def w3_emit(t, mh=mh, wbuf3=wbuf3):
                    last = None
                    for k in range(32):
                        last = t.matmul(psC[:, 0:512],
                                        wbuf3[:, k * 256 + mh * 128:k * 256 + (mh + 1) * 128],
                                        hbuf[:, k * 512:(k + 1) * 512],
                                        start=(k == 0), stop=(k == 31))
                    return last
                OP("T", ("hbuf", w3k), ("psC",), w3_emit)
                OP("S", ("psC", "s_s3"), ("arF1",),
                   lambda s, m=m: s.activation(out=arF1[:, 0:512], in_=psC[:, 0:512],
                                               func=AF.Identity, scale=s_s3[:, m:m + 1]))
                OP("V", ("arF1", "arF2"), ("arF2",),
                   lambda v, m=m: v.tensor_tensor(out=arF2[:, m * 512:(m + 1) * 512],
                                                  in0=arF1[:, 0:512],
                                                  in1=arF2[:, m * 512:(m + 1) * 512],
                                                  op=ALU.add))
        OP("QA", ("arF2",), ("x1h",),
           lambda en, o0=o0: en.dma_start(
               out=ap3(x1h, o0, [[TOWN, 128], [128 * TOWN, 8], [1, 512]]),
               in_=ap3(arF2, 0, [[4096, 128], [512, 8], [1, 512]])))
    P.barrier()
    P.flush(block)

    P.mark("G")
    def nkb_of(j):
        return (j + 2) // 2

    for cc in range(8):
        OP("QG", ("HB",), ("wo1b",),
           lambda g, cc=cc: g.dma_start(out=kv0k[:, cc * 1024:(cc + 1) * 1024],
                                        in_=wo1[cc * 128:(cc + 1) * 128, :]))
    for g4 in range(4):
        o0 = g4 * TT
        OP("QS", ("qT1h",), ("arB2",),
           lambda en, o0=o0: en.dma_start(
               out=ap3(arB2, 0, [[4096, 128], [512, 8], [1, 512]]),
               in_=ap3(qT1h, o0, [[TOWN, 128], [128 * TOWN, 8], [1, 512]])))
        nkb_max = nkb_of(g4 * 4 + 3)
        for kb in range(nkb_max):
            kbuf, kbk = (kblk[0], "kblk0") if kb % 2 == 0 else (kblk[1], "kblk1")
            OP("QS", ("k1h",), (kbk,),
               lambda en, kb=kb, kbuf=kbuf: en.dma_start(
                   out=ap3(wbufA, kbuf.base, [[8192, 128], [512, 8], [1, 512]]),
                   in_=ap3(k1h, kb * 512, [[T, 128], [128 * T, 8], [1, 512]])))
            for j4 in range(4):
                j = g4 * 4 + j4
                if kb >= nkb_of(j):
                    continue
                ps, pk = (psC, "psC") if (kb * 4 + j4) % 2 == 0 else (psD, "psD")
                def ls_emit(t, j4=j4, kbuf=kbuf, ps=ps):
                    last = None
                    for c in range(8):
                        last = t.matmul(ps[:, 0:512],
                                        arB2[:, c * 512 + j4 * 128:c * 512 + (j4 + 1) * 128],
                                        kbuf[:, c * 512:(c + 1) * 512],
                                        start=(c == 0), stop=(c == 7))
                    return last
                OP("T", ("arB2", kbk), (pk,), ls_emit)
                OP("S", (pk,), (f"SgB{j4}",),
                   lambda s, kb=kb, j4=j4, ps=ps: s.copy(
                       Sg[j4][:, kb * 512:(kb + 1) * 512], ps[:, 0:512]))
        for j4 in range(4):
            j = g4 * 4 + j4
            Ec = nkb_of(j) * 512
            OP("QS", ("HB",), ("tb1",),
               lambda en, j=j: en.dma_start(
                   out=tb2[:, 0:512],
                   in_=ap3(m1c, j * 128 * 512, [[512, 128], [1, 512]])))
            OP("V", (f"SgB{j4}", "tb1"), (f"SgB{j4}",),
               lambda v, j4=j4, Ec=Ec: v.tensor_tensor(
                   out=Sg[j4][:, Ec - 512:Ec], in0=Sg[j4][:, Ec - 512:Ec],
                   in1=tb2[:, 0:512], op=ALU.add))
            OP("V", (f"SgB{j4}",), ("mrow",),
               lambda v, j4=j4, Ec=Ec: v.reduce_max(out=mrow[:, j4:j4 + 1],
                                                    in_=Sg[j4][:, 0:Ec], axis=AX.X))
            OP("V", ("mrow",), ("nrow",),
               lambda v, j4=j4: v.tensor_scalar(out=nrow[:, j4:j4 + 1],
                                                in0=mrow[:, j4:j4 + 1],
                                                scalar1=-1.0, scalar2=None, op0=ALU.mult))
            OP("S", (f"SgB{j4}", "nrow"), ("arB1", "lrow"),
               lambda s, j4=j4, Ec=Ec: s.activation(out=arB1[:, 0:Ec],
                                                    in_=Sg[j4][:, 0:Ec], func=AF.Exp,
                                                    bias=nrow[:, j4:j4 + 1],
                                                    accum_out=lrow[:, j4:j4 + 1]))
            OP("V", ("lrow",), ("rlrow",),
               lambda v, j4=j4: v.reciprocal(out=rlrow[:, j4:j4 + 1],
                                             in_=lrow[:, j4:j4 + 1]))
            nkc = Ec // 128
            for gb in range(0, nkc, 8):
                nn = min(8, nkc - gb)
                def pt_emit(t, gb=gb, nn=nn):
                    last = None
                    for k in range(nn):
                        last = t.transpose(psT[:, k * 128:(k + 1) * 128],
                                           arB1[:, (gb + k) * 128:(gb + k + 1) * 128],
                                           ident[:])
                    return last
                OP("T", ("arB1", "ident"), ("psT",), pt_emit)
                OP("S", ("psT",), (f"pTL{j4}",),
                   lambda s, j4=j4, gb=gb, nn=nn: s.copy(
                       pTL[j4][:, gb * 128:(gb + nn) * 128], psT[:, 0:nn * 128]))
        for pair in range(2):
            jlo = g4 * 4 + pair * 2
            jhi = jlo + 1
            nkbp = nkb_of(jhi)
            for kb in range(nkbp):
                vb, vbk = (vblk[0], "vblk0") if kb % 2 == 0 else (vblk[1], "vblk1")
                OP("QS", ("v1h",), (vbk,),
                   lambda en, kb=kb, vb=vb: en.dma_start(
                       out=ap3(wbufB, vb.base, [[8192, 128], [1024, 4], [1, 1024]]),
                       in_=ap3(v1h, kb * 512 * D, [[D, 128], [128 * D, 4], [1, D]])))
                for jj, psv, pvk in ((jlo, psA, "psA"), (jhi, psB, "psB")):
                    j4 = jj - g4 * 4
                    nkc_j = nkb_of(jj) * 4
                    if kb * 4 >= nkc_j:
                        continue
                    def av_emit(t, kb=kb, j4=j4, vb=vb, psv=psv, nkc_j=nkc_j):
                        last = None
                        for kk in range(4):
                            kc = kb * 4 + kk
                            if kc >= nkc_j:
                                break
                            for nn in range(2):
                                last = t.matmul(psv[:, nn * 512:(nn + 1) * 512],
                                                pTL[j4][:, kc * 128:(kc + 1) * 128],
                                                vb[:, kk * 1024 + nn * 512:kk * 1024 + (nn + 1) * 512],
                                                start=(kc == 0), stop=(kc == nkc_j - 1))
                        return last
                    OP("T", (f"pTL{j4}", vbk), (pvk,), av_emit)
            for jj, psv, pvk in ((jlo, psA, "psA"), (jhi, psB, "psB")):
                j4 = jj - g4 * 4
                OP("S", (pvk, "rlrow"), ("arB3",),
                   lambda s, j4=j4, psv=psv: s.activation(out=arB3[:, 0:1024], in_=psv[:],
                                                          func=AF.Identity,
                                                          scale=rlrow[:, j4:j4 + 1]))
                def ot_emit(t):
                    last = None
                    for dd in range(8):
                        last = t.transpose(psT[:, dd * 128:(dd + 1) * 128],
                                           arB3[:, dd * 128:(dd + 1) * 128], ident[:])
                    return last
                OP("T", ("arB3", "ident"), ("psT",), ot_emit)
                OP("S", ("psT",), ("arB4",),
                   lambda s, j4=j4: s.copy(
                       ap3(arB4, j4 * 128, [[4096, 128], [512, 8], [1, 128]]),
                       ap3(psT, 0, [[1024, 128], [128, 8], [1, 128]])))
        OP("QS", ("x1h",), ("arF2",),
           lambda en, o0=o0: en.dma_start(
               out=ap3(arF2, 0, [[4096, 128], [512, 8], [1, 512]]),
               in_=ap3(x1h, o0, [[TOWN, 128], [128 * TOWN, 8], [1, 512]])))
        for m in range(8):
            def wo1_emit(t, m=m):
                last = None
                for c in range(8):
                    last = t.matmul(psC[:, 0:512],
                                    kv0k[:, c * 1024 + m * 128:c * 1024 + (m + 1) * 128],
                                    arB4[:, c * 512:(c + 1) * 512],
                                    start=(c == 0), stop=(c == 7))
                return last
            OP("T", ("arB4", "wo1b"), ("psC",), wo1_emit)
            OP("S", ("psC",), ("arF1",),
               lambda s: s.copy(arF1[:, 0:512], psC[:, 0:512]))
            OP("V", ("arF1", "arF2"), ("arF2",),
               lambda v, m=m: v.tensor_tensor(out=arF2[:, m * 512:(m + 1) * 512],
                                              in0=arF1[:, 0:512],
                                              in1=arF2[:, m * 512:(m + 1) * 512],
                                              op=ALU.add))
        for tc in range(4):
            ps, pk = (psA, "psA") if tc % 2 == 0 else (psB, "psB")
            def ft_emit(t, tc=tc, ps=ps):
                last = None
                for c in range(8):
                    last = t.transpose(ps[:, c * 128:(c + 1) * 128],
                                       arF2[:, c * 512 + tc * 128:c * 512 + (tc + 1) * 128],
                                       idf[:])
                return last
            OP("T", ("arF2", "idf"), (pk,), ft_emit)
            OP("S", (pk,), ("otb",),
               lambda s, ps=ps: s.copy(sqb[:], ps[:]))
            OP("QS", ("otb",), ("yout",),
               lambda en, o0=o0, tc=tc: en.dma_start(
                   out=y_own[o0 + tc * 128:o0 + (tc + 1) * 128, :],
                   in_=sqb[:]))
    P.barrier()
    P.final_emit(block)

    es.close()
    nc.finalize()
    return nc


_NC_CACHE = []


def _host_prep(inputs):
    f32 = np.float32
    bf16 = ml_dtypes.bfloat16

    def c(a, dt=f32):
        return np.ascontiguousarray(np.asarray(a, dtype=dt))

    x = c(inputs["L0_attn_x"])
    HP = [0, 4, 1, 5, 2, 6, 3, 7, 8, 12, 9, 13, 10, 14, 11, 15]
    hperm = np.concatenate([np.arange(h * 64, (h + 1) * 64) for h in HP])
    wq0 = c(np.asarray(inputs["L0_attn_Wq"])[:, hperm])
    wk0 = c(inputs["L0_attn_Wk"])
    wv0 = c(inputs["L0_attn_Wv"])
    wo0 = c(np.asarray(inputs["L0_attn_Wo"])[hperm, :])
    bq0 = c(np.asarray(inputs["L0_attn_bq"])[hperm].reshape(8, 128).T)
    bk0 = c(np.asarray(inputs["L0_attn_bk"]).reshape(2, 128).T)
    bv0c = c(np.asarray(inputs["L0_attn_bv"]).reshape(1, 256))
    bo0 = c(np.asarray(inputs["L0_attn_bo"]).reshape(8, 128).T)
    hw0 = c(np.asarray(inputs["L0_attn_head_weights"])[HP].reshape(1, 16))
    g0 = c(np.asarray(inputs["L0_ffn_w_gate"]).T)
    u0 = c(np.asarray(inputs["L0_ffn_w_up"]).T)
    d0 = c(np.asarray(inputs["L0_ffn_w_down"]).T)
    wq1 = c(np.asarray(inputs["L1_attn_Wq"]).T)
    wk1 = c(np.asarray(inputs["L1_attn_Wk"]).T)
    wv1 = c(np.asarray(inputs["L1_attn_Wv"]).T)
    wo1 = c(np.asarray(inputs["L1_attn_Wo"]).T)
    w1 = c(np.asarray(inputs["L1_ffn_w1_w"]).T)
    w2 = c(np.asarray(inputs["L1_ffn_w2_w"]).T)
    h1 = c(np.asarray(inputs["L1_ffn_h1_w"]).T)
    h2 = c(np.asarray(inputs["L1_ffn_h2_w"]).T)
    w3 = c(np.asarray(inputs["L1_ffn_w3_w"]).T)
    s1c = c(np.asarray(inputs["L1_ffn_w1_s"]).reshape(32, 128).T)
    s2c = c(np.asarray(inputs["L1_ffn_w2_s"]).reshape(32, 128).T)
    sh1c = c(np.asarray(inputs["L1_ffn_h1_s"]).reshape(32, 128).T)
    sh2c = c(np.asarray(inputs["L1_ffn_h2_s"]).reshape(32, 128).T)
    s3c = c(np.asarray(inputs["L1_ffn_w3_s"]).reshape(8, 128).T)

    identc = np.eye(128, dtype=bf16)
    identfc = np.eye(128, dtype=f32)
    nb = HD // 4
    inv_freq = 1.0 / (10000.0 ** (np.arange(nb, dtype=np.float64) / nb))
    tpos = np.arange(T, dtype=np.float64)
    base_cos = np.zeros((64, T), dtype=np.float64)
    base_sin = np.zeros((64, T), dtype=np.float64)
    for dd in range(64):
        ff = (dd % 32) // 2
        ang = tpos * inv_freq[ff]
        base_cos[dd] = np.cos(ang)
        base_sin[dd] = np.sin(ang) * (-1.0 if dd < 32 else 1.0)
    rope_cosc = np.concatenate([base_cos, base_cos], 0).astype(bf16)
    rope_sinc = np.concatenate([base_sin, base_sin], 0).astype(bf16)
    rot2c = np.zeros((128, 128), dtype=bf16)
    for blk in range(2):
        for dd in range(64):
            sdd = blk * 64 + (dd + 32 if dd < 32 else dd - 32)
            rot2c[sdd, blk * 64 + dd] = 1
    m0 = np.zeros((64, 3 * 2048), dtype=f32)
    m0gg = np.zeros((64, 3 * 512), dtype=f32)
    for var, sg in ((0, 0), (1, 1), (2, 63)):
        blockm = np.zeros((64, 128), dtype=f32)
        blockg = np.zeros((64, 32), dtype=f32)
        for q in range(64):
            ii = 64 * sg + q
            for k in range(128):
                jj = 64 * sg - 32 + k
                ok = (0 <= jj < T) and (abs(ii - jj) <= 12 or jj < 8)
                blockm[q, k] = 1.0 if ok else 0.0
            for kk in range(8):
                inwin = (64 * sg - 32 <= kk < 64 * sg + 96)
                blockg[q, kk] = 0.0 if inwin else 1.0
        m0[:, var * 2048:(var + 1) * 2048] = np.tile(blockm, (1, 16))
        m0gg[:, var * 512:(var + 1) * 512] = np.tile(blockg, (1, 16))
    m0c = m0.astype(bf16)
    m0gc = m0gg.astype(bf16)

    y = np.empty((B, T, D), dtype=f32)
    in_maps = []
    shared = dict(wq0=wq0, bq0=bq0, wk0=wk0, bk0=bk0, wv0=wv0, bv0c=bv0c,
                  wo0=wo0, bo0=bo0, hw0=hw0, g0=g0, u0=u0, d0=d0,
                  wq1=wq1, wk1=wk1, wv1=wv1, wo1=wo1,
                  w1=w1, s1c=s1c, w2=w2, s2c=s2c, h1=h1, sh1c=sh1c,
                  h2=h2, sh2c=sh2c, w3=w3, s3c=s3c,
                  identc=identc, identfc=identfc, rope_cosc=rope_cosc,
                  rope_sinc=rope_sinc, rot2c=rot2c, m0c=m0c, m0g=m0gc)
    for core in range(8):
        b = core // 2
        cpar = core % 2
        m1 = np.zeros((NQT * 128, 512), dtype=f32)
        for j in range(NQT):
            Ec = 512 * ((j + 2) // 2)
            for q in range(128):
                ig = 128 * (2 * j + cpar) + q
                keys = Ec - 512 + np.arange(512)
                m1[j * 128 + q] = np.where(keys <= ig, 0.0, -1e9)
        m1c = m1.astype(bf16)
        im = dict(shared)
        im["x_in"] = np.ascontiguousarray(x[b])
        im["m1c"] = m1c
        im["parc"] = np.array([[cpar]], dtype=np.int32)
        sel = np.zeros((128, 2), dtype=f32)
        sel[:, 0] = 1.0 - cpar
        sel[:, 1] = float(cpar)
        im["selc"] = sel
        in_maps.append(im)
    return in_maps, y


_LAST_RES = []


def kernel(**inputs):
    if not _NC_CACHE:
        _NC_CACHE.append(build_nc())
    nc = _NC_CACHE[0]
    in_maps, y = _host_prep(inputs)
    res = run_bass_kernel_spmd(nc, in_maps, core_ids=list(range(8)))
    _LAST_RES.clear()
    _LAST_RES.append(res)
    for core in range(8):
        b = core // 2
        cpar = core % 2
        yo = np.asarray(res.results[core]["y_own"]).reshape(16, 128, D)
        y[b].reshape(32, 128, D)[cpar::2] = yo
    return y


if __name__ == "__main__":
    nc = build_nc()
    print("built ok")


# version 04 (final)
# speedup vs baseline: 1.1045x; 1.0174x over previous
import numpy as np
import ml_dtypes
import concourse.bass as bass
import concourse.bacc as bacc
import concourse.mybir as mybir
from concourse.bass_utils import run_bass_kernel_spmd
from concourse.bass_types import AP as BAP
from contextlib import ExitStack

F32 = mybir.dt.float32
BF16 = mybir.dt.bfloat16
I32 = mybir.dt.int32
AF = mybir.ActivationFunctionType
ALU = mybir.AluOpType
AX = mybir.AxisListType

B, T, D = 4, 4096, 1024
H, HKV, HD = 16, 4, 64
INNER = 4096
TOWN = 2048
NT = 8
TT = 512
NQT = 16
PAD = 32
WTP = T + 2 * PAD
EPS = 1e-5
LN127 = float(np.log(127.0))
RMAGIC = 12582912.0

_TAPS = set()

QKEYS = ("QS", "QA", "QG")
EKEYS = ("T", "V", "S", "G") + QKEYS
STREAM = {"T": "tensor", "V": "vector", "S": "scalar", "G": "gpsimd",
          "QS": "sync", "QA": "scalar", "QG": "gpsimd"}


class Prog:
    LIMIT = None
    MARKS = {}

    def __init__(self, sems):
        self.sems = sems
        self.nops = 0
        self.count = {k: 0 for k in EKEYS}
        self.hwm = {}
        self.last_w = {}
        self.readers = {}
        self.pending = {s: [] for s in ("tensor", "vector", "scalar", "gpsimd", "sync")}
        self.qbounds = {}

    def mark(self, name):
        Prog.MARKS[name] = self.nops

    def op(self, eng, a, b=(), c=(), tag=""):
        self.nops += 1
        if Prog.LIMIT is not None and self.nops > Prog.LIMIT:
            return 0
        if callable(a):
            emit, reads, writes = a, b, c
        else:
            reads, writes, emit = a, b, c
        deps = []
        for b in reads:
            if b in self.last_w:
                deps.append(self.last_w[b])
        for b in writes:
            if b in self.last_w:
                deps.append(self.last_w[b])
            for e2, c2 in self.readers.get(b, {}).items():
                deps.append((e2, c2))
        waits = {}
        for (e2, c2) in deps:
            if c2 <= 0:
                continue
            if e2 in QKEYS:
                if eng == e2:
                    continue
                c2 = self.count[e2]
            if self.hwm.get((eng, e2), 0) < c2:
                waits[e2] = max(waits.get(e2, 0), c2)
        for e2, c2 in waits.items():
            self.hwm[(eng, e2)] = c2
            if e2 in QKEYS:
                self.qbounds.setdefault(e2, set()).add(c2)
        inc = 16 if eng in QKEYS else 1
        self.count[eng] += inc
        cnt = self.count[eng]
        self.pending[STREAM[eng]].append((waits, emit, self.sems[eng], inc, tag))
        for b in reads:
            self.readers.setdefault(b, {})[eng] = cnt
        for b in writes:
            self.last_w[b] = (eng, cnt)
            self.readers[b] = {}
        return cnt

    def barrier(self):
        snap = dict(self.count)
        for eng in ("T", "V", "S", "G", "QS"):
            waits = {}
            for e2, c2 in snap.items():
                if e2 != eng and c2 > 0 and self.hwm.get((eng, e2), 0) < c2:
                    waits[e2] = c2
            for e2, c2 in waits.items():
                self.hwm[(eng, e2)] = c2
                if e2 in QKEYS:
                    self.qbounds.setdefault(e2, set()).add(c2)
            if waits:
                self.pending[STREAM[eng]].append((waits, None, None, 0, "bar"))

    def flush(self, block):
        pass

    def final_emit(self, block):
        qsems = {"QS": "sync", "QA": "scalar", "QG": "gpsimd"}
        for qk, stream in qsems.items():
            need = self.qbounds.get(qk, set())
            if not need:
                continue
            sem = self.sems[qk]
            out = []
            cum = 0
            for entry in self.pending[stream]:
                out.append(entry)
                if entry[2] is sem:
                    cum += entry[3]
                    if cum in need:
                        out.append(({qk: cum}, None, None, 0, "qb"))
            self.pending[stream] = out
        for stream in ("sync", "gpsimd", "scalar", "vector", "tensor"):
            lst = self.pending[stream]
            if not lst:
                continue
            snapshot = list(lst)
            lst.clear()
            sems = self.sems

            def run(en, snapshot=snapshot, sems=sems):
                for waits, emit, sem, inc, tag in snapshot:
                    for e2, c2 in waits.items():
                        en.wait_ge(sems[e2], c2)
                    if emit is not None:
                        emit(en).then_inc(sem, inc)
            getattr(block, stream)(run)


def ap3(t, offset, dims):
    h = t.tensor if hasattr(t, "tensor") else t
    return BAP(tensor=h, offset=offset, ap=[list(d) for d in dims])


def build_nc():
    nc = bacc.Bacc("TRN2", target_bir_lowering=False)

    def din(name, shape, dt=F32):
        return nc.dram_tensor(name, shape, dt, kind="ExternalInput").ap()

    def hbm(name, shape, dt):
        kind = "ExternalOutput" if name in _TAPS else "Internal"
        return nc.dram_tensor(name, shape, dt, kind=kind).ap()

    x_in = din("x_in", [T, D])
    wq0 = din("wq0", [D, D]); bq0 = din("bq0", [128, 8])
    wk0 = din("wk0", [D, 256]); bk0 = din("bk0", [128, 2])
    wv0 = din("wv0", [D, 256]); bv0c = din("bv0c", [1, 256])
    wo0 = din("wo0", [D, D]); bo0 = din("bo0", [128, 8])
    hw0 = din("hw0", [1, 16])
    g0 = din("g0", [D, INNER]); u0 = din("u0", [D, INNER]); d0 = din("d0", [INNER, D])
    wq1 = din("wq1", [D, D]); wk1 = din("wk1", [D, D])
    wv1 = din("wv1", [D, D]); wo1 = din("wo1", [D, D])
    w1 = din("w1", [D, INNER]); s1c = din("s1c", [128, 32])
    w2 = din("w2", [D, INNER]); s2c = din("s2c", [128, 32])
    h1 = din("h1", [INNER, INNER]); sh1c = din("sh1c", [128, 32])
    h2 = din("h2", [INNER, INNER]); sh2c = din("sh2c", [128, 32])
    w3 = din("w3", [INNER, D]); s3c = din("s3c", [128, 8])
    identc = din("identc", [128, 128], BF16)
    identfc = din("identfc", [128, 128], F32)
    rope_cosc = din("rope_cosc", [128, T], BF16)
    rope_sinc = din("rope_sinc", [128, T], BF16)
    rot2c = din("rot2c", [128, 128], BF16)
    m0c = din("m0c", [64, 3 * 2048], BF16)
    m0g = din("m0g", [64, 3 * 512], BF16)
    m1c = din("m1c", [NQT * 128, 512], BF16)
    parc = din("parc", [1, 1], mybir.dt.int32)
    selc = din("selc", [128, 2])

    y_own = nc.dram_tensor("y_own", [TOWN, D], F32, kind="ExternalOutput").ap()

    g0q = hbm("g0q", [D, INNER], BF16)
    u0q = hbm("u0q", [D, INNER], BF16)
    d0q = hbm("d0q", [INNER, D], BF16)
    qT0h = hbm("qT0h", [8 * 128, T], BF16)
    xqh = hbm("xqh", [8 * 128, T], BF16)
    aoth = hbm("aoth", [8 * 128, T], BF16)
    xTh = hbm("xTh", [8 * 128, T], F32)
    x1h = hbm("x1h", [8 * 128, TOWN], F32)
    xn1h = hbm("xn1h", [8 * 128, TOWN], BF16)
    qT1h = hbm("qT1h", [8 * 128, TOWN], BF16)
    k1h = hbm("k1h", [8 * 128, T], BF16)
    v1h = hbm("v1h", [T, D], BF16)

    es = ExitStack()
    e = es.enter_context

    def sbf(name, shape, dt=F32):
        return e(nc.sbuf_tensor(name, shape, dt))

    kv0k = sbf("kv0k", [128, 2 * WTP], BF16)
    v0t = sbf("v0t", [128, 32 * 256], BF16)
    ident = sbf("ident", [128, 128], BF16)
    idf = sbf("idf", [128, 128], F32)
    srot2 = sbf("srot2", [128, 128], BF16)
    sbq0 = sbf("sbq0", [128, 8]); sbk0 = sbf("sbk0", [128, 2]); sbo0 = sbf("sbo0", [128, 8])
    bvb = sbf("bvb", [1, 256], BF16)
    shw = sbf("shw", [1, 16]); shw8 = sbf("shw8", [128, 8]); sbqhw = sbf("sbqhw", [128, 8])
    hwst = sbf("hwst", [32, 16])
    s_s1 = sbf("s_s1", [128, 32]); s_s2 = sbf("s_s2", [128, 32])
    s_sh1 = sbf("s_sh1", [128, 32]); s_sh2 = sbf("s_sh2", [128, 32])
    s_s3 = sbf("s_s3", [128, 8])
    ones1f = sbf("ones1f", [128, 1], F32)
    onesrf = sbf("onesrf", [1, 128], F32)
    onesrb = sbf("onesrb", [1, 128], BF16)
    swsc = sbf("swsc", [128, 4])
    swrec = sbf("swrec", [128, 4])
    sCneg = sbf("sCneg", [64, 1])
    pmask = sbf("pmask", [128, 2])
    rowA = sbf("rowA", [1, 640]); rowB = sbf("rowB", [1, 640]); rowC = sbf("rowC", [1, 640])
    wbufA = sbf("wbufA", [128, 8192], BF16)
    wbufB = sbf("wbufB", [128, 8192], BF16)
    arF1 = sbf("arF1", [128, 2048], F32)
    arF2 = sbf("arF2", [128, 4096], F32)
    sqb = sbf("sqb", [128, 1024], F32)
    amaxb = sbf("amaxb", [128, 1024], F32)
    mb_s = amaxb[:, 0:512]
    rb_s = amaxb[:, 512:1024]
    sxb_s = amaxb[:, 0:512]
    rxb_s = amaxb[:, 512:1024]
    arB1 = sbf("arB1", [128, 4096], BF16)
    arB2 = sbf("arB2", [128, 4096], BF16)
    arB3 = sbf("arB3", [128, 4096], BF16)
    arB4 = sbf("arB4", [128, 4096], BF16)
    hbuf = sbf("hbuf", [128, 16384], BF16)
    hbuf2 = sbf("hbuf2", [128, 16384], BF16)
    i32b = sbf("i32b", [128, 512], I32)
    tb1 = sbf("tb1", [128, 512], BF16)
    tb2 = tb1
    l0_s = sbf("l0_s", [64, 16]); r0_s = sbf("r0_s", [64, 16])
    cl127 = sbf("cl127", [128, 1])
    gq_s = sbf("gq_s", [128, 64], BF16)
    class _Slc:
        def __init__(self, parent, base):
            self.parent = parent
            self.base = base

        def __getitem__(self, idx):
            ps, fs = idx
            assert ps == slice(None)
            return self.parent[:, self.base + fs.start:self.base + fs.stop]

    Sg = [_Slc(hbuf, i * 4096) for i in range(4)]
    pTL = [_Slc(hbuf2, i * 4096) for i in range(4)]
    kblk = [_Slc(wbufA, i * 4096) for i in range(2)]
    vblk = [_Slc(wbufB, i * 4096) for i in range(2)]
    mrow = sbf("mrow", [128, 4]); nrow = sbf("nrow", [128, 4])
    lrow = sbf("lrow", [128, 4]); rlrow = sbf("rlrow", [128, 4])
    wo1_s = kv0k[:, 0:8192]
    winD = _Slc(wbufA, 0)
    p0c = _Slc(wbufA, 2208)
    pT0c = _Slc(wbufA, 4256)
    vwinc = _Slc(wbufA, 5280)
    rcos = hbuf2[:, 0:4096]
    rsin = hbuf2[:, 4096:8192]
    sm0 = hbuf2[0:64, 8192:14336]
    sm0g = hbuf2[0:64, 14336:15872]

    psA = e(nc.psum_tensor("psA", [128, 1024], F32))
    psB = e(nc.psum_tensor("psB", [128, 1024], F32))
    psC = e(nc.psum_tensor("psC", [128, 512], F32))
    psD = e(nc.psum_tensor("psD", [128, 512], F32))
    psE = e(nc.psum_tensor("psE", [128, 512], F32))
    psT = e(nc.psum_tensor("psT", [128, 1024], BF16))

    sems = {k: e(nc.semaphore(f"sem_{k}")) for k in EKEYS}
    block = e(nc.Block())
    P = Prog(sems)
    OP = P.op
    ssel = sbf("ssel", [128, 2])
    OP("QS", ("HB",), ("ssel",), lambda en: en.dma_start(out=ssel[:], in_=selc))

    def ld(buf, dst, src, q="QS"):
        OP(q, lambda en, d=dst, s=src: en.dma_start(out=d, in_=s), ("HB",), (buf,))

    OP("V", (), ("kv0k",), lambda v: v.memset(kv0k[:], 0.0))
    OP("V", (), ("v0t",), lambda v: v.memset(v0t[:], 0.0))
    for _nm, _t in (("arF1", arF1), ("arF2", arF2), ("sqb", sqb), ("amaxb", amaxb),
                    ("arB1", arB1), ("arB2", arB2), ("arB3", arB3), ("arB4", arB4),
                    ("i32b", i32b), ("tb1", tb1), ("rowA", rowA), ("rowB", rowB),
                    ("rowC", rowC), ("l0_s", l0_s), ("r0_s", r0_s), ("gq_s", gq_s),
                    ("mrow", mrow), ("nrow", nrow), ("lrow", lrow), ("rlrow", rlrow),
                    ("swsc", swsc), ("swrec", swrec), ("sCneg", sCneg), ("shw8", shw8)):
        OP("V", (), (_nm,), lambda v, _t=_t: v.memset(_t[:], 0.0))
    for _nm, _t in (("wbufA", wbufA), ("wbufB", wbufB), ("hbuf", hbuf), ("hbuf2", hbuf2)):
        OP("G", (), (_nm, "wq0b", "wk0b", "wv0b", "wo0b", "hbuf2w", "hbuf2x", "p0w", "p0g",
                     "vov", "pT0w", "pT0g", "vg4", "zebE", "zebO",
                     "SgB0", "SgB1", "SgB2", "SgB3",
                     "pTL0", "pTL1", "pTL2", "pTL3", "kblk0", "kblk1", "vblk0", "vblk1",
                     "rcos", "rsin", "sm0", "sm0g", "wo1b"),
           lambda g, _t=_t: g.memset(_t[:], 0.0))

    ld("ident", ident[:], identc); ld("idf", idf[:], identfc)
    ld("rcos", rcos, rope_cosc); ld("rsin", rsin, rope_sinc)
    ld("srot2", srot2[:], rot2c); ld("sm0", sm0, m0c)
    ld("sm0g", sm0g, m0g)
    ld("sbq0", sbq0[:], bq0); ld("sbk0", sbk0[:], bk0); ld("sbo0", sbo0[:], bo0)
    ld("rowB", rowB[0:1, 0:256], bv0c); ld("shw", shw[:], hw0)
    ld("s_s1", s_s1[:], s1c); ld("s_s2", s_s2[:], s2c)
    ld("s_sh1", s_sh1[:], sh1c); ld("s_sh2", s_sh2[:], sh2c); ld("s_s3", s_s3[:], s3c)

    OP("V", (), ("ones",), lambda v: v.memset(ones1f[:], 1.0))
    OP("V", (), ("ones",), lambda v: v.memset(onesrf[:], 1.0))
    OP("V", (), ("ones",), lambda v: v.memset(onesrb[:], 1.0))
    OP("V", (), ("cl127",), lambda v: v.memset(cl127[:], LN127))
    OP("V", (), ("pmask",), lambda v: v.memset(pmask[:], 0.0))
    OP("V", ("pmask",), ("pmask",), lambda v: v.memset(pmask[0:64, 0:1], 1.0))
    OP("V", ("pmask",), ("pmask",), lambda v: v.memset(pmask[64:96, 1:2], 1.0))
    OP("V", ("pmask",), ("pmask",), lambda v: v.memset(pmask[96:128, 1:2], 1.0))
    OP("V", ("rowB",), ("bvb",), lambda v: v.tensor_copy(bvb[:], rowB[0:1, 0:256]))
    for hh in range(16):
        OP("G", ("shw",), ("hwst",),
           lambda g, hh=hh: g.partition_broadcast(
               hwst[0:32, hh:hh + 1], shw[0:1, hh:hh + 1], channels=32))
    for rg in range(4):
        par = rg // 2
        OP("V", ("hwst",), ("shw8",),
           lambda v, rg=rg, par=par: v.tensor_copy(
               shw8[rg * 32:(rg + 1) * 32, 0:8],
               ap3(hwst, par, [[16, 32], [2, 8]])))
    OP("V", ("shw8",), ("shw8",),
       lambda v: v.tensor_scalar(out=shw8[:], in0=shw8[:], scalar1=0.125,
                                 scalar2=None, op0=ALU.mult))
    OP("V", ("shw8", "sbq0"), ("sbqhw",),
       lambda v: v.tensor_tensor(out=sbqhw[:], in0=sbq0[:], in1=shw8[:], op=ALU.mult))
    for cc in range(8):
        OP("QG", ("HB",), ("wq0b",),
           lambda g, cc=cc: g.dma_start(out=wbufA[:, cc * 1024:(cc + 1) * 1024],
                                        in_=wq0[cc * 128:(cc + 1) * 128, :]))
        OP("QG", ("HB",), ("wk0b",),
           lambda g, cc=cc: g.dma_start(out=hbuf[:, cc * 256:(cc + 1) * 256],
                                        in_=wk0[cc * 128:(cc + 1) * 128, :]))
        OP("QG", ("HB",), ("wv0b",),
           lambda g, cc=cc: g.dma_start(out=hbuf[:, 2048 + cc * 256:2048 + (cc + 1) * 256],
                                        in_=wv0[cc * 128:(cc + 1) * 128, :]))
        OP("QG", ("HB",), ("wo0b",),
           lambda g, cc=cc: g.dma_start(out=hbuf[:, 4096 + cc * 1024:4096 + (cc + 1) * 1024],
                                        in_=wo0[cc * 128:(cc + 1) * 128, :]))

    P.mark("W")
    def wq_stats(src, rows, cols, ci):
        nch = rows // 128
        npk = (cols + 2047) // 2048
        for k in range(nch):
            for hh2 in range(0, cols, 2048):
                nn = min(2048, cols - hh2)
                OP("QS", ("HB",), ("arF1",),
                   lambda en, k=k, src=src, hh2=hh2, nn=nn: en.dma_start(
                       out=arF1[:, 0:nn], in_=src[k * 128:(k + 1) * 128, hh2:hh2 + nn]))
                OP("V", ("arF1",), ("arF2",),
                   lambda v, k=k, hh2=hh2, nn=nn, npk=npk: v.reduce_sum(
                       out=arF2[:, npk * k + hh2 // 2048:npk * k + hh2 // 2048 + 1],
                       in_=arF1[:, 0:nn], axis=AX.X, apply_absolute_value=True))
        OP("V", ("arF2",), ("arF2",),
           lambda v, n=nch * npk: v.reduce_sum(out=arF2[:, 80:81], in_=arF2[:, 0:n],
                                               axis=AX.X))
        OP("T", ("arF2", "ones"), ("psE",),
           lambda t: t.matmul(psE[0:1, 0:1], ones1f[:], arF2[:, 80:81],
                              start=True, stop=True))
        OP("S", ("psE",), ("swsc",),
           lambda s, ci=ci, n=rows * cols: s.activation(
               out=swsc[0:1, ci:ci + 1], in_=psE[0:1, 0:1], func=AF.Identity, scale=1.0 / n))
        OP("V", ("swsc",), ("swsc",),
           lambda v, ci=ci: v.tensor_scalar(out=swsc[0:1, ci:ci + 1],
                                            in0=swsc[0:1, ci:ci + 1],
                                            scalar1=1e-5, scalar2=None, op0=ALU.max))
        OP("G", ("swsc",), ("swsc",),
           lambda g, ci=ci: g.partition_broadcast(swsc[:, ci:ci + 1],
                                                  swsc[0:1, ci:ci + 1], channels=128))

    wq_stats(g0, D, INNER, 0)
    wq_stats(u0, D, INNER, 1)
    wq_stats(d0, INNER, D, 2)
    OP("V", ("swsc",), ("swrec",), lambda v: v.reciprocal(out=swrec[:, 0:3], in_=swsc[:, 0:3]))

    def wq_quant(src, dst, rows, cols, ci):
        nch = rows // 128
        for k in range(nch):
            for hh2 in range(0, cols, 2048):
                nnh = min(2048, cols - hh2)
                OP("QS", ("HB",), ("arF1",),
                   lambda en, k=k, src=src, hh2=hh2, nnh=nnh: en.dma_start(
                       out=arF1[:, 0:nnh], in_=src[k * 128:(k + 1) * 128, hh2:hh2 + nnh]))
                for sc0 in range(0, nnh, 512):
                    OP("V", ("arF1", "swrec"), ("sqb",),
                       lambda v, ci=ci, sc0=sc0: v.tensor_scalar(
                           out=sqb[:, 0:512], in0=arF1[:, sc0:sc0 + 512],
                           scalar1=swrec[:, ci:ci + 1], scalar2=1.0,
                           op0=ALU.mult, op1=ALU.min))
                    OP("V", ("sqb",), ("sqb",),
                       lambda v: v.tensor_scalar(out=sqb[:, 512:1024], in0=sqb[:, 0:512],
                                                 scalar1=-1.0, scalar2=RMAGIC,
                                                 op0=ALU.max, op1=ALU.add))
                    OP("V", ("sqb",), ("arB1",),
                       lambda v, hh2=hh2, sc0=sc0: v.tensor_scalar(
                           out=arB1[:, hh2 + sc0:hh2 + sc0 + 512], in0=sqb[:, 512:1024],
                           scalar1=-RMAGIC, scalar2=None, op0=ALU.add))
            OP("QG", ("arB1",), ("HQW",),
               lambda g, k=k, dst=dst, cols=cols: g.dma_start(
                   out=dst[k * 128:(k + 1) * 128, :], in_=arB1[:, 0:cols]))

    wq_quant(g0, g0q, D, INNER, 0)
    wq_quant(u0, u0q, D, INNER, 1)
    wq_quant(d0, d0q, INNER, D, 2)
    P.barrier()
    P.flush(block)

    def layernorm(xf_buf, xf, sq_buf, sq, xn_buf, xn):
        def st_emit(t):
            last = None
            for c in range(8):
                last = t.matmul(psC[0:1, 0:512], ones1f[:], xf[:, c * 512:(c + 1) * 512],
                                start=(c == 0), stop=(c == 7))
            return last
        OP("T", (xf_buf, "ones"), ("psC",), st_emit)
        for c in range(8):
            sl = (c % 2) * 512
            OP("S", (xf_buf,), (f"sqs{c % 2}",),
               lambda s, c=c, sl=sl: s.activation(out=sqb[:, sl:sl + 512],
                                                  in_=xf[:, c * 512:(c + 1) * 512],
                                                  func=AF.Square))
            OP("T", (f"sqs{c % 2}", "ones"), ("psD",),
               lambda t, c=c, sl=sl: t.matmul(psD[0:1, 0:512], ones1f[:],
                                              sqb[:, sl:sl + 512],
                                              start=(c == 0), stop=(c == 7)))
        OP("S", ("psC",), ("rowA",),
           lambda s: s.activation(out=rowA[0:1, 0:512], in_=psC[0:1, 0:512],
                                  func=AF.Identity, scale=1.0 / D))
        OP("S", ("psD",), ("rowB",),
           lambda s: s.activation(out=rowB[0:1, 0:512], in_=psD[0:1, 0:512],
                                  func=AF.Identity, scale=1.0 / D))
        OP("V", ("rowA",), ("rowC",),
           lambda v: v.tensor_tensor(out=rowC[0:1, 0:512], in0=rowA[0:1, 0:512],
                                     in1=rowA[0:1, 0:512], op=ALU.mult))
        OP("V", ("rowB", "rowC"), ("rowC",),
           lambda v: v.tensor_tensor(out=rowC[0:1, 0:512], in0=rowB[0:1, 0:512],
                                     in1=rowC[0:1, 0:512], op=ALU.subtract))
        OP("V", ("rowC",), ("rowC",),
           lambda v: v.tensor_scalar(out=rowC[0:1, 0:512], in0=rowC[0:1, 0:512],
                                     scalar1=EPS, scalar2=None, op0=ALU.add))
        OP("S", ("rowC",), ("rowC",),
           lambda s: s.activation(out=rowC[0:1, 0:512], in_=rowC[0:1, 0:512], func=AF.Ln))
        OP("S", ("rowC",), ("rowC",),
           lambda s: s.activation(out=rowC[0:1, 0:512], in_=rowC[0:1, 0:512],
                                  func=AF.Exp, scale=-0.5))
        OP("T", ("rowA", "ones"), ("psC",),
           lambda t: t.matmul(psC[:, 0:512], onesrf[:], rowA[0:1, 0:512],
                              start=True, stop=True))
        OP("S", ("psC",), ("amaxb",), lambda s: s.copy(amaxb[:, 0:512], psC[:, 0:512]))
        OP("T", ("rowC", "ones"), ("psD",),
           lambda t: t.matmul(psD[:, 0:512], onesrf[:], rowC[0:1, 0:512],
                              start=True, stop=True))
        OP("S", ("psD",), ("amaxb",), lambda s: s.copy(amaxb[:, 512:1024], psD[:, 0:512]))
        for c in range(8):
            eng = "V" if c % 4 != 3 else "G"
            sl = (c % 2) * 512
            OP(eng, (xf_buf, "amaxb"), (f"sqs{c % 2}",),
               lambda v, c=c, sl=sl: v.tensor_tensor(out=sqb[:, sl:sl + 512],
                                                     in0=xf[:, c * 512:(c + 1) * 512],
                                                     in1=amaxb[:, 0:512], op=ALU.subtract))
            OP(eng, (f"sqs{c % 2}", "amaxb"), (xn_buf,),
               lambda v, c=c, sl=sl: v.tensor_tensor(out=xn[:, c * 512:(c + 1) * 512],
                                                     in0=sqb[:, sl:sl + 512],
                                                     in1=amaxb[:, 512:1024], op=ALU.mult))

    def rope(src_buf, src, dst_buf, dst, t0, n):
        OP("T", (src_buf, "srot2"), ("psB",),
           lambda t: t.matmul(psB[:, 0:n], srot2[:], src, start=True, stop=True))
        OP("S", ("psB",), ("tb1",), lambda s: s.copy(tb1[:, 0:n], psB[:, 0:n]))
        OP("V", (src_buf, "rcos"), (src_buf,),
           lambda v: v.tensor_tensor(out=src, in0=src, in1=hbuf2[:, t0:t0 + n], op=ALU.mult))
        OP("V", ("tb1", "rsin"), ("tb1",),
           lambda v: v.tensor_tensor(out=tb1[:, 0:n], in0=tb1[:, 0:n],
                                     in1=hbuf2[:, 4096 + t0:4096 + t0 + n], op=ALU.mult))
        OP("V", (src_buf, "tb1"), (dst_buf,),
           lambda v: v.tensor_tensor(out=dst, in0=src, in1=tb1[:, 0:n], op=ALU.add))

    P.mark("A")
    for tau in range(NT):
        t0 = tau * TT
        for hh2 in range(2):
            OP("QS", ("HB",), ("arF1",),
               lambda en, t0=t0, hh2=hh2: en.dma_start(
                   out=arF1[:],
                   in_=ap3(x_in, (t0 + hh2 * 256) * D, [[D, 128], [128 * D, 2], [1, D]])))
            for i2 in range(2):
                i = hh2 * 2 + i2
                ps, pk = (psA, "psA") if i % 2 == 0 else (psB, "psB")
                def tr_emit(t, i2=i2, ps=ps):
                    last = None
                    for c in range(8):
                        last = t.transpose(ps[:, c * 128:(c + 1) * 128],
                                           arF1[:, i2 * 1024 + c * 128:i2 * 1024 + (c + 1) * 128],
                                           idf[:])
                    return last
                OP("T", ("arF1", "idf"), (pk,), tr_emit)
                OP("S", (pk,), ("arF2",),
                   lambda s, i=i, ps=ps: s.copy(
                       ap3(arF2, i * 128, [[4096, 128], [512, 8], [1, 128]]), ps[:]))
        OP("QA", ("arF2",), ("xTh",),
           lambda en, t0=t0: en.dma_start(
               out=ap3(xTh, t0, [[T, 128], [128 * T, 8], [1, 512]]),
               in_=ap3(arF2, 0, [[4096, 128], [512, 8], [1, 512]])))
        layernorm("arF2", arF2, None, None, "arB2", arB2)
        for m2 in range(2):
            def kp_emit(t, m2=m2):
                last = None
                for c in range(8):
                    last = t.matmul(psC[:, 0:512],
                                    hbuf[:, c * 256 + m2 * 128:c * 256 + (m2 + 1) * 128],
                                    arB2[:, c * 512:(c + 1) * 512],
                                    start=(c == 0), stop=(c == 7))
                return last
            OP("T", ("arB2", "wk0b"), ("psC",), kp_emit)
            OP("S", ("psC", "sbk0"), ("arB3",),
               lambda s, m2=m2: s.activation(out=arB3[:, 0:512], in_=psC[:, 0:512],
                                             func=AF.Identity, bias=sbk0[:, m2:m2 + 1]))
            rope("arB3", arB3[:, 0:512], "kv0k",
                 kv0k[:, m2 * WTP + PAD + t0:m2 * WTP + PAD + t0 + 512], t0, 512)
        for i in range(4):
            def vp_emit(t, i=i):
                for c in range(8):
                    t.matmul(psD[:, 0:256],
                             arB2[:, c * 512 + i * 128:c * 512 + (i + 1) * 128],
                             hbuf[:, 2048 + c * 256:2048 + (c + 1) * 256],
                             start=(c == 0), stop=False)
                return t.matmul(psD[:, 0:256], onesrb[:, 0:128], bvb[:],
                                start=False, stop=True)
            OP("T", ("arB2", "wv0b", "bvb", "ones"), ("psD",), vp_emit)
            OP("S", ("psD",), ("v0t",),
               lambda s, tau=tau, i=i: s.copy(
                   v0t[:, (tau * 4 + i) * 256:(tau * 4 + i + 1) * 256], psD[:, 0:256]))
        for m in range(8):
            def qp_emit(t, m=m):
                last = None
                for c in range(8):
                    last = t.matmul(psC[:, 0:512],
                                    wbufA[:, c * 1024 + m * 128:c * 1024 + (m + 1) * 128],
                                    arB2[:, c * 512:(c + 1) * 512],
                                    start=(c == 0), stop=(c == 7))
                return last
            OP("T", ("arB2", "wq0b"), ("psC",), qp_emit)
            OP("S", ("psC", "sbqhw", "shw8"), ("arB3",),
               lambda s, m=m: s.activation(out=arB3[:, 0:512], in_=psC[:, 0:512],
                                           func=AF.Identity, bias=sbqhw[:, m:m + 1],
                                           scale=shw8[:, m:m + 1]))
            rope("arB3", arB3[:, 0:512], "arB1", arB1[:, 0:512], t0, 512)
            OP("QA", ("arB1",), ("qT0h",),
               lambda en, m=m, t0=t0: en.dma_start(
                   out=ap3(qT0h, m * 128 * T + t0, [[T, 128], [1, 512]]),
                   in_=arB1[:, 0:512]))
        for c in range(8):
            sl = (c % 2) * 512
            OP("S", ("arB2",), (f"sqs{c % 2}",),
               lambda s, c=c, sl=sl: s.activation(out=sqb[:, sl:sl + 512],
                                                  in_=arB2[:, c * 512:(c + 1) * 512],
                                                  func=AF.Abs))
            if c == 0:
                OP("V", ("sqs0",), ("amaxb",),
                   lambda v: v.tensor_copy(amaxb[:, 0:512], sqb[:, 0:512]))
            else:
                OP("V", (f"sqs{c % 2}", "amaxb"), ("amaxb",),
                   lambda v, sl=sl: v.tensor_tensor(out=amaxb[:, 0:512],
                                                    in0=amaxb[:, 0:512],
                                                    in1=sqb[:, sl:sl + 512], op=ALU.max))
        for lev in (64, 32):
            for rp in range(0, lev, 32):
                OP("V", ("amaxb",), ("sqb",),
                   lambda v, lev=lev, rp=rp: v.tensor_copy(
                       sqb[rp:rp + 32, 0:512], amaxb[lev + rp:lev + rp + 32, 0:512]))
            OP("V", ("amaxb", "sqb"), ("amaxb",),
               lambda v, lev=lev: v.tensor_tensor(out=amaxb[0:lev, 0:512],
                                                  in0=amaxb[0:lev, 0:512],
                                                  in1=sqb[0:lev, 0:512], op=ALU.max))
        OP("V", ("amaxb",), ("amaxc",),
           lambda v: v.tensor_copy(sqb[0:32, 512:1024], amaxb[0:32, 0:512]))
        def amt_emit(t):
            last = None
            for p in range(4):
                last = t.transpose(psC[:, p * 32:(p + 1) * 32],
                                   sqb[0:32, 512 + p * 128:512 + (p + 1) * 128],
                                   idf[0:32, 0:32])
            return last
        OP("T", ("amaxc", "idf"), ("psC",), amt_emit)
        OP("S", ("psC",), ("sqb",), lambda s: s.copy(sqb[:, 0:128], psC[:, 0:128]))
        OP("V", ("sqb",), ("amaxt",),
           lambda v: v.reduce_max(out=amaxb[:, 0:4],
                                  in_=ap3(sqb, 0, [[1024, 128], [32, 4], [1, 32]]),
                                  axis=AX.X))
        OP("V", ("amaxt",), ("amaxt",),
           lambda v: v.tensor_scalar(out=amaxb[:, 0:4], in0=amaxb[:, 0:4],
                                     scalar1=1e-5, scalar2=None, op0=ALU.max))
        OP("S", ("amaxt",), ("amaxt2",),
           lambda s: s.activation(out=amaxb[:, 4:8], in_=amaxb[:, 0:4], func=AF.Ln))
        OP("S", ("amaxt2",), ("amaxt2",),
           lambda s: s.activation(out=amaxb[:, 4:8], in_=amaxb[:, 4:8],
                                  func=AF.Exp, scale=-1.0, bias=cl127[:, 0:1]))
        OP("V", ("amaxt",), ("amaxt3",),
           lambda v: v.tensor_scalar(out=amaxb[:, 8:12], in0=amaxb[:, 0:4],
                                     scalar1=1.0 / 127.0, scalar2=None, op0=ALU.mult))
        def rowt_emit(t):
            last = None
            for p in range(4):
                last = t.transpose(psC[0:1, p * 128:(p + 1) * 128],
                                   amaxb[:, 4 + p:5 + p], idf[:])
                last = t.transpose(psD[0:1, p * 128:(p + 1) * 128],
                                   amaxb[:, 8 + p:9 + p], idf[:])
            return last
        OP("T", ("amaxt2", "amaxt3", "idf"), ("psC", "psD"), rowt_emit)
        OP("S", ("psC",), ("rowB",), lambda s: s.copy(rowB[0:1, 0:512], psC[0:1, 0:512]))
        OP("S", ("psD",), ("rowC",), lambda s: s.copy(rowC[0:1, 0:512], psD[0:1, 0:512]))
        OP("T", ("rowB", "ones"), ("psC",),
           lambda t: t.matmul(psC[:, 0:512], onesrf[:], rowB[0:1, 0:512],
                              start=True, stop=True))
        OP("S", ("psC",), ("amaxb",), lambda s: s.copy(amaxb[:, 0:512], psC[:, 0:512]))
        OP("T", ("rowC", "ones"), ("psD",),
           lambda t: t.matmul(psD[:, 0:512], onesrf[:], rowC[0:1, 0:512],
                              start=True, stop=True))
        OP("S", ("psD",), ("amaxb",), lambda s: s.copy(amaxb[:, 512:1024], psD[:, 0:512]))
        for c in range(8):
            OP("V", ("arB2", "amaxb"), ("sqb",),
               lambda v, c=c: v.tensor_tensor(out=sqb[:, 0:512],
                                              in0=arB2[:, c * 512:(c + 1) * 512],
                                              in1=amaxb[:, 0:512], op=ALU.mult))
            OP("V", ("sqb",), ("sqb",),
               lambda v: v.tensor_scalar(out=sqb[:, 512:1024], in0=sqb[:, 0:512],
                                         scalar1=RMAGIC, scalar2=-RMAGIC,
                                         op0=ALU.add, op1=ALU.add))
            OP("V", ("sqb", "amaxb"), ("arB1",),
               lambda v: v.tensor_tensor(out=arB1[:, 0:512], in0=sqb[:, 512:1024],
                                         in1=amaxb[:, 512:1024], op=ALU.mult))
            OP("QA", ("arB1",), ("xqh",),
               lambda en, c=c, t0=t0: en.dma_start(
                   out=ap3(xqh, c * 128 * T + t0, [[T, 128], [1, 512]]),
                   in_=arB1[:, 0:512]))
    P.barrier()
    P.flush(block)

    P.mark("B")
    OP("V", (), ("knrun",), lambda v: v.memset(rowA[0:1, 600:601], 0.0))
    for seg0 in range(0, WTP, 512):
        L = min(512, WTP - seg0)
        for ck in range(2):
            sl = (ck % 2) * 512
            OP("V", ("kv0k",), (f"sqs{ck}",),
               lambda v, ck=ck, seg0=seg0, L=L, sl=sl: v.tensor_tensor(
                   out=sqb[:, sl:sl + L],
                   in0=kv0k[:, ck * WTP + seg0:ck * WTP + seg0 + L],
                   in1=kv0k[:, ck * WTP + seg0:ck * WTP + seg0 + L], op=ALU.mult))
            OP("T", (f"sqs{ck}", "ones"), ("psC",),
               lambda t, ck=ck, L=L, sl=sl: t.matmul(
                   psC[0:1, 0:L], ones1f[:], sqb[:, sl:sl + L],
                   start=(ck == 0), stop=(ck == 1)))
        OP("S", ("psC",), ("rowB",),
           lambda sc, L=L: sc.copy(rowB[0:1, 0:L], psC[0:1, 0:L]))
        OP("V", ("rowB",), ("knseg",),
           lambda v, L=L: v.reduce_max(out=rowB[0:1, 630:631], in_=rowB[0:1, 0:L],
                                       axis=AX.X))
        OP("V", ("knseg", "knrun"), ("knrun",),
           lambda v: v.tensor_tensor(out=rowA[0:1, 600:601], in0=rowA[0:1, 600:601],
                                     in1=rowB[0:1, 630:631], op=ALU.max))
    OP("S", ("knrun",), ("knrun",),
       lambda sc: sc.activation(out=rowA[0:1, 600:601], in_=rowA[0:1, 600:601],
                                func=AF.Sqrt))
    C_PT0W, C_PT0G, C_VGP, C_ZEBE, C_ZEBO = 2560, 3584, 4608, 5120, 5632
    OP("V", (), ("vg4",),
       lambda v: v.memset(wbufA[0:32, C_VGP:C_VGP + 512], 0.0))
    for g in range(4):
        OP("V", ("v0t",), ("vg4",),
           lambda v, g=g: v.tensor_copy(
               wbufA[0:32, C_VGP + g * 128 + (g % 2) * 64:C_VGP + g * 128 + (g % 2) * 64 + 64],
               v0t[0:32, g * 64:(g + 1) * 64]))
    for tau in range(NT):
        t0 = tau * TT
        OP("QS", ("qT0h",), ("arB2",),
           lambda en, t0=t0: en.dma_start(
               out=ap3(arB2, 0, [[4096, 128], [512, 8], [1, 512]]),
               in_=ap3(qT0h, t0, [[T, 128], [128 * T, 8], [1, 512]])))
        for c in range(8):
            sl = (c % 2) * 512
            OP("V", ("arB2",), (f"sqs{c % 2}",),
               lambda v, c=c, sl=sl: v.tensor_tensor(
                   out=sqb[:, sl:sl + 512], in0=arB2[:, c * 512:(c + 1) * 512],
                   in1=arB2[:, c * 512:(c + 1) * 512], op=ALU.mult))
            OP("T", (f"sqs{c % 2}", "ones"), ("psC",),
               lambda t, c=c, sl=sl: t.matmul(psC[0:1, 0:512], ones1f[:],
                                              sqb[:, sl:sl + 512],
                                              start=(c == 0), stop=(c == 7)))
        OP("S", ("psC",), ("rowB",),
           lambda sc: sc.copy(rowB[0:1, 0:512], psC[0:1, 0:512]))
        OP("V", ("rowB",), ("qn2",),
           lambda v: v.reduce_max(out=rowB[0:1, 620:621], in_=rowB[0:1, 0:512],
                                  axis=AX.X))
        OP("S", ("qn2",), ("qn2",),
           lambda sc: sc.activation(out=rowB[0:1, 620:621], in_=rowB[0:1, 620:621],
                                    func=AF.Sqrt))
        OP("V", ("qn2", "knrun"), ("Cval",),
           lambda v: v.tensor_tensor(out=rowB[0:1, 621:622], in0=rowB[0:1, 620:621],
                                     in1=rowA[0:1, 600:601], op=ALU.mult))
        OP("V", ("Cval",), ("Cval",),
           lambda v: v.tensor_scalar(out=rowB[0:1, 621:622], in0=rowB[0:1, 621:622],
                                     scalar1=-1.0, scalar2=None, op0=ALU.mult))
        OP("G", ("Cval",), ("sCneg",),
           lambda g: g.partition_broadcast(sCneg[:], rowB[0:1, 621:622], channels=64))
        OP("V", (), ("vov",), lambda v: v.memset(wbufB[:, 0:4096], 0.0))
        for s8 in range(8):
            sg = tau * 8 + s8
            base = s8 * 512
            for rp in range(4):
                tok0 = 64 * sg - 32 + rp * 32
                if tok0 < 0 or tok0 >= T:
                    continue
                c0 = tok0 // 128
                r0 = tok0 % 128
                for g in range(4):
                    OP("V", ("v0t",), ("vov",),
                       lambda v, base=base, rp=rp, c0=c0, r0=r0, g=g: v.tensor_copy(
                           wbufB[rp * 32:(rp + 1) * 32,
                                 base + g * 128 + (g % 2) * 64:base + g * 128 + (g % 2) * 64 + 64],
                           v0t[r0:r0 + 32, c0 * 256 + g * 64:c0 * 256 + (g + 1) * 64]))
        for s8 in range(8):
            i0 = t0 + s8 * 64
            sg = i0 // 64
            mvar = 0 if sg == 0 else (2 if sg == 63 else 1)
            wcol0 = PAD + 64 * sg - 32
            OP("V", ("arB2", "pmask"), ("zebE",),
               lambda v, s8=s8: v.tensor_scalar(
                   out=ap3(wbufA, C_ZEBE, [[8192, 128], [64, 8], [1, 64]]),
                   in0=ap3(arB2, s8 * 64, [[4096, 128], [512, 8], [1, 64]]),
                   scalar1=pmask[:, 0:1], scalar2=None, op0=ALU.mult))
            OP("V", ("arB2", "pmask"), ("zebO",),
               lambda v, s8=s8: v.tensor_scalar(
                   out=ap3(wbufA, C_ZEBO, [[8192, 128], [64, 8], [1, 64]]),
                   in0=ap3(arB2, s8 * 64, [[4096, 128], [512, 8], [1, 64]]),
                   scalar1=pmask[:, 1:2], scalar2=None, op0=ALU.mult))
            def sc_emit(t, wcol0=wcol0, lo=True):
                last = None
                ps = psA if lo else psB
                for j in (range(0, 8) if lo else range(8, 16)):
                    cch = j // 2
                    p = j % 2
                    ck = 0 if cch < 4 else 1
                    slot = j % 8
                    zc = C_ZEBE if p == 0 else C_ZEBO
                    last = t.matmul(ps[0:64, slot * 128:(slot + 1) * 128],
                                    wbufA[:, zc + cch * 64:zc + (cch + 1) * 64],
                                    kv0k[:, ck * WTP + wcol0:ck * WTP + wcol0 + 128],
                                    start=True, stop=True)
                return last
            OP("T", ("zebE", "zebO", "kv0k"), ("psA",), lambda t, w=wcol0: sc_emit(t, w, True))
            OP("T", ("zebE", "zebO", "kv0k"), ("psB",), lambda t, w=wcol0: sc_emit(t, w, False))
            def sg_emit(t):
                last = None
                for j in range(16):
                    cch = j // 2
                    p = j % 2
                    ck = 0 if cch < 4 else 1
                    zc = C_ZEBE if p == 0 else C_ZEBO
                    last = t.matmul(psE[0:64, j * 8:(j + 1) * 8],
                                    wbufA[:, zc + cch * 64:zc + (cch + 1) * 64],
                                    kv0k[:, ck * WTP + PAD:ck * WTP + PAD + 8],
                                    start=True, stop=True)
                return last
            OP("T", ("zebE", "zebO", "kv0k"), ("psE",), sg_emit)
            OP("S", ("psA", "sCneg"), ("p0w",),
               lambda sc: sc.activation(out=wbufA[0:64, 0:1024], in_=psA[0:64, :],
                                        func=AF.Exp, bias=sCneg[:]))
            OP("S", ("psB", "sCneg"), ("p0w",),
               lambda sc: sc.activation(out=wbufA[0:64, 1024:2048], in_=psB[0:64, :],
                                        func=AF.Exp, bias=sCneg[:]))
            OP("V", (), ("p0g",), lambda v: v.memset(wbufA[0:64, 2048:2560], 0.0))
            OP("S", ("psE", "sCneg", "p0g"), ("p0g",),
               lambda sc: sc.activation(
                   out=ap3(wbufA, 2048, [[8192, 64], [32, 16], [1, 8]]),
                   in_=ap3(psE, 0, [[512, 64], [8, 16], [1, 8]]),
                   func=AF.Exp, bias=sCneg[:]))
            OP("V", ("p0w", "sm0"), ("p0w",),
               lambda v, mvar=mvar: v.tensor_tensor(
                   out=wbufA[0:64, 0:2048], in0=wbufA[0:64, 0:2048],
                   in1=hbuf2[0:64, 8192 + mvar * 2048:8192 + (mvar + 1) * 2048],
                   op=ALU.mult))
            OP("V", ("p0g", "sm0g"), ("p0g",),
               lambda v, mvar=mvar: v.tensor_tensor(
                   out=wbufA[0:64, 2048:2560], in0=wbufA[0:64, 2048:2560],
                   in1=hbuf2[0:64, 14336 + mvar * 512:14336 + (mvar + 1) * 512],
                   op=ALU.mult))
            OP("V", ("p0w",), ("l0_s",),
               lambda v: v.reduce_sum(out=l0_s[:],
                                      in_=ap3(wbufA, 0, [[8192, 64], [128, 16], [1, 128]]),
                                      axis=AX.X))
            OP("V", ("p0g",), ("r0_s",),
               lambda v: v.reduce_sum(out=r0_s[:],
                                      in_=ap3(wbufA, 2048, [[8192, 64], [32, 16], [1, 32]]),
                                      axis=AX.X))
            OP("V", ("l0_s", "r0_s"), ("l0_s",),
               lambda v: v.tensor_tensor(out=l0_s[:], in0=l0_s[:], in1=r0_s[:], op=ALU.add))
            OP("V", ("l0_s",), ("r0_s",), lambda v: v.reciprocal(out=r0_s[:], in_=l0_s[:]))
            OP("V", ("p0w", "r0_s"), ("p0w",),
               lambda v: v.tensor_tensor(
                   out=ap3(wbufA, 0, [[8192, 64], [128, 16], [1, 128]]),
                   in0=ap3(wbufA, 0, [[8192, 64], [128, 16], [1, 128]]),
                   in1=ap3(r0_s, 0, [[16, 64], [1, 16], [0, 128]]), op=ALU.mult))
            OP("V", ("p0g", "r0_s"), ("p0g",),
               lambda v: v.tensor_tensor(
                   out=ap3(wbufA, 2048, [[8192, 64], [32, 16], [1, 32]]),
                   in0=ap3(wbufA, 2048, [[8192, 64], [32, 16], [1, 32]]),
                   in1=ap3(r0_s, 0, [[16, 64], [1, 16], [0, 32]]), op=ALU.mult))
            def tpw_emit(t):
                last = None
                for h in range(16):
                    last = t.transpose(psT[:, h * 64:(h + 1) * 64],
                                       wbufA[0:64, h * 128:(h + 1) * 128],
                                       ident[0:64, 0:64])
                return last
            OP("T", ("p0w", "ident"), ("psT",), tpw_emit)
            OP("S", ("psT",), ("pT0w",),
               lambda sc: sc.copy(wbufA[:, C_PT0W:C_PT0W + 1024], psT[:]))
            def tpg_emit(t):
                last = None
                for j in range(16):
                    last = t.transpose(psT[0:32, j * 64:(j + 1) * 64],
                                       wbufA[0:64, 2048 + j * 32:2048 + (j + 1) * 32],
                                       ident[0:64, 0:64])
                return last
            OP("T", ("p0g", "ident"), ("psT",), tpg_emit)
            OP("S", ("psT",), ("pT0g",),
               lambda sc: sc.copy(wbufA[0:32, C_PT0G:C_PT0G + 1024], psT[0:32, 0:1024]))
            def av_emit(t, s8=s8):
                last = None
                for cch in range(8):
                    for p in range(2):
                        j = 2 * cch + p
                        g = (0 if cch < 4 else 2) + p
                        last = t.matmul(
                            psC[:, cch * 64:(cch + 1) * 64],
                            wbufB[:, s8 * 512 + g * 128:s8 * 512 + (g + 1) * 128],
                            wbufA[:, C_PT0W + j * 64:C_PT0W + (j + 1) * 64],
                            start=(p == 0), stop=False)
                        last = t.matmul(
                            psC[:, cch * 64:(cch + 1) * 64],
                            wbufA[0:32, C_VGP + g * 128:C_VGP + (g + 1) * 128],
                            wbufA[0:32, C_PT0G + j * 64:C_PT0G + (j + 1) * 64],
                            start=False, stop=(p == 1))
                return last
            OP("T", ("vov", "pT0w", "pT0g", "vg4"), ("psC",), av_emit)
            OP("S", ("psC",), ("arB4",),
               lambda sc, s8=s8: sc.copy(
                   ap3(arB4, s8 * 64, [[4096, 128], [512, 8], [1, 64]]),
                   ap3(psC, 0, [[512, 128], [64, 8], [1, 64]])))
        for m in range(8):
            def wo_emit(t, m=m):
                last = None
                for c in range(8):
                    last = t.matmul(psD[:, 0:512],
                                    hbuf[:, 4096 + c * 1024 + m * 128:4096 + c * 1024 + (m + 1) * 128],
                                    arB4[:, c * 512:(c + 1) * 512],
                                    start=(c == 0), stop=(c == 7))
                return last
            OP("T", ("arB4", "wo0b"), ("psD",), wo_emit)
            OP("S", ("psD", "sbo0"), ("arB1",),
               lambda s, m=m: s.activation(out=arB1[:, 0:512], in_=psD[:, 0:512],
                                           func=AF.Identity, bias=sbo0[:, m:m + 1]))
            OP("QA", ("arB1",), ("aoth",),
               lambda en, m=m, t0=t0: en.dma_start(
                   out=ap3(aoth, m * 128 * T + t0, [[T, 128], [1, 512]]),
                   in_=arB1[:, 0:512]))

    P.mark("Bg")
    OP("QS", ("qT0h",), ("gq_s",),
       lambda en: en.dma_start(out=ap3(gq_s, 0, [[64, 128], [8, 8], [1, 8]]),
                               in_=ap3(qT0h, 0, [[T, 128], [128 * T, 8], [1, 8]])))
    OP("V", (), ("arB1",), lambda v: v.memset(arB1[:, 0:256], 0.0))
    for j in range(16):
        cch = j // 2
        p = j % 2
        g = (0 if cch < 4 else 2) + p
        ck = g // 2
        for rp in range(2):
            OP("V", ("gq_s",), ("arB1",),
               lambda v, j=j, p=p, ck=ck, rp=rp: v.tensor_copy(
                   arB1[p * 64 + rp * 32:p * 64 + rp * 32 + 32,
                        ck * 128 + j * 8:ck * 128 + (j + 1) * 8],
                   gq_s[p * 64 + rp * 32:p * 64 + rp * 32 + 32,
                        cch * 8:cch * 8 + 8]))
    for nb in range(8):
        ps, pk = (psC, "psC") if nb % 2 == 0 else (psD, "psD")
        def gs_emit(t, nb=nb, ps=ps):
            last = None
            for ck in range(2):
                last = t.matmul(ps[:, 0:512], arB1[:, ck * 128:(ck + 1) * 128],
                                kv0k[:, ck * WTP + PAD + nb * 512:ck * WTP + PAD + (nb + 1) * 512],
                                start=(ck == 0), stop=(ck == 1))
            return last
        OP("T", ("arB1", "kv0k"), (pk,), gs_emit)
        OP("S", (pk,), ("arF2",),
           lambda s, nb=nb, ps=ps: s.copy(arF2[:, nb * 512:(nb + 1) * 512], ps[:, 0:512]))
    OP("V", ("arF2",), ("gm",),
       lambda v: v.reduce_max(out=arF1[:, 0:1], in_=arF2[:], axis=AX.X))
    OP("V", ("gm",), ("gm",),
       lambda v: v.tensor_scalar(out=arF1[:, 1:2], in0=arF1[:, 0:1],
                                 scalar1=-1.0, scalar2=None, op0=ALU.mult))
    OP("S", ("arF2", "gm"), ("arB3", "gl"),
       lambda s: s.activation(out=arB3[:], in_=arF2[:], func=AF.Exp,
                              bias=arF1[:, 1:2], accum_out=arF1[:, 2:3]))
    OP("V", ("gl",), ("gr",),
       lambda v: v.reciprocal(out=arF1[:, 3:4], in_=arF1[:, 2:3]))
    OP("V", ("arB3", "gr"), ("arB3",),
       lambda v: v.tensor_scalar(out=arB3[:], in0=arB3[:], scalar1=arF1[:, 3:4],
                                 scalar2=None, op0=ALU.mult))
    for grp in range(4):
        def gt_emit(t, grp=grp):
            last = None
            for j in range(8):
                kk = grp * 8 + j
                last = t.transpose(psT[:, j * 128:(j + 1) * 128],
                                   arB3[:, kk * 128:(kk + 1) * 128], ident[:])
            return last
        OP("T", ("arB3", "ident"), ("psT",), gt_emit)
        OP("S", ("psT",), ("arB2",),
           lambda s, grp=grp: s.copy(arB2[:, grp * 1024:(grp + 1) * 1024], psT[:]))
    def gav_emit(t):
        last = None
        for kk in range(32):
            last = t.matmul(psE[:, 0:256], arB2[:, kk * 128:(kk + 1) * 128],
                            v0t[:, kk * 256:(kk + 1) * 256],
                            start=(kk == 0), stop=(kk == 31))
        return last
    OP("T", ("arB2", "v0t"), ("psE",), gav_emit)
    OP("S", ("psE",), ("arB4",), lambda s: s.copy(arB4[:, 0:256], psE[:, 0:256]))
    def ghead_emit(t):
        t.transpose(psT[:, 0:128], arB4[:, 0:128], ident[:])
        return t.transpose(psT[:, 128:256], arB4[:, 128:256], ident[:])
    OP("T", ("arB4", "ident"), ("psT",), ghead_emit)
    OP("S", ("psT",), ("arB3",), lambda s: s.copy(arB3[:, 0:256], psT[:, 0:256]))
    for j in range(16):
        cch = j // 2
        p = j % 2
        g = (0 if cch < 4 else 2) + p
        for rp in range(2):
            OP("V", ("arB3",), ("gout",),
               lambda v, j=j, cch=cch, p=p, g=g, rp=rp: v.tensor_copy(
                   arF2[p * 64 + rp * 32:p * 64 + rp * 32 + 32,
                        cch * 8:cch * 8 + 8],
                   arB3[p * 64 + rp * 32:p * 64 + rp * 32 + 32,
                        (g // 2) * 128 + j * 8:(g // 2) * 128 + (j + 1) * 8]))
    for m in range(8):
        def gwo_emit(t, m=m):
            last = None
            for c in range(8):
                last = t.matmul(psE[:, 0:8],
                                hbuf[:, 4096 + c * 1024 + m * 128:4096 + c * 1024 + (m + 1) * 128],
                                arF3[:, c * 8:(c + 1) * 8].astype(BF16) if False else arF3[:, c * 8:(c + 1) * 8],
                                start=(c == 0), stop=(c == 7))
            return last
        pass
    OP("V", ("gout",), ("goutb",), lambda v: v.tensor_copy(arB4[:, 0:64], arF2[:, 0:64]))
    for m in range(8):
        def gwo_emit(t, m=m):
            last = None
            for c in range(8):
                last = t.matmul(psE[:, 0:8],
                                hbuf[:, 4096 + c * 1024 + m * 128:4096 + c * 1024 + (m + 1) * 128],
                                arB4[:, c * 8:(c + 1) * 8],
                                start=(c == 0), stop=(c == 7))
            return last
        OP("T", ("goutb", "wo0b"), ("psE",), gwo_emit)
        OP("S", ("psE", "sbo0"), ("arB1",),
           lambda s, m=m: s.activation(out=arB1[:, m * 8:(m + 1) * 8], in_=psE[:, 0:8],
                                       func=AF.Identity, bias=sbo0[:, m:m + 1]))
    OP("QA", ("arB1",), ("aoth",),
       lambda en: en.dma_start(out=ap3(aoth, 0, [[T, 128], [128 * T, 8], [1, 8]]),
                               in_=ap3(arB1, 0, [[4096, 128], [8, 8], [1, 8]])))
    P.barrier()
    P.flush(block)

    P.mark("C")
    for tau in range(NT):
        t0 = tau * TT
        OP("QS", ("xqh",), ("arB2",),
           lambda en, t0=t0: en.dma_start(
               out=ap3(arB2, 0, [[4096, 128], [512, 8], [1, 512]]),
               in_=ap3(xqh, t0, [[T, 128], [128 * T, 8], [1, 512]])))
        OP("QS", ("aoth",), ("arB3",),
           lambda en, t0=t0: en.dma_start(
               out=ap3(arB3, 0, [[4096, 128], [512, 8], [1, 512]]),
               in_=ap3(aoth, t0, [[T, 128], [128 * T, 8], [1, 512]])))
        OP("QS", ("xTh",), ("arF2",),
           lambda en, t0=t0: en.dma_start(
               out=ap3(arF2, 0, [[4096, 128], [512, 8], [1, 512]]),
               in_=ap3(xTh, t0, [[T, 128], [128 * T, 8], [1, 512]])))
        for qm in range(4):
            gbuf, gk = (wbufA, "wbufA") if qm % 2 == 0 else (wbufB, "wbufB")
            uof, uk = (0, "hbuf2w") if qm % 2 == 0 else (8192, "hbuf2x")
            OP("QG", ("HQW",), (gk,),
               lambda en, qm=qm, gbuf=gbuf: en.dma_start(
                   out=ap3(gbuf, 0, [[8192, 128], [1024, 8], [1, 1024]]),
                   in_=ap3(g0q, qm * 1024, [[INNER, 128], [128 * INNER, 8], [1, 1024]])))
            OP("QG", ("HQW",), (uk,),
               lambda en, qm=qm, uof=uof: en.dma_start(
                   out=ap3(hbuf2, uof, [[16384, 128], [1024, 8], [1, 1024]]),
                   in_=ap3(u0q, qm * 1024, [[INNER, 128], [128 * INNER, 8], [1, 1024]])))
            for mh in range(8):
                m = qm * 8 + mh
                def ga_emit(t, mh=mh, gbuf=gbuf):
                    last = None
                    for c in range(8):
                        last = t.matmul(psC[:, 0:512],
                                        gbuf[:, c * 1024 + mh * 128:c * 1024 + (mh + 1) * 128],
                                        arB2[:, c * 512:(c + 1) * 512],
                                        start=(c == 0), stop=(c == 7))
                    return last
                OP("T", ("arB2", gk), ("psC",), ga_emit)
                def up_emit(t, mh=mh, uof=uof):
                    last = None
                    for c in range(8):
                        last = t.matmul(psD[:, 0:512],
                                        hbuf2[:, uof + c * 1024 + mh * 128:uof + c * 1024 + (mh + 1) * 128],
                                        arB2[:, c * 512:(c + 1) * 512],
                                        start=(c == 0), stop=(c == 7))
                    return last
                OP("T", ("arB2", uk), ("psD",), up_emit)
                OP("S", ("psC", "swsc"), ("arF1",),
                   lambda s: s.activation(out=arF1[:, 0:512], in_=psC[:, 0:512],
                                          func=AF.Silu, scale=swsc[:, 0:1]))
                OP("S", ("psD", "swsc"), ("arF1b",),
                   lambda s: s.activation(out=arF1[:, 512:1024], in_=psD[:, 0:512],
                                          func=AF.Identity, scale=swsc[:, 1:2]))
                OP("V", ("arF1", "arF1b"), ("hbuf",),
                   lambda v, m=m: v.tensor_tensor(out=hbuf[:, m * 512:(m + 1) * 512],
                                                  in0=arF1[:, 0:512], in1=arF1[:, 512:1024],
                                                  op=ALU.mult))
                if m == 0:
                    OP("V", ("hbuf",), ("amaxb",),
                       lambda v: v.tensor_copy(amaxb[:, 0:512], hbuf[:, 0:512]))
                    OP("V", ("hbuf",), ("amaxb",),
                       lambda v: v.tensor_copy(amaxb[:, 512:1024], hbuf[:, 0:512]))
                else:
                    OP("V", ("hbuf", "amaxb"), ("amaxb",),
                       lambda v, m=m: v.tensor_tensor(out=amaxb[:, 0:512],
                                                      in0=amaxb[:, 0:512],
                                                      in1=hbuf[:, m * 512:(m + 1) * 512],
                                                      op=ALU.max))
                    OP("V", ("hbuf", "amaxb"), ("amaxb",),
                       lambda v, m=m: v.tensor_tensor(out=amaxb[:, 512:1024],
                                                      in0=amaxb[:, 512:1024],
                                                      in1=hbuf[:, m * 512:(m + 1) * 512],
                                                      op=ALU.min))
        OP("V", ("amaxb",), ("amaxb",),
           lambda v: v.tensor_scalar(out=amaxb[:, 512:1024], in0=amaxb[:, 512:1024],
                                     scalar1=-1.0, scalar2=None, op0=ALU.mult))
        OP("V", ("amaxb", "amaxb"), ("amaxb",),
           lambda v: v.tensor_tensor(out=amaxb[:, 0:512], in0=amaxb[:, 0:512],
                                     in1=amaxb[:, 512:1024], op=ALU.max))
        for lev in (64, 32):
            for rp in range(0, lev, 32):
                OP("V", ("amaxb",), ("sqb",),
                   lambda v, lev=lev, rp=rp: v.tensor_copy(
                       sqb[rp:rp + 32, 0:512], amaxb[lev + rp:lev + rp + 32, 0:512]))
            OP("V", ("amaxb", "sqb"), ("amaxb",),
               lambda v, lev=lev: v.tensor_tensor(out=amaxb[0:lev, 0:512],
                                                  in0=amaxb[0:lev, 0:512],
                                                  in1=sqb[0:lev, 0:512], op=ALU.max))
        OP("V", ("amaxb",), ("amaxc",),
           lambda v: v.tensor_copy(sqb[0:32, 512:1024], amaxb[0:32, 0:512]))
        def amt_emit(t):
            last = None
            for p in range(4):
                last = t.transpose(psC[:, p * 32:(p + 1) * 32],
                                   sqb[0:32, 512 + p * 128:512 + (p + 1) * 128],
                                   idf[0:32, 0:32])
            return last
        OP("T", ("amaxc", "idf"), ("psC",), amt_emit)
        OP("S", ("psC",), ("sqb",), lambda s: s.copy(sqb[:, 0:128], psC[:, 0:128]))
        OP("V", ("sqb",), ("amaxt",),
           lambda v: v.reduce_max(out=amaxb[:, 0:4],
                                  in_=ap3(sqb, 0, [[1024, 128], [32, 4], [1, 32]]),
                                  axis=AX.X))
        OP("V", ("amaxt",), ("amaxt",),
           lambda v: v.tensor_scalar(out=amaxb[:, 0:4], in0=amaxb[:, 0:4],
                                     scalar1=1e-5, scalar2=None, op0=ALU.max))
        OP("S", ("amaxt",), ("amaxt2",),
           lambda s: s.activation(out=amaxb[:, 4:8], in_=amaxb[:, 0:4], func=AF.Ln))
        OP("S", ("amaxt2",), ("amaxt2",),
           lambda s: s.activation(out=amaxb[:, 4:8], in_=amaxb[:, 4:8],
                                  func=AF.Exp, scale=-1.0, bias=cl127[:, 0:1]))
        OP("V", ("amaxt",), ("amaxt3",),
           lambda v: v.tensor_scalar(out=amaxb[:, 8:12], in0=amaxb[:, 0:4],
                                     scalar1=1.0 / 127.0, scalar2=None, op0=ALU.mult))
        def rowt_emit(t):
            last = None
            for p in range(4):
                last = t.transpose(psC[0:1, p * 128:(p + 1) * 128],
                                   amaxb[:, 4 + p:5 + p], idf[:])
                last = t.transpose(psD[0:1, p * 128:(p + 1) * 128],
                                   amaxb[:, 8 + p:9 + p], idf[:])
            return last
        OP("T", ("amaxt2", "amaxt3", "idf"), ("psC", "psD"), rowt_emit)
        OP("S", ("psC",), ("rowB",), lambda s: s.copy(rowB[0:1, 0:512], psC[0:1, 0:512]))
        OP("S", ("psD",), ("rowC",), lambda s: s.copy(rowC[0:1, 0:512], psD[0:1, 0:512]))
        OP("T", ("rowB", "ones"), ("psC",),
           lambda t: t.matmul(psC[:, 0:512], onesrf[:], rowB[0:1, 0:512],
                              start=True, stop=True))
        OP("S", ("psC",), ("amaxb",), lambda s: s.copy(amaxb[:, 0:512], psC[:, 0:512]))
        OP("T", ("rowC", "ones"), ("psD",),
           lambda t: t.matmul(psD[:, 0:512], onesrf[:], rowC[0:1, 0:512],
                              start=True, stop=True))
        OP("S", ("psD",), ("amaxb",), lambda s: s.copy(amaxb[:, 512:1024], psD[:, 0:512]))
        for m in range(32):
            OP("V", ("hbuf", "amaxb"), ("sqb",),
               lambda v, m=m: v.tensor_tensor(out=sqb[:, 0:512],
                                              in0=hbuf[:, m * 512:(m + 1) * 512],
                                              in1=amaxb[:, 0:512], op=ALU.mult))
            OP("V", ("sqb",), ("sqb",),
               lambda v: v.tensor_scalar(out=sqb[:, 512:1024], in0=sqb[:, 0:512],
                                         scalar1=RMAGIC, scalar2=-RMAGIC,
                                         op0=ALU.add, op1=ALU.add))
            OP("V", ("sqb", "amaxb"), ("hbuf",),
               lambda v, m=m: v.tensor_tensor(out=hbuf[:, m * 512:(m + 1) * 512],
                                              in0=sqb[:, 512:1024], in1=amaxb[:, 512:1024], op=ALU.mult))
        for qm in range(4):
            dbuf, dk = (wbufA, "wbufA") if qm % 2 == 0 else (wbufB, "wbufB")
            OP("QG", ("HQW",), (dk,),
               lambda en, qm=qm, dbuf=dbuf: en.dma_start(
                   out=ap3(dbuf, 0, [[8192, 128], [256, 32], [1, 256]]),
                   in_=ap3(d0q, qm * 256, [[D, 128], [128 * D, 32], [1, 256]])))
            for mh in range(2):
                m = qm * 2 + mh
                def dn_emit(t, mh=mh, dbuf=dbuf):
                    last = None
                    for k in range(32):
                        last = t.matmul(psC[:, 0:512],
                                        dbuf[:, k * 256 + mh * 128:k * 256 + (mh + 1) * 128],
                                        hbuf[:, k * 512:(k + 1) * 512],
                                        start=(k == 0), stop=(k == 31))
                    return last
                OP("T", ("hbuf", dk), ("psC",), dn_emit)
                OP("S", ("psC", "swsc"), ("arF1",),
                   lambda s: s.activation(out=arF1[:, 0:512], in_=psC[:, 0:512],
                                          func=AF.Identity, scale=swsc[:, 2:3]))
                OP("V", ("arF1", "arB3"), ("arF1",),
                   lambda v, m=m: v.tensor_tensor(out=arF1[:, 0:512], in0=arF1[:, 0:512],
                                                  in1=arB3[:, m * 512:(m + 1) * 512],
                                                  op=ALU.add))
                OP("V", ("arF1", "arF2"), ("arF2",),
                   lambda v, m=m: v.tensor_tensor(out=arF2[:, m * 512:(m + 1) * 512],
                                                  in0=arF1[:, 0:512],
                                                  in1=arF2[:, m * 512:(m + 1) * 512],
                                                  op=ALU.add))
        layernorm("arF2", arF2, None, None, "arB4", arB4)
        for widx, (wsrc, wkey) in enumerate(((wk1, "l1k"), (wv1, "l1v"), (wq1, "l1q"))):
            OP("QG", ("HB",), ("wbufB",),
               lambda en, wsrc=wsrc: en.dma_start(
                   out=ap3(wbufB, 0, [[8192, 128], [1024, 8], [1, 1024]]),
                   in_=ap3(wsrc, 0, [[D, 128], [128 * D, 8], [1, 1024]])))
            if wkey == "l1v":
                for i in range(4):
                    def v1_emit(t, i=i):
                        last = None
                        for c in range(8):
                            for nn in range(2):
                                last = t.matmul(
                                    psB[:, nn * 512:(nn + 1) * 512],
                                    arB4[:, c * 512 + i * 128:c * 512 + (i + 1) * 128],
                                    wbufB[:, c * 1024 + nn * 512:c * 1024 + (nn + 1) * 512],
                                    start=(c == 0), stop=(c == 7))
                        return last
                    OP("T", ("arB4", "wbufB"), ("psB",), v1_emit)
                    OP("S", ("psB",), ("arB1",),
                       lambda s, i=i: s.copy(arB1[:, i * 1024:(i + 1) * 1024], psB[:]))
                OP("QA", ("arB1",), ("v1h",),
                   lambda en, t0=t0: en.dma_start(
                       out=ap3(v1h, t0 * D, [[D, 128], [128 * D, 4], [1, D]]),
                       in_=ap3(arB1, 0, [[4096, 128], [1024, 4], [1, 1024]])))
            else:
                for m in range(8):
                    def kq_emit(t, m=m):
                        last = None
                        for c in range(8):
                            last = t.matmul(psC[:, 0:512],
                                            wbufB[:, c * 1024 + m * 128:c * 1024 + (m + 1) * 128],
                                            arB4[:, c * 512:(c + 1) * 512],
                                            start=(c == 0), stop=(c == 7))
                        return last
                    OP("T", ("arB4", "wbufB"), ("psC",), kq_emit)
                    if wkey == "l1q":
                        OP("S", ("psC",), ("arB1",),
                           lambda s, m=m: s.activation(
                               out=arB1[:, m * 512:(m + 1) * 512], in_=psC[:, 0:512],
                               func=AF.Identity, scale=1.0 / 32.0))
                    else:
                        OP("S", ("psC",), ("arB1",),
                           lambda s, m=m: s.copy(arB1[:, m * 512:(m + 1) * 512], psC[:, 0:512]))
                if wkey == "l1k":
                    OP("QA", ("arB1",), ("k1h",),
                       lambda en, t0=t0: en.dma_start(
                           out=ap3(k1h, t0, [[T, 128], [128 * T, 8], [1, 512]]),
                           in_=ap3(arB1, 0, [[4096, 128], [512, 8], [1, 512]])))
                else:
                    OP("V", ("arB1", "ssel"), ("arB3",),
                       lambda v: v.tensor_scalar(
                           out=ap3(arB3, 0, [[4096, 128], [256, 8], [128, 2], [1, 128]]),
                           in0=ap3(arB1, 0, [[4096, 128], [512, 8], [256, 2], [1, 128]]),
                           scalar1=ssel[:, 0:1], scalar2=None, op0=ALU.mult))
                    OP("V", ("arB1", "ssel", "arB3"), ("arB3",),
                       lambda v: v.scalar_tensor_tensor(
                           out=ap3(arB3, 0, [[4096, 128], [256, 8], [128, 2], [1, 128]]),
                           in0=ap3(arB1, 128, [[4096, 128], [512, 8], [256, 2], [1, 128]]),
                           scalar=ssel[:, 1:2],
                           in1=ap3(arB3, 0, [[4096, 128], [256, 8], [128, 2], [1, 128]]),
                           op0=ALU.mult, op1=ALU.add))
                    OP("QA", ("arB3",), ("qT1h",),
                       lambda en, t0=t0: en.dma_start(
                           out=ap3(qT1h, t0 // 2, [[TOWN, 128], [128 * TOWN, 8], [1, 256]]),
                           in_=ap3(arB3, 0, [[4096, 128], [256, 8], [1, 256]])))
        OP("V", ("arF2", "ssel"), ("arF1",),
           lambda v: v.tensor_scalar(
               out=ap3(arF1, 0, [[2048, 128], [256, 8], [128, 2], [1, 128]]),
               in0=ap3(arF2, 0, [[4096, 128], [512, 8], [256, 2], [1, 128]]),
               scalar1=ssel[:, 0:1], scalar2=None, op0=ALU.mult))
        OP("V", ("arF2", "ssel", "arF1"), ("arF1",),
           lambda v: v.scalar_tensor_tensor(
               out=ap3(arF1, 0, [[2048, 128], [256, 8], [128, 2], [1, 128]]),
               in0=ap3(arF2, 128, [[4096, 128], [512, 8], [256, 2], [1, 128]]),
               scalar=ssel[:, 1:2],
               in1=ap3(arF1, 0, [[2048, 128], [256, 8], [128, 2], [1, 128]]),
               op0=ALU.mult, op1=ALU.add))
        OP("V", ("arB4", "ssel"), ("arB2",),
           lambda v: v.tensor_scalar(
               out=ap3(arB2, 0, [[4096, 128], [256, 8], [128, 2], [1, 128]]),
               in0=ap3(arB4, 0, [[4096, 128], [512, 8], [256, 2], [1, 128]]),
               scalar1=ssel[:, 0:1], scalar2=None, op0=ALU.mult))
        OP("V", ("arB4", "ssel", "arB2"), ("arB2",),
           lambda v: v.scalar_tensor_tensor(
               out=ap3(arB2, 0, [[4096, 128], [256, 8], [128, 2], [1, 128]]),
               in0=ap3(arB4, 128, [[4096, 128], [512, 8], [256, 2], [1, 128]]),
               scalar=ssel[:, 1:2],
               in1=ap3(arB2, 0, [[4096, 128], [256, 8], [128, 2], [1, 128]]),
               op0=ALU.mult, op1=ALU.add))
        OP("QA", ("arF1",), ("x1h",),
           lambda en, t0=t0: en.dma_start(
               out=ap3(x1h, t0 // 2, [[TOWN, 128], [128 * TOWN, 8], [1, 256]]),
               in_=ap3(arF1, 0, [[2048, 128], [256, 8], [1, 256]])))
        OP("QA", ("arB2",), ("xn1h",),
           lambda en, t0=t0: en.dma_start(
               out=ap3(xn1h, t0 // 2, [[TOWN, 128], [128 * TOWN, 8], [1, 256]]),
               in_=ap3(arB2, 0, [[4096, 128], [256, 8], [1, 256]])))
    P.barrier()
    P.flush(block)

    P.mark("F")
    for it in range(4):
        o0 = it * TT
        OP("QS", ("xn1h",), ("arB2",),
           lambda en, o0=o0: en.dma_start(
               out=ap3(arB2, 0, [[4096, 128], [512, 8], [1, 512]]),
               in_=ap3(xn1h, o0, [[TOWN, 128], [128 * TOWN, 8], [1, 512]])))
        OP("QS", ("x1h",), ("arF2",),
           lambda en, o0=o0: en.dma_start(
               out=ap3(arF2, 0, [[4096, 128], [512, 8], [1, 512]]),
               in_=ap3(x1h, o0, [[TOWN, 128], [128 * TOWN, 8], [1, 512]])))
        for qm in range(4):
            w1o, w1b_, w1k = ((0, wbufA, "wbufA") if qm % 2 == 0
                              else (0, hbuf2, "hbuf2w"))
            w2o, w2b_, w2k = ((0, wbufB, "wbufB") if qm % 2 == 0
                              else (8192, hbuf2, "hbuf2x"))
            w1w = (w1k,) if qm % 2 == 0 else (w1k, "hbuf2")
            w2w = (w2k,) if qm % 2 == 0 else (w2k, "hbuf2")
            OP("QG", ("HB",), w1w,
               lambda en, qm=qm, w1o=w1o, w1b_=w1b_: en.dma_start(
                   out=ap3(w1b_, w1o, [[w1b_.shape[1], 128], [1024, 8], [1, 1024]]),
                   in_=ap3(w1, qm * 1024, [[INNER, 128], [128 * INNER, 8], [1, 1024]])))
            OP("QG", ("HB",), w2w,
               lambda en, qm=qm, w2o=w2o, w2b_=w2b_: en.dma_start(
                   out=ap3(w2b_, w2o, [[w2b_.shape[1], 128], [1024, 8], [1, 1024]]),
                   in_=ap3(w2, qm * 1024, [[INNER, 128], [128 * INNER, 8], [1, 1024]])))
            for mh in range(8):
                m = qm * 8 + mh
                def f1_emit(t, mh=mh, w1o=w1o, w1b_=w1b_):
                    last = None
                    for c in range(8):
                        last = t.matmul(psC[:, 0:512],
                                        w1b_[:, w1o + c * 1024 + mh * 128:w1o + c * 1024 + (mh + 1) * 128],
                                        arB2[:, c * 512:(c + 1) * 512],
                                        start=(c == 0), stop=(c == 7))
                    return last
                OP("T", ("arB2", w1k), ("psC",), f1_emit)
                def f2_emit(t, mh=mh, w2o=w2o, w2b_=w2b_):
                    last = None
                    for c in range(8):
                        last = t.matmul(psD[:, 0:512],
                                        w2b_[:, w2o + c * 1024 + mh * 128:w2o + c * 1024 + (mh + 1) * 128],
                                        arB2[:, c * 512:(c + 1) * 512],
                                        start=(c == 0), stop=(c == 7))
                    return last
                OP("T", ("arB2", w2k), ("psD",), f2_emit)
                OP("S", ("psC", "s_s1"), ("arF1",),
                   lambda s, m=m: s.activation(out=arF1[:, 0:512], in_=psC[:, 0:512],
                                               func=AF.Silu, scale=s_s1[:, m:m + 1]))
                OP("S", ("psD", "s_s2"), ("arF1b",),
                   lambda s, m=m: s.activation(out=arF1[:, 512:1024], in_=psD[:, 0:512],
                                               func=AF.Identity, scale=s_s2[:, m:m + 1]))
                OP("V", ("arF1", "arF1b"), ("hbuf",),
                   lambda v, m=m: v.tensor_tensor(out=hbuf[:, m * 512:(m + 1) * 512],
                                                  in0=arF1[:, 0:512], in1=arF1[:, 512:1024],
                                                  op=ALU.mult))
        for hi, (hsrc, hscale, src_buf, dst_buf, srcb, dstb) in enumerate(
                ((h1, s_sh1, "hbuf", "hbuf2", hbuf, hbuf2),
                 (h2, s_sh2, "hbuf2", "hbuf", hbuf2, hbuf))):
            for g in range(16):
                wb, wbk = (wbufA, "wbufA") if g % 2 == 0 else (wbufB, "wbufB")
                OP("QG", ("HB",), (wbk,),
                   lambda en, g=g, wb=wb, hsrc=hsrc: en.dma_start(
                       out=ap3(wb, 0, [[8192, 128], [256, 32], [1, 256]]),
                       in_=ap3(hsrc, g * 256, [[INNER, 128], [128 * INNER, 32], [1, 256]])))
                for mg in range(2):
                    m = g * 2 + mg
                    def hh_emit(t, mg=mg, wb=wb, srcb=srcb):
                        last = None
                        for k in range(32):
                            last = t.matmul(psC[:, 0:512],
                                            wb[:, k * 256 + mg * 128:k * 256 + (mg + 1) * 128],
                                            srcb[:, k * 512:(k + 1) * 512],
                                            start=(k == 0), stop=(k == 31))
                        return last
                    OP("T", (src_buf, wbk), ("psC",), hh_emit)
                    OP("S", ("psC", "sc"), (dst_buf,),
                       lambda s, m=m, hscale=hscale, dstb=dstb: s.activation(
                           out=dstb[:, m * 512:(m + 1) * 512], in_=psC[:, 0:512],
                           func=AF.Gelu, scale=hscale[:, m:m + 1]))
        for qm in range(4):
            wbuf3, w3k = (wbufA, "wbufA") if qm % 2 == 0 else (wbufB, "wbufB")
            OP("QG", ("HB",), (w3k,),
               lambda en, qm=qm, wbuf3=wbuf3: en.dma_start(
                   out=ap3(wbuf3, 0, [[8192, 128], [256, 32], [1, 256]]),
                   in_=ap3(w3, qm * 256, [[D, 128], [128 * D, 32], [1, 256]])))
            for mh in range(2):
                m = qm * 2 + mh
                def w3_emit(t, mh=mh, wbuf3=wbuf3):
                    last = None
                    for k in range(32):
                        last = t.matmul(psC[:, 0:512],
                                        wbuf3[:, k * 256 + mh * 128:k * 256 + (mh + 1) * 128],
                                        hbuf[:, k * 512:(k + 1) * 512],
                                        start=(k == 0), stop=(k == 31))
                    return last
                OP("T", ("hbuf", w3k), ("psC",), w3_emit)
                OP("S", ("psC", "s_s3"), ("arF1",),
                   lambda s, m=m: s.activation(out=arF1[:, 0:512], in_=psC[:, 0:512],
                                               func=AF.Identity, scale=s_s3[:, m:m + 1]))
                OP("V", ("arF1", "arF2"), ("arF2",),
                   lambda v, m=m: v.tensor_tensor(out=arF2[:, m * 512:(m + 1) * 512],
                                                  in0=arF1[:, 0:512],
                                                  in1=arF2[:, m * 512:(m + 1) * 512],
                                                  op=ALU.add))
        OP("QA", ("arF2",), ("x1h",),
           lambda en, o0=o0: en.dma_start(
               out=ap3(x1h, o0, [[TOWN, 128], [128 * TOWN, 8], [1, 512]]),
               in_=ap3(arF2, 0, [[4096, 128], [512, 8], [1, 512]])))
    P.barrier()
    P.flush(block)

    P.mark("G")
    def nkb_of(j):
        return (j + 2) // 2

    for cc in range(8):
        OP("QG", ("HB",), ("wo1b",),
           lambda g, cc=cc: g.dma_start(out=kv0k[:, cc * 1024:(cc + 1) * 1024],
                                        in_=wo1[cc * 128:(cc + 1) * 128, :]))
    for g4 in range(4):
        o0 = g4 * TT
        OP("QS", ("qT1h",), ("arB2",),
           lambda en, o0=o0: en.dma_start(
               out=ap3(arB2, 0, [[4096, 128], [512, 8], [1, 512]]),
               in_=ap3(qT1h, o0, [[TOWN, 128], [128 * TOWN, 8], [1, 512]])))
        nkb_max = nkb_of(g4 * 4 + 3)
        for kb in range(nkb_max):
            kbuf, kbk = (kblk[0], "kblk0") if kb % 2 == 0 else (kblk[1], "kblk1")
            OP("QS", ("k1h",), (kbk,),
               lambda en, kb=kb, kbuf=kbuf: en.dma_start(
                   out=ap3(wbufA, kbuf.base, [[8192, 128], [512, 8], [1, 512]]),
                   in_=ap3(k1h, kb * 512, [[T, 128], [128 * T, 8], [1, 512]])))
            for j4 in range(4):
                j = g4 * 4 + j4
                if kb >= nkb_of(j):
                    continue
                ps, pk = (psC, "psC") if (kb * 4 + j4) % 2 == 0 else (psD, "psD")
                def ls_emit(t, j4=j4, kbuf=kbuf, ps=ps):
                    last = None
                    for c in range(8):
                        last = t.matmul(ps[:, 0:512],
                                        arB2[:, c * 512 + j4 * 128:c * 512 + (j4 + 1) * 128],
                                        kbuf[:, c * 512:(c + 1) * 512],
                                        start=(c == 0), stop=(c == 7))
                    return last
                OP("T", ("arB2", kbk), (pk,), ls_emit)
                OP("S", (pk,), (f"SgB{j4}",),
                   lambda s, kb=kb, j4=j4, ps=ps: s.copy(
                       Sg[j4][:, kb * 512:(kb + 1) * 512], ps[:, 0:512]))
        for j4 in range(4):
            j = g4 * 4 + j4
            Ec = nkb_of(j) * 512
            OP("QS", ("HB",), ("tb1",),
               lambda en, j=j: en.dma_start(
                   out=tb2[:, 0:512],
                   in_=ap3(m1c, j * 128 * 512, [[512, 128], [1, 512]])))
            OP("V", (f"SgB{j4}", "tb1"), (f"SgB{j4}",),
               lambda v, j4=j4, Ec=Ec: v.tensor_tensor(
                   out=Sg[j4][:, Ec - 512:Ec], in0=Sg[j4][:, Ec - 512:Ec],
                   in1=tb2[:, 0:512], op=ALU.add))
            OP("V", (f"SgB{j4}",), ("mrow",),
               lambda v, j4=j4, Ec=Ec: v.reduce_max(out=mrow[:, j4:j4 + 1],
                                                    in_=Sg[j4][:, 0:Ec], axis=AX.X))
            OP("V", ("mrow",), ("nrow",),
               lambda v, j4=j4: v.tensor_scalar(out=nrow[:, j4:j4 + 1],
                                                in0=mrow[:, j4:j4 + 1],
                                                scalar1=-1.0, scalar2=None, op0=ALU.mult))
            OP("S", (f"SgB{j4}", "nrow"), ("arB1", "lrow"),
               lambda s, j4=j4, Ec=Ec: s.activation(out=arB1[:, 0:Ec],
                                                    in_=Sg[j4][:, 0:Ec], func=AF.Exp,
                                                    bias=nrow[:, j4:j4 + 1],
                                                    accum_out=lrow[:, j4:j4 + 1]))
            OP("V", ("lrow",), ("rlrow",),
               lambda v, j4=j4: v.reciprocal(out=rlrow[:, j4:j4 + 1],
                                             in_=lrow[:, j4:j4 + 1]))
            nkc = Ec // 128
            for gb in range(0, nkc, 8):
                nn = min(8, nkc - gb)
                def pt_emit(t, gb=gb, nn=nn):
                    last = None
                    for k in range(nn):
                        last = t.transpose(psT[:, k * 128:(k + 1) * 128],
                                           arB1[:, (gb + k) * 128:(gb + k + 1) * 128],
                                           ident[:])
                    return last
                OP("T", ("arB1", "ident"), ("psT",), pt_emit)
                OP("S", ("psT",), (f"pTL{j4}",),
                   lambda s, j4=j4, gb=gb, nn=nn: s.copy(
                       pTL[j4][:, gb * 128:(gb + nn) * 128], psT[:, 0:nn * 128]))
        for pair in range(2):
            jlo = g4 * 4 + pair * 2
            jhi = jlo + 1
            nkbp = nkb_of(jhi)
            for kb in range(nkbp):
                vb, vbk = (vblk[0], "vblk0") if kb % 2 == 0 else (vblk[1], "vblk1")
                OP("QS", ("v1h",), (vbk,),
                   lambda en, kb=kb, vb=vb: en.dma_start(
                       out=ap3(wbufB, vb.base, [[8192, 128], [1024, 4], [1, 1024]]),
                       in_=ap3(v1h, kb * 512 * D, [[D, 128], [128 * D, 4], [1, D]])))
                for jj, psv, pvk in ((jlo, psA, "psA"), (jhi, psB, "psB")):
                    j4 = jj - g4 * 4
                    nkc_j = nkb_of(jj) * 4
                    if kb * 4 >= nkc_j:
                        continue
                    def av_emit(t, kb=kb, j4=j4, vb=vb, psv=psv, nkc_j=nkc_j):
                        last = None
                        for kk in range(4):
                            kc = kb * 4 + kk
                            if kc >= nkc_j:
                                break
                            for nn in range(2):
                                last = t.matmul(psv[:, nn * 512:(nn + 1) * 512],
                                                pTL[j4][:, kc * 128:(kc + 1) * 128],
                                                vb[:, kk * 1024 + nn * 512:kk * 1024 + (nn + 1) * 512],
                                                start=(kc == 0), stop=(kc == nkc_j - 1))
                        return last
                    OP("T", (f"pTL{j4}", vbk), (pvk,), av_emit)
            for jj, psv, pvk in ((jlo, psA, "psA"), (jhi, psB, "psB")):
                j4 = jj - g4 * 4
                OP("S", (pvk, "rlrow"), ("arB3",),
                   lambda s, j4=j4, psv=psv: s.activation(out=arB3[:, 0:1024], in_=psv[:],
                                                          func=AF.Identity,
                                                          scale=rlrow[:, j4:j4 + 1]))
                def ot_emit(t):
                    last = None
                    for dd in range(8):
                        last = t.transpose(psT[:, dd * 128:(dd + 1) * 128],
                                           arB3[:, dd * 128:(dd + 1) * 128], ident[:])
                    return last
                OP("T", ("arB3", "ident"), ("psT",), ot_emit)
                OP("S", ("psT",), ("arB4",),
                   lambda s, j4=j4: s.copy(
                       ap3(arB4, j4 * 128, [[4096, 128], [512, 8], [1, 128]]),
                       ap3(psT, 0, [[1024, 128], [128, 8], [1, 128]])))
        OP("QS", ("x1h",), ("arF2",),
           lambda en, o0=o0: en.dma_start(
               out=ap3(arF2, 0, [[4096, 128], [512, 8], [1, 512]]),
               in_=ap3(x1h, o0, [[TOWN, 128], [128 * TOWN, 8], [1, 512]])))
        for m in range(8):
            def wo1_emit(t, m=m):
                last = None
                for c in range(8):
                    last = t.matmul(psC[:, 0:512],
                                    kv0k[:, c * 1024 + m * 128:c * 1024 + (m + 1) * 128],
                                    arB4[:, c * 512:(c + 1) * 512],
                                    start=(c == 0), stop=(c == 7))
                return last
            OP("T", ("arB4", "wo1b"), ("psC",), wo1_emit)
            OP("S", ("psC",), ("arF1",),
               lambda s: s.copy(arF1[:, 0:512], psC[:, 0:512]))
            OP("V", ("arF1", "arF2"), ("arF2",),
               lambda v, m=m: v.tensor_tensor(out=arF2[:, m * 512:(m + 1) * 512],
                                              in0=arF1[:, 0:512],
                                              in1=arF2[:, m * 512:(m + 1) * 512],
                                              op=ALU.add))
        for tc in range(4):
            ps, pk = (psA, "psA") if tc % 2 == 0 else (psB, "psB")
            def ft_emit(t, tc=tc, ps=ps):
                last = None
                for c in range(8):
                    last = t.transpose(ps[:, c * 128:(c + 1) * 128],
                                       arF2[:, c * 512 + tc * 128:c * 512 + (tc + 1) * 128],
                                       idf[:])
                return last
            OP("T", ("arF2", "idf"), (pk,), ft_emit)
            OP("S", (pk,), ("otb",),
               lambda s, ps=ps: s.copy(sqb[:], ps[:]))
            OP("QS", ("otb",), ("yout",),
               lambda en, o0=o0, tc=tc: en.dma_start(
                   out=y_own[o0 + tc * 128:o0 + (tc + 1) * 128, :],
                   in_=sqb[:]))
    P.barrier()
    P.final_emit(block)

    es.close()
    nc.finalize()
    return nc


_NC_CACHE = []


def _host_prep(inputs):
    f32 = np.float32
    bf16 = ml_dtypes.bfloat16

    def c(a, dt=f32):
        return np.ascontiguousarray(np.asarray(a, dtype=dt))

    x = c(inputs["L0_attn_x"])
    HP = [0, 4, 1, 5, 2, 6, 3, 7, 8, 12, 9, 13, 10, 14, 11, 15]
    hperm = np.concatenate([np.arange(h * 64, (h + 1) * 64) for h in HP])
    wq0 = c(np.asarray(inputs["L0_attn_Wq"])[:, hperm])
    wk0 = c(inputs["L0_attn_Wk"])
    wv0 = c(inputs["L0_attn_Wv"])
    wo0 = c(np.asarray(inputs["L0_attn_Wo"])[hperm, :])
    bq0 = c(np.asarray(inputs["L0_attn_bq"])[hperm].reshape(8, 128).T)
    bk0 = c(np.asarray(inputs["L0_attn_bk"]).reshape(2, 128).T)
    bv0c = c(np.asarray(inputs["L0_attn_bv"]).reshape(1, 256))
    bo0 = c(np.asarray(inputs["L0_attn_bo"]).reshape(8, 128).T)
    hw0 = c(np.asarray(inputs["L0_attn_head_weights"])[HP].reshape(1, 16))
    g0 = c(np.asarray(inputs["L0_ffn_w_gate"]).T)
    u0 = c(np.asarray(inputs["L0_ffn_w_up"]).T)
    d0 = c(np.asarray(inputs["L0_ffn_w_down"]).T)
    wq1 = c(np.asarray(inputs["L1_attn_Wq"]).T)
    wk1 = c(np.asarray(inputs["L1_attn_Wk"]).T)
    wv1 = c(np.asarray(inputs["L1_attn_Wv"]).T)
    wo1 = c(np.asarray(inputs["L1_attn_Wo"]).T)
    w1 = c(np.asarray(inputs["L1_ffn_w1_w"]).T)
    w2 = c(np.asarray(inputs["L1_ffn_w2_w"]).T)
    h1 = c(np.asarray(inputs["L1_ffn_h1_w"]).T)
    h2 = c(np.asarray(inputs["L1_ffn_h2_w"]).T)
    w3 = c(np.asarray(inputs["L1_ffn_w3_w"]).T)
    s1c = c(np.asarray(inputs["L1_ffn_w1_s"]).reshape(32, 128).T)
    s2c = c(np.asarray(inputs["L1_ffn_w2_s"]).reshape(32, 128).T)
    sh1c = c(np.asarray(inputs["L1_ffn_h1_s"]).reshape(32, 128).T)
    sh2c = c(np.asarray(inputs["L1_ffn_h2_s"]).reshape(32, 128).T)
    s3c = c(np.asarray(inputs["L1_ffn_w3_s"]).reshape(8, 128).T)

    identc = np.eye(128, dtype=bf16)
    identfc = np.eye(128, dtype=f32)
    nb = HD // 4
    inv_freq = 1.0 / (10000.0 ** (np.arange(nb, dtype=np.float64) / nb))
    tpos = np.arange(T, dtype=np.float64)
    base_cos = np.zeros((64, T), dtype=np.float64)
    base_sin = np.zeros((64, T), dtype=np.float64)
    for dd in range(64):
        ff = (dd % 32) // 2
        ang = tpos * inv_freq[ff]
        base_cos[dd] = np.cos(ang)
        base_sin[dd] = np.sin(ang) * (-1.0 if dd < 32 else 1.0)
    rope_cosc = np.concatenate([base_cos, base_cos], 0).astype(bf16)
    rope_sinc = np.concatenate([base_sin, base_sin], 0).astype(bf16)
    rot2c = np.zeros((128, 128), dtype=bf16)
    for blk in range(2):
        for dd in range(64):
            sdd = blk * 64 + (dd + 32 if dd < 32 else dd - 32)
            rot2c[sdd, blk * 64 + dd] = 1
    m0 = np.zeros((64, 3 * 2048), dtype=f32)
    m0gg = np.zeros((64, 3 * 512), dtype=f32)
    for var, sg in ((0, 0), (1, 1), (2, 63)):
        blockm = np.zeros((64, 128), dtype=f32)
        blockg = np.zeros((64, 32), dtype=f32)
        for q in range(64):
            ii = 64 * sg + q
            for k in range(128):
                jj = 64 * sg - 32 + k
                ok = (0 <= jj < T) and (abs(ii - jj) <= 12 or jj < 8)
                blockm[q, k] = 1.0 if ok else 0.0
            for kk in range(8):
                inwin = (64 * sg - 32 <= kk < 64 * sg + 96)
                blockg[q, kk] = 0.0 if inwin else 1.0
        m0[:, var * 2048:(var + 1) * 2048] = np.tile(blockm, (1, 16))
        m0gg[:, var * 512:(var + 1) * 512] = np.tile(blockg, (1, 16))
    m0c = m0.astype(bf16)
    m0gc = m0gg.astype(bf16)

    y = np.empty((B, T, D), dtype=f32)
    in_maps = []
    shared = dict(wq0=wq0, bq0=bq0, wk0=wk0, bk0=bk0, wv0=wv0, bv0c=bv0c,
                  wo0=wo0, bo0=bo0, hw0=hw0, g0=g0, u0=u0, d0=d0,
                  wq1=wq1, wk1=wk1, wv1=wv1, wo1=wo1,
                  w1=w1, s1c=s1c, w2=w2, s2c=s2c, h1=h1, sh1c=sh1c,
                  h2=h2, sh2c=sh2c, w3=w3, s3c=s3c,
                  identc=identc, identfc=identfc, rope_cosc=rope_cosc,
                  rope_sinc=rope_sinc, rot2c=rot2c, m0c=m0c, m0g=m0gc)
    for core in range(8):
        b = core // 2
        cpar = core % 2
        m1 = np.zeros((NQT * 128, 512), dtype=f32)
        for j in range(NQT):
            Ec = 512 * ((j + 2) // 2)
            for q in range(128):
                ig = 128 * (2 * j + cpar) + q
                keys = Ec - 512 + np.arange(512)
                m1[j * 128 + q] = np.where(keys <= ig, 0.0, -1e9)
        m1c = m1.astype(bf16)
        im = dict(shared)
        im["x_in"] = np.ascontiguousarray(x[b])
        im["m1c"] = m1c
        im["parc"] = np.array([[cpar]], dtype=np.int32)
        sel = np.zeros((128, 2), dtype=f32)
        sel[:, 0] = 1.0 - cpar
        sel[:, 1] = float(cpar)
        im["selc"] = sel
        in_maps.append(im)
    return in_maps, y


_LAST_RES = []


def kernel(**inputs):
    if not _NC_CACHE:
        _NC_CACHE.append(build_nc())
    nc = _NC_CACHE[0]
    in_maps, y = _host_prep(inputs)
    res = run_bass_kernel_spmd(nc, in_maps, core_ids=list(range(8)))
    _LAST_RES.clear()
    _LAST_RES.append(res)
    for core in range(8):
        b = core // 2
        cpar = core % 2
        yo = np.asarray(res.results[core]["y_own"]).reshape(16, 128, D)
        y[b].reshape(32, 128, D)[cpar::2] = yo
    return y


if __name__ == "__main__":
    nc = build_nc()
    print("built ok")
